# version 01
# baseline (speedup 1.0000x reference)
import numpy as np
from contextlib import ExitStack

import concourse.bass as bass
import concourse.mybir as mybir
from concourse.bass_utils import run_bass_kernel_spmd

F32 = mybir.dt.float32
BF16 = mybir.dt.bfloat16
AF = mybir.ActivationFunctionType
ALU = mybir.AluOpType
AX = mybir.AxisListType

NCORE = 8
NSEQ = 2
D = 1024
T = 2048
NCH = D // 128
NH = 16
DH = 64
NTG = T // 512
NQT = T // 128


def tile_w(W, oc=128):
    IN, OUT = W.shape
    return np.ascontiguousarray(
        W.reshape(IN // 128, 128, OUT // oc, oc).transpose(2, 1, 0, 3)
        .reshape(OUT // oc, 128, (IN // 128) * oc))


def colvec(v):
    return np.ascontiguousarray(np.asarray(v).reshape(-1, 128).T)


def prep_weights(inp):
    w = {}
    w["w_qkv0"] = tile_w(inp["L0_attn_W_qkv"])
    w["b_qkv0"] = colvec(inp["L0_attn_b_qkv"])
    w["w_proj0"] = tile_w(inp["L0_attn_W_proj"])
    w["b_proj0"] = colvec(inp["L0_attn_b_proj"])
    w["ln0w"] = colvec(inp["L0_attn_ln_w"])
    w["ln0b"] = colvec(inp["L0_attn_ln_b"])
    w["rw0"] = tile_w(inp["L0_ffn_router_w"], oc=8)
    w["w1_0"] = np.stack([tile_w(inp["L0_ffn_W1"][e]) for e in range(8)])
    w["b1_0"] = np.concatenate([colvec(inp["L0_ffn_b1"][e]) for e in range(8)], axis=1)
    w["w2_0"] = np.stack([tile_w(inp["L0_ffn_W2"][e]) for e in range(8)])
    w["b2_0"] = np.concatenate([colvec(inp["L0_ffn_b2"][e]) for e in range(8)], axis=1)
    w["w_attn1"] = tile_w(inp["L1_attn_c_attn_w"])
    w["b_attn1"] = colvec(inp["L1_attn_c_attn_b"])
    w["w_proj1"] = tile_w(inp["L1_attn_c_proj_w"])
    w["b_proj1"] = colvec(inp["L1_attn_c_proj_b"])
    w["wkc1T"] = np.ascontiguousarray(inp["L1_attn_wk_c"].T)
    w["wke1"] = np.ascontiguousarray(inp["L1_attn_wk_e"])
    w["wvc1T"] = np.ascontiguousarray(inp["L1_attn_wv_c"].T)
    w["wve1"] = np.ascontiguousarray(inp["L1_attn_wv_e"])
    w["n1w"] = colvec(inp["L1_ffn_n1_w"])
    w["n2w"] = colvec(inp["L1_ffn_n2_w"])
    w["wq1"] = tile_w(inp["L1_ffn_wq"].T)
    w["wk1"] = tile_w(inp["L1_ffn_wk"].T)
    w["wv1"] = tile_w(inp["L1_ffn_wv"].T)
    w["wo1"] = tile_w(inp["L1_ffn_wo"].T)
    w["qn1"] = np.ascontiguousarray(np.tile(inp["L1_ffn_qn_w"], 2)[:, None])
    w["kn1"] = np.ascontiguousarray(np.tile(inp["L1_ffn_kn_w"], 2)[:, None])
    w["gw1"] = tile_w(inp["L1_ffn_gate_w"].T, oc=4)
    w["e0up"] = tile_w(inp["L1_ffn_e0_up"].T)
    w["e0dn"] = tile_w(inp["L1_ffn_e0_down"].T)
    w["e1w1"] = tile_w(inp["L1_ffn_e1_w1"].T)
    w["e1w2"] = tile_w(inp["L1_ffn_e1_w2"].T)
    w["e1dn"] = tile_w(inp["L1_ffn_e1_down"].T)
    w["e2l1"] = tile_w(inp["L1_ffn_e2_l1"].T)
    w["e2l2"] = tile_w(inp["L1_ffn_e2_l2"].T)
    w["e2l3"] = tile_w(inp["L1_ffn_e2_l3"].T)
    w["e2l4"] = tile_w(inp["L1_ffn_e2_l4"].T)
    w["e3dn"] = tile_w(inp["L1_ffn_e3_down"].T)
    w["e3up"] = tile_w(inp["L1_ffn_e3_up"].T)
    w["e3out"] = tile_w(inp["L1_ffn_e3_out"].T)
    return w


def prep_xin(inp, core):
    z = inp["L0_attn_z"][2 * core:2 * core + 2]
    return np.ascontiguousarray(z.transpose(0, 2, 1))



class Prog:
    ENG = ("sync", "gpsimd", "scalar", "vector", "tensor")

    def __init__(self, nc, stack):
        self.nc = nc
        self.steps = {e: [] for e in self.ENG}
        self.cum = 0
        self.sem = stack.enter_context(nc.semaphore("serial"))

    def do(self, eng, fn, dma=False):
        amt = 16 if dma else 1
        self.steps[eng].append((self.cum, fn, amt))
        self.cum += amt

    def dma(self, q, out, in_):
        self.do(q, lambda e, o=out, i=in_: e.dma_start(out=o, in_=i), dma=True)

    def finalize(self, block):
        sem = self.sem
        for e in self.ENG:
            items = self.steps[e]
            if not items:
                continue

            def body(eng, items=items):
                for wait, fn, amt in items:
                    if wait > 0:
                        eng.wait_ge(sem, wait)
                    ins = fn(eng)
                    if ins is None:
                        ins = eng.nop()
                    ins.then_inc(sem, amt)
            getattr(block, e)(body)
        total = self.cum

        def tail(eng):
            eng.wait_ge(sem, total)
        block.gpsimd(tail)



def mm(p, out_ps, lhsT, rhs, start, stop):
    p.do("tensor", lambda t: t.matmul(out_ps, lhsT, rhs, start=start, stop=stop))


def mm_group(p, group):
    def fn(t):
        last = None
        for o, l, r, st, sp in group:
            last = t.matmul(o, l, r, start=st, stop=sp)
        return last
    p.do("tensor", fn)


def vop(p, name, *args, **kw):
    p.do("vector", lambda e: getattr(e, name)(*args, **kw))


def sop(p, name, *args, **kw):
    p.do("scalar", lambda e: getattr(e, name)(*args, **kw))


def gop(p, name, *args, **kw):
    p.do("gpsimd", lambda e: getattr(e, name)(*args, **kw))


EMITTERS = {}
_UNIQ = [0]


def uname(n):
    _UNIQ[0] += 1
    return f"{n}_{_UNIQ[0]}"


def emit_ln(p, tn, seq, layer):
    x = tn["xin"] if layer == 0 else tn["xcur"]
    sb = tn["sb"]
    st = tn["segstack"]
    nc = tn["nc"]
    xf = [st.enter_context(nc.sbuf_tensor(uname(f"lnx{c}"), [128, T], F32))
          for c in range(NCH)]
    xsq = st.enter_context(nc.sbuf_tensor(uname("lnsq"), [128, T], F32))
    t1 = st.enter_context(nc.sbuf_tensor(uname("lnt1"), [128, T], F32))
    znt = st.enter_context(nc.sbuf_tensor(uname("lnzn"), [128, T], BF16))
    R = st.enter_context(nc.sbuf_tensor(uname("lnrows"), [1, 4 * T], F32))
    Ab = st.enter_context(nc.sbuf_tensor(uname("lnAb"), [128, T], F32))
    Bb = st.enter_context(nc.sbuf_tensor(uname("lnBb"), [128, T], F32))
    Cb = st.enter_context(nc.sbuf_tensor(uname("lnCb"), [128, T], F32))
    ones_col = sb["ones_col"]
    ones_row = sb["ones_row"]
    PS_A = tn["PS_A"]
    PS_B = tn["PS_B"]
    R1 = R[0:1, 0:2048]; R2 = R[0:1, 2048:4096]
    R3 = R[0:1, 4096:6144]; R4 = R[0:1, 6144:8192]

    for c in range(NCH):
        p.dma("sync", xf[c][:], x[seq, c * 128:(c + 1) * 128, :])
    for c in range(NCH):
        mm_group(p, [(PS_A[0:1, g * 512:(g + 1) * 512], ones_col[:],
                      xf[c][:, g * 512:(g + 1) * 512], c == 0, c == NCH - 1)
                     for g in range(NTG)])
    vop(p, "tensor_copy", R1, PS_A[0:1, :])
    for c in range(NCH):
        sop(p, "square", xsq[:], xf[c][:])
        mm_group(p, [(PS_A[0:1, g * 512:(g + 1) * 512], ones_col[:],
                      xsq[:, g * 512:(g + 1) * 512], c == 0, c == NCH - 1)
                     for g in range(NTG)])
    vop(p, "tensor_copy", R2, PS_A[0:1, :])
    vop(p, "tensor_scalar_mul", R1, R1, 1.0 / D)
    vop(p, "tensor_scalar_mul", R2, R2, 1.0 / D)
    vop(p, "tensor_mul", R3, R1, R1)
    vop(p, "tensor_sub", R2, R2, R3)
    vop(p, "tensor_scalar_add", R3, R2, 1e-5)
    sop(p, "sqrt", R3, R3)
    vop(p, "reciprocal", R3, R3)
    vop(p, "tensor_mul", R4, R1, R3)
    def bcast(row, dst):
        for g in range(NTG):
            mm(p, PS_B[:, 0:512], ones_row[:], row[0:1, g * 512:(g + 1) * 512],
               True, True)
            vop(p, "tensor_copy", dst[:, g * 512:(g + 1) * 512], PS_B[:, 0:512])
    bcast(R3, Ab)
    bcast(R4, Bb)
    eps_i = 1e-5 if layer == 0 else 1.1920929e-07
    vop(p, "tensor_mul", R1, R3, R3)
    vop(p, "tensor_mul", R1, R1, R2)
    vop(p, "tensor_scalar_add", R1, R1, eps_i)
    sop(p, "sqrt", R1, R1)
    vop(p, "reciprocal", R1, R1)
    bcast(R1, Cb)
    wcol = sb["ln0w_s"] if layer == 0 else sb["n1w_s"]
    bcol = sb["ln0b_s"] if layer == 0 else None
    for c in range(NCH):
        vop(p, "tensor_mul", t1[:], xf[c][:], Ab[:])
        vop(p, "tensor_sub", t1[:], t1[:], Bb[:])
        p.dma("sync", tn["xn32"][seq, c * 128:(c + 1) * 128, :], t1[:])
        vop(p, "tensor_mul", t1[:], t1[:], Cb[:])
        if bcol is not None:
            vop(p, "tensor_scalar", znt[:], t1[:], wcol[:, c:c + 1],
                bcol[:, c:c + 1], ALU.mult, ALU.add)
        else:
            vop(p, "tensor_scalar_mul", znt[:], t1[:], wcol[:, c:c + 1])
        p.dma("sync", tn["znbf"][seq, c * 128:(c + 1) * 128, :], znt[:])



def emit_attn(p, tn, seq, cfg):
    nc = tn["nc"]
    st = tn["segstack"]
    sb = tn["sb"]
    PS_A, PS_B, PS_T, PS_O = tn["PS_A"], tn["PS_B"], tn["PS_T"], tn["PS_O"]
    ident = sb["ident"]
    ones_col = sb["ones_col"]
    ones_row = sb["ones_row"]
    causal = cfg["causal"]

    def S_(name, shape, dt):
        return st.enter_context(nc.sbuf_tensor(uname(name), shape, dt))

    zn = [S_(f"at_zn{c}", [128, T], BF16) for c in range(NCH)]
    Ob = [S_(f"at_o{c}", [128, T], BF16) for c in range(NCH)]
    q2 = S_("at_q2", [128, T], BF16)
    k2 = S_("at_k2", [128, T], BF16)
    v2 = S_("at_v2", [128, T], BF16)
    vtm = S_("at_vtm", [128, T], BF16)
    Pt = S_("at_p", [128, T], BF16)
    ptq = S_("at_ptq", [128, T], BF16)
    wt = S_("at_wt", [128, 3 * 1024], BF16)
    stt4 = S_("at_st", [128, 8], F32)
    sqf = S_("at_sqf", [128, T], F32) if cfg.get("qk_rms") else None
    rqb = S_("at_rqb", [128, T], F32) if cfg.get("qk_rms") else None
    rkb = S_("at_rkb", [128, T], F32) if cfg.get("qk_rms") else None
    R = S_("at_rows", [1, T], F32) if cfg.get("qk_rms") else None
    if cfg.get("latent"):
        q2p = S_("at_q2p", [128, T], BF16)
        v2e = S_("at_v2e", [128, T], BF16)
        lw = S_("at_lw", [32, 4 * 64], BF16)
        W64 = S_("at_w64", [128, 128], BF16)
        p.dma("gpsimd", lw[:, 0:64], tn["wke1"][:])
        p.dma("gpsimd", lw[:, 64:128], tn["wkc1T"][:])
        p.dma("gpsimd", lw[:, 128:192], tn["wvc1T"][:])
        p.dma("gpsimd", lw[:, 192:256], tn["wve1"][:])
        for b64 in (0, 64):
            mm(p, PS_O[b64:b64 + 64, 0:64], lw[:, 0:64], lw[:, 64:128], True, True)
            vop(p, "tensor_copy", W64[b64:b64 + 64, 0:64], PS_O[b64:b64 + 64, 0:64])
            mm(p, PS_O[b64:b64 + 64, 0:64], lw[:, 128:192], lw[:, 192:256], True, True)
            vop(p, "tensor_copy", W64[b64:b64 + 64, 64:128], PS_O[b64:b64 + 64, 0:64])

    if cfg["src"] == "xn32cast":
        for c in range(NCH):
            p.dma("gpsimd", zn[c][:], tn["xn32"][seq, c * 128:(c + 1) * 128, :])
    else:
        src = tn[cfg["src"]]
        for c in range(NCH):
            p.dma("sync", zn[c][:], src[seq, c * 128:(c + 1) * 128, :])

    bias = sb[cfg["qkv_bias"] + "_s"] if cfg.get("qkv_bias") else None
    qk_rms = cfg.get("qk_rms")

    for hp in range(8):
        if "fused_qkv" in cfg:
            wten = tn[cfg["fused_qkv"]]
            ocs = [hp, 8 + hp, 16 + hp]
            srcs = [wten[o] for o in ocs]
        else:
            srcs = [tn[n][hp] for n in cfg["sep_qkv"]]
            ocs = [hp, 8 + hp, 16 + hp]
        for i in range(3):
            p.dma("gpsimd", wt[:, i * 1024:(i + 1) * 1024], srcs[i][:, :])
        for i, dst in enumerate((q2, k2, v2)):
            mm_group(p, [(PS_A[:, g * 512:(g + 1) * 512],
                          wt[:, i * 1024 + c * 128:i * 1024 + (c + 1) * 128],
                          zn[c][:, g * 512:(g + 1) * 512], c == 0, c == NCH - 1)
                         for c in range(NCH) for g in range(NTG)])
            if qk_rms and i < 2:
                qn_s = sb[qk_rms[i] + "_s"]
                eps = qk_rms[2]
                rb = rqb if i == 0 else rkb
                sop(p, "square", sqf[:], PS_A[:, :])
                for h2 in range(2):
                    b64 = h2 * 64
                    Rrow = R[0:1, 0:T]
                    for g in range(NTG):
                        mm(p, PS_B[0:1, 0:512], ones_col[b64:b64 + 64, :],
                           sqf[b64:b64 + 64, g * 512:(g + 1) * 512], True, True)
                        vop(p, "tensor_copy", Rrow[0:1, g * 512:(g + 1) * 512],
                            PS_B[0:1, 0:512])
                    vop(p, "tensor_scalar_mul", Rrow, Rrow, 1.0 / 64)
                    vop(p, "tensor_scalar_add", Rrow, Rrow, eps)
                    sop(p, "sqrt", Rrow, Rrow)
                    vop(p, "reciprocal", Rrow, Rrow)
                    for g in range(NTG):
                        mm(p, PS_B[:, 0:512], ones_row[:],
                           Rrow[0:1, g * 512:(g + 1) * 512], True, True)
                        vop(p, "tensor_copy",
                            rb[b64:b64 + 64, g * 512:(g + 1) * 512],
                            PS_B[b64:b64 + 64, 0:512])
                vop(p, "scalar_tensor_tensor", dst[:], PS_A[:, :],
                    qn_s[:, 0:1], rb[:], ALU.mult, ALU.mult)
            elif bias is not None:
                vop(p, "tensor_scalar_add", dst[:], PS_A[:, :],
                    bias[:, ocs[i]:ocs[i] + 1])
            else:
                vop(p, "tensor_copy", dst[:], PS_A[:, :])
        qsrc, vsrc = q2, v2
        if cfg.get("latent"):
            for h2 in range(2):
                b64 = h2 * 64
                for g in range(NTG):
                    gsl = slice(g * 512, (g + 1) * 512)
                    mm(p, PS_B[b64:b64 + 64, 0:512], W64[b64:b64 + 64, 0:64],
                       q2[b64:b64 + 64, gsl], True, True)
                    vop(p, "tensor_copy", q2p[b64:b64 + 64, gsl],
                        PS_B[b64:b64 + 64, 0:512])
                    mm(p, PS_B[b64:b64 + 64, 0:512], W64[b64:b64 + 64, 64:128],
                       v2[b64:b64 + 64, gsl], True, True)
                    vop(p, "tensor_copy", v2e[b64:b64 + 64, gsl],
                        PS_B[b64:b64 + 64, 0:512])
            qsrc, vsrc = q2p, v2e
        for half in range(2):
            def tr(t, half=half):
                last = None
                for i in range(8):
                    kt = half * 8 + i
                    last = t.transpose(PS_T[:, i * 128:(i + 1) * 128],
                                       vsrc[:, kt * 128:(kt + 1) * 128], ident[:])
                return last
            p.do("tensor", tr)
            vop(p, "tensor_copy", vtm[:, half * 1024:(half + 1) * 1024], PS_T[:, :])
        for h2 in range(2):
            b64 = h2 * 64
            for qt in range(NQT):
                nk = (qt + 1) * 128 if causal else T
                nkc = (nk + 511) // 512
                mm_group(p, [(PS_A[:, g * 512:min((g + 1) * 512, nk)],
                              qsrc[b64:b64 + 64, qt * 128:(qt + 1) * 128],
                              k2[b64:b64 + 64, g * 512:min((g + 1) * 512, nk)],
                              True, True) for g in range(nkc)])
                if causal:
                    vop(p, "tensor_add", PS_A[:, nk - 128:nk],
                        PS_A[:, nk - 128:nk], sb["tri"][:])
                vop(p, "reduce_max", stt4[:, 0:1], PS_A[:, 0:nk], AX.X,
                    negate=True)
                vop(p, "tensor_scalar_mul", stt4[:, 1:2], stt4[:, 0:1], 0.125)
                sop(p, "activation", Pt[:, 0:nk], PS_A[:, 0:nk], AF.Exp,
                    bias=stt4[:, 1:2], scale=0.125, accum_out=stt4[:, 2:3])
                vop(p, "reciprocal", stt4[:, 3:4], stt4[:, 2:3])
                vop(p, "tensor_scalar_mul", Pt[:, 0:nk], Pt[:, 0:nk],
                    stt4[:, 3:4])
                ktn = nk // 128
                for half in range((ktn + 7) // 8):
                    n8 = min(8, ktn - half * 8)
                    def trp(t, half=half, n8=n8):
                        last = None
                        for i in range(n8):
                            kt = half * 8 + i
                            last = t.transpose(PS_T[:, i * 128:(i + 1) * 128],
                                               Pt[:, kt * 128:(kt + 1) * 128],
                                               ident[:])
                        return last
                    p.do("tensor", trp)
                    vop(p, "tensor_copy",
                        ptq[:, half * 1024:half * 1024 + n8 * 128],
                        PS_T[:, 0:n8 * 128])
                mm_group(p, [(PS_O[0:64, 0:128],
                              vtm[:, kt * 128 + b64:kt * 128 + b64 + 64],
                              ptq[:, kt * 128:(kt + 1) * 128],
                              kt == 0, kt == ktn - 1) for kt in range(ktn)])
                vop(p, "tensor_copy",
                    Ob[hp][b64:b64 + 64, qt * 128:(qt + 1) * 128],
                    PS_O[0:64, 0:128])
    pbias = sb[cfg["proj_bias"] + "_s"] if cfg.get("proj_bias") else None
    xt = S_("at_xt", [128, 512], F32)
    s_t = S_("at_s", [128, 512], F32)
    wp = tn[cfg["proj_w"]]
    out = tn[cfg["out"]]
    for oc in range(NCH):
        if oc in cfg.get("skip_oc", ()):
            for g in range(NTG):
                gsl = slice(g * 512, (g + 1) * 512)
                if cfg.get("add_xn"):
                    p.dma("sync", xt[:], tn["xn32"][seq, oc * 128:(oc + 1) * 128, gsl])
                    p.dma("sync", out[seq, oc * 128:(oc + 1) * 128, gsl], xt[:])
                else:
                    gop(p, "memset", s_t[:], 0.0)
                    p.dma("sync", out[seq, oc * 128:(oc + 1) * 128, gsl], s_t[:])
            continue
        p.dma("gpsimd", wt[:, 0:1024], wp[oc][:, :])
        for g in range(NTG):
            gsl = slice(g * 512, (g + 1) * 512)
            mm_group(p, [(PS_O[:, 0:512], wt[:, c * 128:(c + 1) * 128],
                          Ob[c][:, gsl], c == 0, c == NCH - 1)
                         for c in range(NCH)])
            if cfg.get("add_xn"):
                p.dma("sync", xt[:], tn["xn32"][seq, oc * 128:(oc + 1) * 128, gsl])
                if pbias is not None:
                    vop(p, "scalar_tensor_tensor", s_t[:], PS_O[:, 0:512],
                        pbias[:, oc:oc + 1], xt[:], ALU.add, ALU.add)
                else:
                    vop(p, "tensor_add", s_t[:], PS_O[:, 0:512], xt[:])
            else:
                if pbias is not None:
                    vop(p, "tensor_scalar_add", s_t[:], PS_O[:, 0:512],
                        pbias[:, oc:oc + 1])
                else:
                    vop(p, "tensor_copy", s_t[:], PS_O[:, 0:512])
            p.dma("sync", out[seq, oc * 128:(oc + 1) * 128, gsl], s_t[:])


def _emit_attn0(p, tn, seq):
    emit_attn(p, tn, seq, dict(
        src="znbf", fused_qkv="w_qkv0", qkv_bias="b_qkv0", causal=False,
        proj_w="w_proj0", proj_bias="b_proj0", add_xn=True, out="sbr"))


def _emit_attn1(p, tn, seq):
    emit_attn(p, tn, seq, dict(
        src="xn32cast", fused_qkv="w_attn1", qkv_bias="b_attn1", causal=True,
        latent=True, proj_w="w_proj1", proj_bias="b_proj1", add_xn=False,
        out="sbr"))


def _emit_ffn1a(p, tn, seq):
    emit_attn(p, tn, seq, dict(
        src="znbf", sep_qkv=("wq1", "wk1", "wv1"), causal=True,
        qk_rms=("qn1", "kn1", 1.1920929e-07),
        proj_w="wo1", add_xn=True, skip_oc={0, 1}, out="xloc"))


EMITTERS["attn0"] = _emit_attn0
EMITTERS["attn1"] = _emit_attn1
EMITTERS["ffn1a"] = _emit_ffn1a



def emit_moe0(p, tn, seq):
    nc = tn["nc"]
    st = tn["segstack"]
    sb = tn["sb"]
    PS_A, PS_B, PS_T, PS_O = tn["PS_A"], tn["PS_B"], tn["PS_T"], tn["PS_O"]
    ones_row = sb["ones_row"]
    identf = sb["identf"]

    def S_(name, shape, dt):
        return st.enter_context(nc.sbuf_tensor(uname(name), shape, dt))

    xnb = [S_(f"mo_xn{c}", [128, T], BF16) for c in range(NCH)]
    act = [S_(f"mo_act{j}", [128, T], BF16) for j in range(21)]
    sg = S_("mo_sg", [128, T], BF16)
    wt = S_("mo_wt", [128, 1024], BF16)
    wt2 = S_("mo_wt2", [128, 2688], BF16)
    rw = S_("mo_rw", [128, 64], F32)
    xc = S_("mo_xc", [128, T], F32)
    route_f = S_("mo_rf", [128, T], F32)
    route_tm = S_("mo_rtm", [128, 128], F32)
    cw_tm = S_("mo_cwtm", [128, 128], F32)
    cw_fm = S_("mo_cwfm", [8, T], F32)
    eqt = S_("mo_eq", [128, 8], F32)
    mskt = S_("mo_msk", [128, 8], F32)
    stt8 = S_("mo_st", [128, 8], F32)
    cwrow = S_("mo_cwrow", [1, T], F32)
    cwb = S_("mo_cwb", [128, T], F32)
    y_t = S_("mo_y", [128, 512], F32)
    ft = route_f
    s_t = cwb

    for c in range(NCH):
        p.dma("gpsimd", xnb[c][:], tn["xn32"][seq, c * 128:(c + 1) * 128, :])
    p.dma("sync", rw[:], tn["rw0"][0][:, :])
    for c in range(NCH):
        p.dma("sync", xc[:], tn["xn32"][seq, c * 128:(c + 1) * 128, :])
        mm_group(p, [(PS_A[0:8, g * 512:(g + 1) * 512], rw[:, c * 8:(c + 1) * 8],
                      xc[:, g * 512:(g + 1) * 512], c == 0, c == NCH - 1)
                     for g in range(NTG)])
    gop(p, "memset", route_f[:], 0.0)
    vop(p, "tensor_copy", route_f[0:8, :], PS_A[0:8, :])
    for kt in range(NQT):
        p.do("tensor", lambda t, kt=kt: t.transpose(
            PS_B[:, 0:128], route_f[:, kt * 128:(kt + 1) * 128], identf[:]))
        vop(p, "tensor_copy", route_tm[:, kt * 8:(kt + 1) * 8], PS_B[:, 0:8])
    for kt in range(NQT):
        blk = route_tm[:, kt * 8:(kt + 1) * 8]
        m1 = stt8[:, 0:1]; negm2 = stt8[:, 1:2]; m2 = stt8[:, 2:3]
        negm1 = stt8[:, 3:4]; w1 = stt8[:, 4:5]; w2 = stt8[:, 5:6]
        vop(p, "reduce_max", m1, blk, AX.X)
        vop(p, "tensor_scalar", eqt[:], blk, m1, None, ALU.is_equal)
        vop(p, "scalar_tensor_tensor", mskt[:], eqt[:], -1e30, blk,
            ALU.mult, ALU.add)
        vop(p, "reduce_max", negm2, mskt[:], AX.X, negate=True)
        vop(p, "tensor_scalar_mul", m2, negm2, -1.0)
        vop(p, "tensor_scalar_mul", negm1, m1, -1.0)
        sop(p, "activation", w1, m1, AF.Sigmoid, bias=negm2, scale=1.0)
        sop(p, "activation", w2, m2, AF.Sigmoid, bias=negm1, scale=1.0)
        vop(p, "tensor_scalar", eqt[:], mskt[:], m2, None, ALU.is_equal)
        cwb_blk = cw_tm[:, kt * 8:(kt + 1) * 8]
        vop(p, "tensor_scalar", mskt[:], blk, m1, None, ALU.is_equal)
        vop(p, "tensor_scalar_mul", mskt[:], mskt[:], w1)
        vop(p, "tensor_scalar", eqt[:], eqt[:], w2, None, ALU.mult)
        vop(p, "tensor_add", cwb_blk, mskt[:], eqt[:])
    for kt in range(NQT):
        p.do("tensor", lambda t, kt=kt: t.transpose(
            PS_B[0:8, 0:128], cw_tm[:, kt * 8:(kt + 1) * 8], identf[:]))
        vop(p, "tensor_copy", cw_fm[0:8, kt * 128:(kt + 1) * 128], PS_B[0:8, 0:128])

    b1s = sb["b1_0_s"]
    b2s = sb["b2_0_s"]
    for e in range(8):
        p.dma("sync", cwrow[:], cw_fm[e:e + 1, :])
        for g in range(NTG):
            mm(p, PS_B[:, 0:512], ones_row[:], cwrow[0:1, g * 512:(g + 1) * 512],
               True, True)
            vop(p, "tensor_copy", cwb[:, g * 512:(g + 1) * 512], PS_B[:, 0:512])
        for j in range(21):
            for oc, kind in ((21 + j, "g"), (j, "a")):
                p.dma("gpsimd", wt[:], tn["w1_0"][e, oc][:, :])
                mm_group(p, [(PS_A[:, g * 512:(g + 1) * 512],
                              wt[:, c * 128:(c + 1) * 128],
                              xnb[c][:, g * 512:(g + 1) * 512],
                              c == 0, c == NCH - 1)
                             for c in range(NCH) for g in range(NTG)])
                bcol = b1s[:, e * 42 + oc:e * 42 + oc + 1]
                if kind == "g":
                    sop(p, "activation", sg[:], PS_A[:, :], AF.Silu,
                        bias=bcol, scale=1.0)
                else:
                    vop(p, "scalar_tensor_tensor", act[j][:], PS_A[:, :],
                        bcol, sg[:], ALU.add, ALU.mult)
        for oc in range(NCH):
            p.dma("gpsimd", wt2[:], tn["w2_0"][e, oc][:, :])
            for g in range(NTG):
                gsl = slice(g * 512, (g + 1) * 512)
                mm_group(p, [(PS_O[:, 0:512], wt2[:, kc * 128:(kc + 1) * 128],
                              act[kc][:, gsl], kc == 0, kc == 20)
                             for kc in range(21)])
                vop(p, "scalar_tensor_tensor", y_t[:], PS_O[:, 0:512],
                    b2s[:, e * 8 + oc:e * 8 + oc + 1], cwb[:, gsl],
                    ALU.add, ALU.mult)
                dst = tn["facc"][seq, oc * 128:(oc + 1) * 128, gsl]
                if e == 0:
                    p.dma("gpsimd", dst, y_t[:])
                else:
                    p.do("gpsimd", lambda en, d=dst, s=y_t: en.dma_start(
                        out=d, in_=s[:], accum_op=ALU.add), dma=True)
    for c in range(NCH):
        csl = slice(c * 128, (c + 1) * 128)
        p.dma("sync", ft[:], tn["facc"][seq, csl, :])
        p.dma("sync", s_t[:], tn["sbr"][seq, csl, :])
        p.dma("sync", xc[:], tn["xin"][seq, csl, :])
        sop(p, "activation", ft[:], ft[:], AF.Silu, bias=0.0, scale=1.0)
        vop(p, "tensor_mul", ft[:], ft[:], s_t[:])
        vop(p, "tensor_add", ft[:], ft[:], xc[:])
        p.dma("sync", tn["xcur"][seq, csl, :], ft[:])


EMITTERS["moe0"] = emit_moe0



def emit_ffn1b(p, tn, seq):
    nc = tn["nc"]
    st = tn["segstack"]
    sb = tn["sb"]
    PS_A, PS_B, PS_T, PS_O = tn["PS_A"], tn["PS_B"], tn["PS_T"], tn["PS_O"]
    identf = sb["identf"]
    onesF = sb["onesF"]
    ones_col = sb["ones_col"]
    EPS = 1.1920929e-07

    def S_(name, shape, dt):
        return st.enter_context(nc.sbuf_tensor(uname(name), shape, dt))

    xfb = [S_(f"fb_xf{c}", [128, T], BF16) for c in range(NCH)]
    A = [S_(f"fb_A{j}", [128, T], BF16) for j in range(16)]
    B = [S_(f"fb_B{j}", [128, T], BF16) for j in range(16)]
    wt = S_("fb_wt", [128, 2048], BF16)
    xlc = S_("fb_xl", [128, T], F32)
    gw2 = S_("fb_gw", [128, 32], F32)
    gc_f = S_("fb_gc", [128, T], F32)
    Ctm = S_("fb_ctm", [128, 16], F32)
    ltm = S_("fb_ltm", [128, 64], F32)
    cw_tm = S_("fb_cwtm", [128, 64], F32)
    cw_fm = S_("fb_cwfm", [8, T], F32)
    eqt = S_("fb_eq", [128, 4], F32)
    mskt = S_("fb_msk", [128, 4], F32)
    stt8 = S_("fb_st", [128, 8], F32)
    cwb = S_("fb_cwb", [128, T], F32)
    Cb = cwb[:, 0:512]
    sq = gc_f
    y_t = gc_f[:, 0:512]
    fb_s = cwb
    n2s = sb["n2w_s"]
    cwrow = gc_f[0:1, 0:T]

    p.dma("sync", gw2[:], tn["gw1"][0][:, :])
    for c in range(NCH):
        vop(p, "tensor_scalar_mul", gw2[:, c * 4:(c + 1) * 4],
            gw2[:, c * 4:(c + 1) * 4], n2s[:, c:c + 1])
    for c in range(NCH):
        p.dma("sync", xlc[:], tn["xloc"][seq, c * 128:(c + 1) * 128, :])
        sop(p, "square", sq[:], xlc[:])
        mm_group(p, [(PS_A[64:65, g * 512:(g + 1) * 512], ones_col[:],
                      sq[:, g * 512:(g + 1) * 512], c == 0, c == NCH - 1)
                     for g in range(NTG)])
        mm_group(p, [(PS_A[0:4, g * 512:(g + 1) * 512], gw2[:, c * 4:(c + 1) * 4],
                      xlc[:, g * 512:(g + 1) * 512], c == 0, c == NCH - 1)
                     for g in range(NTG)])
    gop(p, "memset", gc_f[:], 0.0)
    Crow = gc_f[64:65, :]
    vop(p, "tensor_copy", Crow, PS_A[64:65, :])
    vop(p, "tensor_scalar_mul", Crow, Crow, 1.0 / D)
    vop(p, "tensor_scalar_add", Crow, Crow, EPS)
    sop(p, "sqrt", Crow, Crow)
    vop(p, "reciprocal", Crow, Crow)
    vop(p, "tensor_copy", gc_f[0:4, :], PS_A[0:4, :])
    for kt in range(NQT):
        p.do("tensor", lambda t, kt=kt: t.transpose(
            PS_B[:, 0:128], gc_f[:, kt * 128:(kt + 1) * 128], identf[:]))
        vop(p, "tensor_copy", ltm[:, kt * 4:(kt + 1) * 4], PS_B[:, 0:4])
        vop(p, "tensor_copy", Ctm[:, kt:kt + 1], PS_B[:, 64:65])
    for g in range(NTG):
        gsl = slice(g * 512, (g + 1) * 512)
        mm(p, PS_B[:, 0:512], onesF[64:65, 0:128], Crow[0:1, gsl], True, True)
        vop(p, "tensor_copy", Cb, PS_B[:, 0:512])
        for c in range(NCH):
            p.dma("sync", xlc[:, 0:512], tn["xloc"][seq, c * 128:(c + 1) * 128, gsl])
            vop(p, "scalar_tensor_tensor", xfb[c][:, gsl], xlc[:, 0:512],
                n2s[:, c:c + 1], Cb, ALU.mult, ALU.mult)
    for kt in range(NQT):
        blk = ltm[:, kt * 4:(kt + 1) * 4]
        vop(p, "tensor_scalar_mul", blk, blk, Ctm[:, kt:kt + 1])
        negm = stt8[:, 0:1]; Z = stt8[:, 1:2]; rZ = stt8[:, 2:3]
        m1 = stt8[:, 3:4]; negm2 = stt8[:, 4:5]; m2 = stt8[:, 5:6]
        s12 = stt8[:, 6:7]; r12 = stt8[:, 7:8]
        vop(p, "reduce_max", negm, blk, AX.X, negate=True)
        sop(p, "activation", blk, blk, AF.Exp, bias=negm, scale=1.0,
            accum_out=Z)
        vop(p, "reciprocal", rZ, Z)
        vop(p, "tensor_scalar_mul", blk, blk, rZ)
        vop(p, "reduce_max", m1, blk, AX.X)
        vop(p, "tensor_scalar", eqt[:], blk, m1, None, ALU.is_equal)
        vop(p, "scalar_tensor_tensor", mskt[:], eqt[:], -1e30, blk,
            ALU.mult, ALU.add)
        vop(p, "reduce_max", negm2, mskt[:], AX.X, negate=True)
        vop(p, "tensor_scalar_mul", m2, negm2, -1.0)
        vop(p, "tensor_sub", s12, m1, negm2)
        vop(p, "tensor_scalar_add", s12, s12, 1e-8)
        vop(p, "reciprocal", r12, s12)
        cwblk = cw_tm[:, kt * 4:(kt + 1) * 4]
        vop(p, "tensor_scalar", cwblk, eqt[:], m1, None, ALU.mult)
        vop(p, "tensor_scalar", eqt[:], mskt[:], m2, None, ALU.is_equal)
        vop(p, "tensor_scalar", eqt[:], eqt[:], m2, None, ALU.mult)
        vop(p, "tensor_add", cwblk, cwblk, eqt[:])
        vop(p, "tensor_scalar_mul", cwblk, cwblk, r12)
    for kt in range(NQT):
        p.do("tensor", lambda t, kt=kt: t.transpose(
            PS_B[0:4, 0:128], cw_tm[:, kt * 4:(kt + 1) * 4], identf[:]))
        vop(p, "tensor_copy", cw_fm[0:4, kt * 128:(kt + 1) * 128], PS_B[0:4, 0:128])

    def bcast_cw(e):
        p.dma("sync", cwrow[:], cw_fm[e:e + 1, :])
        for g in range(NTG):
            mm(p, PS_B[:, 0:512], onesF[0:1, 0:128],
               cwrow[0:1, g * 512:(g + 1) * 512], True, True)
            vop(p, "tensor_copy", cwb[:, g * 512:(g + 1) * 512], PS_B[:, 0:512])

    def up_mm(wname, oc, src_list, nkc):
        p.dma("gpsimd", wt[:, 0:nkc * 128], tn[wname][oc][:, :])
        mm_group(p, [(PS_A[:, g * 512:(g + 1) * 512],
                      wt[:, kc * 128:(kc + 1) * 128],
                      src_list[kc][:, g * 512:(g + 1) * 512],
                      kc == 0, kc == nkc - 1)
                     for kc in range(nkc) for g in range(NTG)])

    def down_acc(wname, src_list, nkc, first_e):
        for oc in range(2, NCH):
            p.dma("gpsimd", wt[:, 0:nkc * 128], tn[wname][oc][:, :])
            for g in range(NTG):
                gsl = slice(g * 512, (g + 1) * 512)
                mm_group(p, [(PS_O[:, 0:512], wt[:, kc * 128:(kc + 1) * 128],
                              src_list[kc][:, gsl], kc == 0, kc == nkc - 1)
                             for kc in range(nkc)])
                vop(p, "tensor_mul", y_t[:], PS_O[:, 0:512], cwb[:, gsl])
                dst = tn["facc"][seq, oc * 128:(oc + 1) * 128, gsl]
                if first_e:
                    p.dma("gpsimd", dst, y_t[:])
                else:
                    p.do("gpsimd", lambda en, d=dst, s=y_t: en.dma_start(
                        out=d, in_=s[:], accum_op=ALU.add), dma=True)

    bcast_cw(0)
    for oc in range(16):
        up_mm("e0up", oc, xfb, NCH)
        sop(p, "activation", A[oc][:], PS_A[:, :], AF.Gelu, bias=0.0, scale=1.0)
    down_acc("e0dn", A, 16, True)
    bcast_cw(1)
    for oc in range(16):
        up_mm("e1w1", oc, xfb, NCH)
        sop(p, "activation", A[oc][:], PS_A[:, :], AF.Silu, bias=0.0, scale=1.0)
    for oc in range(16):
        up_mm("e1w2", oc, xfb, NCH)
        vop(p, "tensor_mul", A[oc][:], PS_A[:, :], A[oc][:])
    down_acc("e1dn", A, 16, False)
    bcast_cw(2)
    for oc in range(16):
        up_mm("e2l1", oc, xfb, NCH)
        sop(p, "activation", A[oc][:], PS_A[:, :], AF.Gelu, bias=0.0, scale=1.0)
    for oc in range(16):
        up_mm("e2l2", oc, A, 16)
        sop(p, "activation", B[oc][:], PS_A[:, :], AF.Gelu, bias=0.0, scale=1.0)
    for oc in range(16):
        up_mm("e2l3", oc, B, 16)
        sop(p, "activation", A[oc][:], PS_A[:, :], AF.Gelu, bias=0.0, scale=1.0)
    down_acc("e2l4", A, 16, False)
    bcast_cw(3)
    for oc in range(NCH):
        up_mm("e3dn", oc, xfb, NCH)
        sop(p, "activation", A[oc][:], PS_A[:, :], AF.Gelu, bias=0.0, scale=1.0)
    for oc in range(16):
        up_mm("e3up", oc, A, NCH)
        sop(p, "activation", B[oc][:], PS_A[:, :], AF.Gelu, bias=0.0, scale=1.0)
    down_acc("e3out", B, 16, False)
    for c in range(NCH):
        csl = slice(c * 128, (c + 1) * 128)
        p.dma("sync", xlc[:], tn["xloc"][seq, csl, :])
        if c >= 2:
            p.dma("sync", gc_f[:], tn["facc"][seq, csl, :])
            vop(p, "tensor_add", gc_f[:], gc_f[:], xlc[:])
        else:
            vop(p, "tensor_copy", gc_f[:], xlc[:])
        sop(p, "activation", gc_f[:], gc_f[:], AF.Silu, bias=0.0, scale=1.0)
        p.dma("sync", fb_s[:], tn["sbr"][seq, csl, :])
        vop(p, "tensor_mul", gc_f[:], gc_f[:], fb_s[:])
        p.dma("sync", fb_s[:], tn["xcur"][seq, csl, :])
        vop(p, "tensor_add", gc_f[:], gc_f[:], fb_s[:])
        p.dma("sync", tn["yout"][seq, csl, :], gc_f[:])


EMITTERS["ffn1b"] = emit_ffn1b



SCRATCH_F32 = ("xn32", "xcur", "sbr", "xloc", "facc")
SCRATCH_BF16 = ("znbf",)


def declare_tensors(nc, segs, produce, consume):
    tn = {}
    tn["xin"] = nc.dram_tensor("xin", [NSEQ, D, T], F32, kind="ExternalInput").ap()
    tn["yout"] = nc.dram_tensor("yout", [NSEQ, D, T], F32, kind="ExternalOutput").ap()
    wshapes = {
        "w_qkv0": ([24, 128, 1024], F32), "b_qkv0": ([128, 24], F32),
        "w_proj0": ([8, 128, 1024], F32), "b_proj0": ([128, 8], F32),
        "ln0w": ([128, 8], F32), "ln0b": ([128, 8], F32),
        "rw0": ([1, 128, 64], F32),
        "w1_0": ([8, 42, 128, 1024], F32), "b1_0": ([128, 336], F32),
        "w2_0": ([8, 8, 128, 2688], F32), "b2_0": ([128, 64], F32),
        "w_attn1": ([24, 128, 1024], F32), "b_attn1": ([128, 24], F32),
        "w_proj1": ([8, 128, 1024], F32), "b_proj1": ([128, 8], F32),
        "wkc1T": ([32, 64], F32), "wke1": ([32, 64], F32),
        "wvc1T": ([32, 64], F32), "wve1": ([32, 64], F32),
        "n1w": ([128, 8], F32), "n2w": ([128, 8], F32),
        "wq1": ([8, 128, 1024], F32), "wk1": ([8, 128, 1024], F32),
        "wv1": ([8, 128, 1024], F32), "wo1": ([8, 128, 1024], F32),
        "qn1": ([128, 1], F32), "kn1": ([128, 1], F32),
        "gw1": ([1, 128, 32], F32),
        "e0up": ([16, 128, 1024], F32), "e0dn": ([8, 128, 2048], F32),
        "e1w1": ([16, 128, 1024], F32), "e1w2": ([16, 128, 1024], F32),
        "e1dn": ([8, 128, 2048], F32),
        "e2l1": ([16, 128, 1024], F32), "e2l2": ([16, 128, 2048], F32),
        "e2l3": ([16, 128, 2048], F32), "e2l4": ([8, 128, 2048], F32),
        "e3dn": ([8, 128, 1024], F32), "e3up": ([16, 128, 1024], F32),
        "e3out": ([8, 128, 2048], F32),
    }
    need = weights_needed(segs)
    for name in need:
        shape, dt = wshapes[name]
        tn[name] = nc.dram_tensor(name, shape, dt, kind="ExternalInput").ap()
    for name in SCRATCH_F32:
        if name in produce and name in consume:
            kind = "Internal"
        elif name in produce:
            kind = "ExternalOutput"
        elif name in consume:
            kind = "ExternalInput"
        else:
            continue
        tn[name] = nc.dram_tensor(name, [NSEQ, D, T], F32, kind=kind).ap()
    for name in SCRATCH_BF16:
        if name in produce and name in consume:
            kind = "Internal"
        elif name in produce:
            kind = "ExternalOutput"
        elif name in consume:
            kind = "ExternalInput"
        else:
            continue
        tn[name] = nc.dram_tensor(name, [NSEQ, D, T], BF16, kind=kind).ap()
    return tn


def weights_needed(segs):
    m = {
        "ln0": ["ln0w", "ln0b"],
        "ln1": ["n1w"],
        "attn0": ["w_qkv0", "b_qkv0", "w_proj0", "b_proj0"],
        "moe0": ["rw0", "w1_0", "b1_0", "w2_0", "b2_0"],
        "attn1": ["w_attn1", "b_attn1", "w_proj1", "b_proj1",
                  "wkc1T", "wke1", "wvc1T", "wve1"],
        "ffn1a": ["wq1", "wk1", "wv1", "wo1", "qn1", "kn1"],
        "ffn1b": ["n2w", "gw1", "e0up", "e0dn", "e1w1", "e1w2", "e1dn",
                  "e2l1", "e2l2", "e2l3", "e2l4", "e3dn", "e3up", "e3out"],
    }
    out = []
    for s in segs:
        for wn in m.get(s, []):
            if wn not in out:
                out.append(wn)
    return out


SEG_IO = {
    "ln0": ((), ("xn32", "znbf")),
    "attn0": (("xn32", "znbf"), ("sbr",)),
    "moe0": (("xn32", "sbr"), ("xcur", "facc")),
    "ln1": (("xcur",), ("xn32", "znbf")),
    "attn1": (("xn32",), ("sbr",)),
    "ffn1a": (("xn32", "znbf"), ("xloc",)),
    "ffn1b": (("xloc", "xcur", "sbr"), ("facc",)),
}


SMALLS = {
    "ln0w": [128, 8], "ln0b": [128, 8], "n1w": [128, 8], "n2w": [128, 8],
    "b_qkv0": [128, 24], "b_attn1": [128, 24],
    "b_proj0": [128, 8], "b_proj1": [128, 8],
    "qn1": [128, 1], "kn1": [128, 1],
    "b1_0": [128, 8 * 42], "b2_0": [128, 8 * 8],
}


def alloc_sbuf(nc, stack, segs):
    sb = {}
    def S(name, shape, dt):
        sb[name] = stack.enter_context(nc.sbuf_tensor(name, shape, dt))
        return sb[name]
    for name in weights_needed(segs):
        if name in SMALLS:
            S(name + "_s", SMALLS[name], F32)
    S("ones_col", [128, 1], F32)
    S("ones_row", [1, 128], F32)
    S("onesb_col", [128, 1], BF16)
    S("onesb_row", [1, 128], BF16)
    S("ident", [128, 128], BF16)
    S("tri", [128, 128], F32)
    S("identf", [128, 128], F32)
    S("onesF", [128, 128], F32)
    return sb


def init_const(p, tn, segs):
    sb = tn["sb"]
    for name in weights_needed(segs):
        if name in SMALLS:
            mir = sb[name + "_s"]
            src = tn[name]
            p.dma("sync", mir[:], src[:])
    gop(p, "memset", sb["ones_col"][:], 1.0)
    gop(p, "memset", sb["ones_row"][:], 1.0)
    gop(p, "memset", sb["onesb_col"][:], 1.0)
    gop(p, "memset", sb["onesb_row"][:], 1.0)
    gop(p, "memset", sb["ident"][:], 1.0)
    gop(p, "affine_select", sb["ident"][:], sb["ident"][:], [[-1, 128]],
        ALU.is_equal, 0.0, base=0, channel_multiplier=1)
    gop(p, "memset", sb["tri"][:], 0.0)
    gop(p, "affine_select", sb["tri"][:], sb["tri"][:], [[-1, 128]],
        ALU.is_ge, -1e30, base=0, channel_multiplier=1)
    gop(p, "memset", sb["identf"][:], 1.0)
    gop(p, "affine_select", sb["identf"][:], sb["identf"][:], [[-1, 128]],
        ALU.is_equal, 0.0, base=0, channel_multiplier=1)
    gop(p, "memset", sb["onesF"][:], 1.0)


def build_nc(segs):
    consume, produce = set(), set()
    for s in segs:
        c, pr = SEG_IO[s]
        consume.update(x for x in c if x not in produce)
        produce.update(pr)
    nc = bass.Bass("TRN2", target_bir_lowering=False)
    stack = ExitStack()
    with stack:
        tn = declare_tensors(nc, segs, produce, consume)
        sb = alloc_sbuf(nc, stack, segs)
        tn["sb"] = sb
        tn["PS_A"] = stack.enter_context(nc.psum_tensor("PS_A", [128, 2048], F32))
        tn["PS_B"] = stack.enter_context(nc.psum_tensor("PS_B", [128, 1024], F32))
        tn["PS_T"] = stack.enter_context(nc.psum_tensor("PS_T", [128, 1024], BF16))
        tn["PS_O"] = stack.enter_context(nc.psum_tensor("PS_O", [128, 512], F32))
        tn["nc"] = nc
        with nc.Block() as block:
            p = Prog(nc, stack)
            init_const(p, tn, segs)
            for seq in range(NSEQ):
                for s in segs:
                    with ExitStack() as segstack:
                        tn["segstack"] = segstack
                        EMITTERS[s](p, tn, seq)
            p.finalize(block)
    return nc


def _emit_ln0(p, tn, seq):
    emit_ln(p, tn, seq, 0)


def _emit_ln1(p, tn, seq):
    emit_ln(p, tn, seq, 1)


EMITTERS["ln0"] = _emit_ln0
EMITTERS["ln1"] = _emit_ln1



ALL_SEGS = ["ln0", "attn0", "moe0", "ln1", "attn1", "ffn1a", "ffn1b"]


def kernel(**inputs):
    w = prep_weights(inputs)
    nc = build_nc(ALL_SEGS)
    need = weights_needed(ALL_SEGS)
    in_maps = []
    for core in range(NCORE):
        im = {"xin": prep_xin(inputs, core)}
        for n in need:
            im[n] = w[n]
        in_maps.append(im)
    res = run_bass_kernel_spmd(nc, in_maps, core_ids=list(range(NCORE)))
    out = np.empty((16, 2048, 1024), np.float32)
    for core in range(NCORE):
        y = np.asarray(res.results[core]["yout"])
        out[2 * core:2 * core + 2] = y.transpose(0, 2, 1)
    return out


# version 03
# speedup vs baseline: 1.3871x; 1.3871x over previous
import numpy as np
from contextlib import ExitStack

import concourse.bass as bass
import concourse.mybir as mybir
from concourse.bass_utils import run_bass_kernel_spmd

F32 = mybir.dt.float32
BF16 = mybir.dt.bfloat16
AF = mybir.ActivationFunctionType
ALU = mybir.AluOpType
AX = mybir.AxisListType

NCORE = 8
SERIAL_MODE = False
DMA_FULL_SERIAL = False
NSEQ = 2
D = 1024
T = 2048
NCH = D // 128
NH = 16
DH = 64
NTG = T // 512
NQT = T // 128


def tile_w(W, oc=128):
    IN, OUT = W.shape
    return np.ascontiguousarray(
        W.reshape(IN // 128, 128, OUT // oc, oc).transpose(2, 1, 0, 3)
        .reshape(OUT // oc, 128, (IN // 128) * oc))


def colvec(v):
    return np.ascontiguousarray(np.asarray(v).reshape(-1, 128).T)


def prep_weights(inp):
    w = {}
    w["w_qkv0"] = tile_w(inp["L0_attn_W_qkv"])
    w["b_qkv0"] = colvec(inp["L0_attn_b_qkv"])
    w["w_proj0"] = tile_w(inp["L0_attn_W_proj"])
    w["b_proj0"] = colvec(inp["L0_attn_b_proj"])
    w["ln0w"] = colvec(inp["L0_attn_ln_w"])
    w["ln0b"] = colvec(inp["L0_attn_ln_b"])
    w["rw0"] = tile_w(inp["L0_ffn_router_w"], oc=8)
    w["w1_0"] = np.stack([tile_w(inp["L0_ffn_W1"][e]) for e in range(8)])
    w["b1_0"] = np.concatenate([colvec(inp["L0_ffn_b1"][e]) for e in range(8)], axis=1)
    w["w2_0"] = np.stack([tile_w(inp["L0_ffn_W2"][e]) for e in range(8)])
    w["b2_0"] = np.concatenate([colvec(inp["L0_ffn_b2"][e]) for e in range(8)], axis=1)
    w["w_attn1"] = tile_w(inp["L1_attn_c_attn_w"])
    w["b_attn1"] = colvec(inp["L1_attn_c_attn_b"])
    w["w_proj1"] = tile_w(inp["L1_attn_c_proj_w"])
    w["b_proj1"] = colvec(inp["L1_attn_c_proj_b"])
    w["wkc1T"] = np.ascontiguousarray(inp["L1_attn_wk_c"].T)
    w["wke1"] = np.ascontiguousarray(inp["L1_attn_wk_e"])
    w["wvc1T"] = np.ascontiguousarray(inp["L1_attn_wv_c"].T)
    w["wve1"] = np.ascontiguousarray(inp["L1_attn_wv_e"])
    w["n1w"] = colvec(inp["L1_ffn_n1_w"])
    w["n2w"] = colvec(inp["L1_ffn_n2_w"])
    w["wq1"] = tile_w(inp["L1_ffn_wq"].T)
    w["wk1"] = tile_w(inp["L1_ffn_wk"].T)
    w["wv1"] = tile_w(inp["L1_ffn_wv"].T)
    w["wo1"] = tile_w(inp["L1_ffn_wo"].T)
    w["qn1"] = np.ascontiguousarray(np.tile(inp["L1_ffn_qn_w"], 2)[:, None])
    w["kn1"] = np.ascontiguousarray(np.tile(inp["L1_ffn_kn_w"], 2)[:, None])
    w["gw1"] = tile_w(inp["L1_ffn_gate_w"].T, oc=4)
    w["e0up"] = tile_w(inp["L1_ffn_e0_up"].T)
    w["e0dn"] = tile_w(inp["L1_ffn_e0_down"].T)
    w["e1w1"] = tile_w(inp["L1_ffn_e1_w1"].T)
    w["e1w2"] = tile_w(inp["L1_ffn_e1_w2"].T)
    w["e1dn"] = tile_w(inp["L1_ffn_e1_down"].T)
    w["e2l1"] = tile_w(inp["L1_ffn_e2_l1"].T)
    w["e2l2"] = tile_w(inp["L1_ffn_e2_l2"].T)
    w["e2l3"] = tile_w(inp["L1_ffn_e2_l3"].T)
    w["e2l4"] = tile_w(inp["L1_ffn_e2_l4"].T)
    w["e3dn"] = tile_w(inp["L1_ffn_e3_down"].T)
    w["e3up"] = tile_w(inp["L1_ffn_e3_up"].T)
    w["e3out"] = tile_w(inp["L1_ffn_e3_out"].T)
    return w


def prep_xin(inp, core):
    z = inp["L0_attn_z"][2 * core:2 * core + 2]
    return np.ascontiguousarray(z.transpose(0, 2, 1))



def _is_ap(x):
    return hasattr(x, "tensor") and hasattr(x, "ap") and hasattr(x, "offset")


def _ap_keys(ap):
    t = ap.tensor
    name = t.name
    kind = type(t).__name__
    dims = [(int(s), int(c)) for s, c in ap.ap]
    off = int(ap.offset)
    if kind == "PSumTensorHandle" or kind == "SBTensorHandle":
        F = int(t.shape[-1])
        pcount = 1
        free = []
        for s, c in dims:
            if s % F == 0 and s != 0 and c > 1:
                pcount = max(pcount, ((c - 1) * s) // F + 1)
            else:
                free.append((s, c))
        p0 = off // F
        c0 = off % F
        cspan = 1 + sum(s * (c - 1) for s, c in free)
        if c0 + cspan > F:
            p0, pcount, c0, cspan = 0, 128, 0, F
        if kind == "PSumTensorHandle":
            bank = 512 if str(t.dtype) == "dt.float32" else 1024
            return {(name, b) for b in range(c0 // bank, (c0 + cspan - 1) // bank + 1)}
        keys = set()
        for pb in range(p0 // 32, (p0 + pcount - 1) // 32 + 1):
            for cb in range(c0 // 512, (c0 + cspan - 1) // 512 + 1):
                keys.add((name, pb, cb))
        return keys
    span = 1 + sum(s * (c - 1) for s, c in dims)
    return {(name, "f", b) for b in range(off // 262144, (off + span - 1) // 262144 + 1)}


class Prog:
    ENG = ("sync", "gpsimd", "scalar", "vector", "tensor")

    def __init__(self, nc, stack, serial=False):
        self.nc = nc
        self.serial = serial
        self.steps = []
        self.writer = {}
        self.readers = {}
        self.sems = {}
        for e in self.ENG:
            self.sems[e] = stack.enter_context(nc.semaphore("sem_" + e))
        self.NSLOT = 8
        for q in ("dsync", "dgps"):
            for s in range(self.NSLOT):
                self.sems[(q, s)] = stack.enter_context(
                    nc.semaphore(f"sem_{q}{s}"))
        self.dma_count = {"sync": 0, "gpsimd": 0}
        self.last_slot_step = {}
        self.last_by_sem = {}
        self.pending_fence = None
        self.fence_done = set()

    def do(self, eng, fn, reads=(), writes=(), dma=False):
        idx = len(self.steps)
        if dma:
            qname = {"sync": "dsync", "gpsimd": "dgps"}[eng]
            slot = self.dma_count[eng] % self.NSLOT
            self.dma_count[eng] += 1
            sem_id = (qname, slot)
        else:
            sem_id = eng
        edges = set()
        if dma:
            key = qname if DMA_FULL_SERIAL else sem_id
            prev = self.last_slot_step.get(key)
            if prev is not None:
                edges.add(prev)
            self.last_slot_step[key] = idx
        rkeys = set()
        for ap in reads:
            rkeys |= _ap_keys(ap)
        wkeys = set()
        for ap in writes:
            wkeys |= _ap_keys(ap)
        if self.serial and idx > 0:
            edges.add(idx - 1)
        for k in rkeys:
            w = self.writer.get(k)
            if w is not None:
                edges.add(w)
        for k in wkeys:
            w = self.writer.get(k)
            if w is not None:
                edges.add(w)
            for r in self.readers.get(k, ()):
                edges.add(r)
        pruned = edges
        if self.pending_fence is not None and eng not in self.fence_done:
            for pi in self.pending_fence.values():
                pruned.add(pi)
            self.fence_done.add(eng)
            if len(self.fence_done) == len(self.ENG):
                self.pending_fence = None
        for k in rkeys:
            self.readers.setdefault(k, []).append(idx)
        for k in wkeys:
            self.writer[k] = idx
            self.readers[k] = []
        self.last_by_sem[sem_id] = idx
        self.steps.append((eng, fn, sem_id, 16 if dma else 1, pruned))

    def fence(self):
        self.pending_fence = dict(self.last_by_sem)
        self.fence_done = set()

    def dma(self, q, out, in_):
        self.do(q, lambda e, o=out, i=in_: e.dma_start(out=o, in_=i),
                reads=[in_], writes=[out], dma=True)

    def finalize(self, block):
        steps = self.steps
        n = len(steps)
        need = [False] * n
        for eng, fn, sem_id, amt, edges in steps:
            for e in edges:
                need[e] = True
        for i, (eng, fn, sem_id, amt, edges) in enumerate(steps):
            if amt == 16:
                need[i] = True
        val = [0] * n
        cum = {}
        for i, (eng, fn, sem_id, amt, edges) in enumerate(steps):
            if need[i]:
                cum[sem_id] = cum.get(sem_id, 0) + amt
            val[i] = cum.get(sem_id, 0)
        totals = dict(cum)
        per_eng = {e: [] for e in self.ENG}
        for i, (eng, fn, sem_id, amt, edges) in enumerate(steps):
            per_eng[eng].append(i)
        sems = self.sems

        def make_body(eng, idxs):
            def body(engh):
                known = {}
                for i in idxs:
                    _, fn, sem_id, amt, edges = steps[i]
                    waits = {}
                    for e in edges:
                        psem = steps[e][2]
                        v = val[e]
                        if v > waits.get(psem, 0):
                            waits[psem] = v
                    for psem, v in sorted(waits.items(), key=lambda kv: str(kv[0])):
                        if known.get(psem, 0) >= v:
                            continue
                        engh.wait_ge(sems[psem], v)
                        known[psem] = v
                    ins = fn(engh)
                    if need[i]:
                        if ins is None:
                            ins = engh.nop()
                        ins.then_inc(sems[sem_id], amt)
                return None
            return body

        for e in self.ENG:
            if per_eng[e]:
                block.__getattribute__(e)(make_body(e, per_eng[e]))

        def tail(engh):
            for q, tot in totals.items():
                if isinstance(q, tuple) and tot:
                    engh.wait_ge(sems[q], tot)
            for e in ("tensor", "vector", "scalar", "sync"):
                if totals.get(e):
                    engh.wait_ge(sems[e], totals[e])
        block.gpsimd(tail)



def mm(p, out_ps, lhsT, rhs, start, stop):
    p.do("tensor", lambda t: t.matmul(out_ps, lhsT, rhs, start=start, stop=stop),
         reads=[lhsT, rhs], writes=[out_ps])


def mm_group(p, group):
    def fn(t):
        last = None
        for o, l, r, st, sp in group:
            last = t.matmul(o, l, r, start=st, stop=sp)
        return last
    reads = [g[1] for g in group] + [g[2] for g in group]
    writes = [g[0] for g in group]
    p.do("tensor", fn, reads=reads, writes=writes)


def trgroup(p, items, ident):
    def fn(t):
        last = None
        for o, i in items:
            last = t.transpose(o, i, ident)
        return last
    p.do("tensor", fn, reads=[i for _, i in items] + [ident],
         writes=[o for o, _ in items])


def _roles(args, kw):
    writes = [a for a in args[:1] if _is_ap(a)]
    reads = [a for a in args[1:] if _is_ap(a)]
    for k, v in kw.items():
        if _is_ap(v):
            (writes if k == "accum_out" else reads).append(v)
    return reads, writes


def vop(p, name, *args, **kw):
    reads, writes = _roles(args, kw)
    p.do("vector", lambda e: getattr(e, name)(*args, **kw),
         reads=reads, writes=writes)


def sop(p, name, *args, **kw):
    reads, writes = _roles(args, kw)
    p.do("scalar", lambda e: getattr(e, name)(*args, **kw),
         reads=reads, writes=writes)


def gop(p, name, *args, **kw):
    reads, writes = _roles(args, kw)
    p.do("gpsimd", lambda e: getattr(e, name)(*args, **kw),
         reads=reads, writes=writes)


EMITTERS = {}
_UNIQ = [0]


def uname(n):
    _UNIQ[0] += 1
    return f"{n}_{_UNIQ[0]}"


def emit_ln(p, tn, seq, layer):
    x = tn["xin"] if layer == 0 else tn["xcur"]
    sb = tn["sb"]
    st = tn["segstack"]
    nc = tn["nc"]
    xf = [st.enter_context(nc.sbuf_tensor(uname(f"lnx{c}"), [128, T], F32))
          for c in range(NCH)]
    xsq = [st.enter_context(nc.sbuf_tensor(uname(f"lnsq{i}"), [128, T], F32))
           for i in range(2)]
    t1 = [st.enter_context(nc.sbuf_tensor(uname(f"lnt1{i}"), [128, T], F32))
          for i in range(2)]
    znt = [st.enter_context(nc.sbuf_tensor(uname(f"lnzn{i}"), [128, T], BF16))
           for i in range(2)]
    R = st.enter_context(nc.sbuf_tensor(uname("lnrows"), [1, 4 * T], F32))
    Ab = st.enter_context(nc.sbuf_tensor(uname("lnAb"), [128, T], F32))
    Bb = st.enter_context(nc.sbuf_tensor(uname("lnBb"), [128, T], F32))
    Cb = st.enter_context(nc.sbuf_tensor(uname("lnCb"), [128, T], F32))
    ones_col = sb["ones_col"]
    ones_row = sb["ones_row"]
    PS_A = tn["PS_A"]
    PS_B = tn["PS_B"]
    R1 = R[0:1, 0:2048]; R2 = R[0:1, 2048:4096]
    R3 = R[0:1, 4096:6144]; R4 = R[0:1, 6144:8192]

    for c in range(NCH):
        p.dma("sync", xf[c][:], x[seq, c * 128:(c + 1) * 128, :])
    for c in range(NCH):
        mm_group(p, [(PS_A[0:1, g * 512:(g + 1) * 512], ones_col[:],
                      xf[c][:, g * 512:(g + 1) * 512], c == 0, c == NCH - 1)
                     for g in range(NTG)])
    vop(p, "tensor_copy", R1, PS_A[0:1, :])
    for c in range(NCH):
        sop(p, "square", xsq[c % 2][:], xf[c][:])
        mm_group(p, [(PS_A[0:1, g * 512:(g + 1) * 512], ones_col[:],
                      xsq[c % 2][:, g * 512:(g + 1) * 512], c == 0, c == NCH - 1)
                     for g in range(NTG)])
    vop(p, "tensor_copy", R2, PS_A[0:1, :])
    vop(p, "tensor_scalar_mul", R1, R1, 1.0 / D)
    vop(p, "tensor_scalar_mul", R2, R2, 1.0 / D)
    vop(p, "tensor_mul", R3, R1, R1)
    vop(p, "tensor_sub", R2, R2, R3)
    vop(p, "tensor_scalar_add", R3, R2, 1e-5)
    sop(p, "sqrt", R3, R3)
    vop(p, "reciprocal", R3, R3)
    vop(p, "tensor_mul", R4, R1, R3)
    def bcast(row, dst):
        for g in range(NTG):
            mm(p, PS_B[:, 0:512], ones_row[:], row[0:1, g * 512:(g + 1) * 512],
               True, True)
            vop(p, "tensor_copy", dst[:, g * 512:(g + 1) * 512], PS_B[:, 0:512])
    bcast(R3, Ab)
    bcast(R4, Bb)
    eps_i = 1e-5 if layer == 0 else 1.1920929e-07
    vop(p, "tensor_mul", R1, R3, R3)
    vop(p, "tensor_mul", R1, R1, R2)
    vop(p, "tensor_scalar_add", R1, R1, eps_i)
    sop(p, "sqrt", R1, R1)
    vop(p, "reciprocal", R1, R1)
    bcast(R1, Cb)
    wcol = sb["ln0w_s"] if layer == 0 else sb["n1w_s"]
    bcol = sb["ln0b_s"] if layer == 0 else None
    for c in range(NCH):
        tb, zb = t1[c % 2], znt[c % 2]
        vop(p, "tensor_mul", tb[:], xf[c][:], Ab[:])
        vop(p, "tensor_sub", tb[:], tb[:], Bb[:])
        p.dma("sync", tn["xn32"][seq, c * 128:(c + 1) * 128, :], tb[:])
        vop(p, "tensor_mul", tb[:], tb[:], Cb[:])
        if bcol is not None:
            vop(p, "tensor_scalar", zb[:], tb[:], wcol[:, c:c + 1],
                bcol[:, c:c + 1], ALU.mult, ALU.add)
        else:
            vop(p, "tensor_scalar_mul", zb[:], tb[:], wcol[:, c:c + 1])
        p.dma("sync", tn["znbf"][seq, c * 128:(c + 1) * 128, :], zb[:])



def emit_attn(p, tn, seq, cfg):
    nc = tn["nc"]
    st = tn["segstack"]
    sb = tn["sb"]
    PS_A, PS_B, PS_O = tn["PS_A"], tn["PS_B"], tn["PS_O"]
    PS_T = [tn["PS_T"], tn["PS_T2"]]
    ident = sb["ident"]
    ones_col = sb["ones_col"]
    ones_row = sb["ones_row"]
    causal = cfg["causal"]
    scale = 0.125

    def S_(name, shape, dt):
        return st.enter_context(nc.sbuf_tensor(uname(name), shape, dt))

    zn = [S_(f"at_zn{c}", [128, T], BF16) for c in range(NCH)]
    Ob = [S_(f"at_o{c}", [128, T], BF16) for c in range(NCH)]
    q2 = S_("at_q2", [128, T], BF16)
    k2 = S_("at_k2", [128, T], BF16)
    v2 = S_("at_v2", [128, T], BF16)
    vtm = S_("at_vtm", [128, T], BF16)
    Pt = [S_("at_p0", [128, T], BF16), S_("at_p1", [128, T], BF16)]
    ptq = [S_("at_pt0", [128, T], BF16), S_("at_pt1", [128, T], BF16)]
    wt = [S_("at_wt0", [128, 3 * 1024], BF16), S_("at_wt1", [128, 3 * 1024], BF16)]
    stt4 = S_("at_st", [128, 8], F32)
    bsq = S_("at_bsq", [128, T], F32)
    nbf = S_("at_nbf", [128, T], F32)
    nbias = S_("at_nb", [128, 2 * NQT], F32)
    brow = S_("at_brow", [1, T + 8], F32)
    qk_rms = cfg.get("qk_rms")
    if qk_rms:
        rqb = S_("at_rqb", [128, T], F32)
        rkb = S_("at_rkb", [128, T], F32)
        R = S_("at_rows", [1, T], F32)
    if cfg.get("latent"):
        q2p = S_("at_q2p", [128, T], BF16)
        v2e = S_("at_v2e", [128, T], BF16)
        lw = S_("at_lw", [32, 4 * 64], BF16)
        W64 = S_("at_w64", [128, 128], BF16)
        p.dma("gpsimd", lw[:, 0:64], tn["wke1"][:])
        p.dma("gpsimd", lw[:, 64:128], tn["wkc1T"][:])
        p.dma("gpsimd", lw[:, 128:192], tn["wvc1T"][:])
        p.dma("gpsimd", lw[:, 192:256], tn["wve1"][:])
        for b64 in (0, 64):
            mm(p, PS_O[b64:b64 + 64, 0:64], lw[:, 0:64], lw[:, 64:128], True, True)
            vop(p, "tensor_copy", W64[b64:b64 + 64, 0:64], PS_O[b64:b64 + 64, 0:64])
            mm(p, PS_O[b64:b64 + 64, 0:64], lw[:, 128:192], lw[:, 192:256], True, True)
            vop(p, "tensor_copy", W64[b64:b64 + 64, 64:128], PS_O[b64:b64 + 64, 0:64])

    gop(p, "memset", nbf[:], 0.0)
    if cfg["src"] == "xn32cast":
        for c in range(NCH):
            p.dma("gpsimd", zn[c][:], tn["xn32"][seq, c * 128:(c + 1) * 128, :])
    else:
        src = tn[cfg["src"]]
        for c in range(NCH):
            p.dma("sync", zn[c][:], src[seq, c * 128:(c + 1) * 128, :])

    bias = sb[cfg["qkv_bias"] + "_s"] if cfg.get("qkv_bias") else None

    for hp in range(8):
        wbuf = wt[hp % 2]
        if "fused_qkv" in cfg:
            wten = tn[cfg["fused_qkv"]]
            ocs = [hp, 8 + hp, 16 + hp]
            srcs = [wten[o] for o in ocs]
        else:
            srcs = [tn[n][hp] for n in cfg["sep_qkv"]]
            ocs = [hp, 8 + hp, 16 + hp]
        for i in range(3):
            p.dma("gpsimd", wbuf[:, i * 1024:(i + 1) * 1024], srcs[i][:, :])
        for i, dst in enumerate((q2, k2, v2)):
            if qk_rms and i < 2:
                mm_group(p, [(PS_A[:, g * 512:(g + 1) * 512],
                              wbuf[:, i * 1024 + c * 128:i * 1024 + (c + 1) * 128],
                              zn[c][:, g * 512:(g + 1) * 512], c == 0, c == NCH - 1)
                             for c in range(NCH) for g in range(NTG)])
                qn_s = sb[qk_rms[i] + "_s"]
                eps = qk_rms[2]
                rb = rqb if i == 0 else rkb
                sop(p, "square", bsq[:], PS_A[:, :])
                for h2 in range(2):
                    b64 = h2 * 64
                    Rrow = R[0:1, 0:T]
                    for g in range(NTG):
                        mm(p, PS_B[0:1, 0:512], ones_col[b64:b64 + 64, :],
                           bsq[b64:b64 + 64, g * 512:(g + 1) * 512], True, True)
                        vop(p, "tensor_copy", Rrow[0:1, g * 512:(g + 1) * 512],
                            PS_B[0:1, 0:512])
                    vop(p, "tensor_scalar_mul", Rrow, Rrow, 1.0 / 64)
                    vop(p, "tensor_scalar_add", Rrow, Rrow, eps)
                    sop(p, "sqrt", Rrow, Rrow)
                    vop(p, "reciprocal", Rrow, Rrow)
                    for g in range(NTG):
                        mm(p, PS_B[:, 0:512], ones_row[:],
                           Rrow[0:1, g * 512:(g + 1) * 512], True, True)
                        vop(p, "tensor_copy",
                            rb[b64:b64 + 64, g * 512:(g + 1) * 512],
                            PS_B[b64:b64 + 64, 0:512])
                vop(p, "scalar_tensor_tensor", dst[:], PS_A[:, :],
                    qn_s[:, 0:1], rb[:], ALU.mult, ALU.mult)
            else:
                for h in range(2):
                    hsl = slice(h * 1024, (h + 1) * 1024)
                    mm_group(p, [(PS_A[:, h * 1024 + g * 512:h * 1024 + (g + 1) * 512],
                                  wbuf[:, i * 1024 + c * 128:i * 1024 + (c + 1) * 128],
                                  zn[c][:, h * 1024 + g * 512:h * 1024 + (g + 1) * 512],
                                  c == 0, c == NCH - 1)
                                 for c in range(NCH) for g in range(2)])
                    if bias is not None:
                        vop(p, "tensor_scalar_add", dst[:, hsl], PS_A[:, hsl],
                            bias[:, ocs[i]:ocs[i] + 1])
                    else:
                        vop(p, "tensor_copy", dst[:, hsl], PS_A[:, hsl])
        qsrc, vsrc = q2, v2
        if cfg.get("latent"):
            for h2 in range(2):
                b64 = h2 * 64
                for g in range(NTG):
                    gsl = slice(g * 512, (g + 1) * 512)
                    mm(p, PS_B[b64:b64 + 64, 0:512], W64[b64:b64 + 64, 0:64],
                       q2[b64:b64 + 64, gsl], True, True)
                    vop(p, "tensor_copy", q2p[b64:b64 + 64, gsl],
                        PS_B[b64:b64 + 64, 0:512])
                    mm(p, PS_B[b64:b64 + 64, 0:512], W64[b64:b64 + 64, 64:128],
                       v2[b64:b64 + 64, gsl], True, True)
                    vop(p, "tensor_copy", v2e[b64:b64 + 64, gsl],
                        PS_B[b64:b64 + 64, 0:512])
            qsrc, vsrc = q2p, v2e
        for half in range(2):
            trgroup(p, [(PS_T[half][:, i * 128:(i + 1) * 128],
                         vsrc[:, (half * 8 + i) * 128:(half * 8 + i + 1) * 128])
                        for i in range(8)], ident[:])
            vop(p, "tensor_copy", vtm[:, half * 1024:(half + 1) * 1024],
                PS_T[half][:, :])
        sop(p, "square", bsq[:], qsrc[:, :])
        for h2 in range(2):
            b64 = h2 * 64
            for g in range(NTG):
                mm(p, PS_B[0:1, 0:512], ones_col[b64:b64 + 64, :],
                   bsq[b64:b64 + 64, g * 512:(g + 1) * 512], True, True)
                vop(p, "tensor_copy", brow[0:1, g * 512:(g + 1) * 512],
                    PS_B[0:1, 0:512])
            vop(p, "tensor_copy", nbf[0:1, 0:T], brow[0:1, 0:T])
            sop(p, "square", bsq[b64:b64 + 64, :], k2[b64:b64 + 64, :])
            for g in range(NTG):
                mm(p, PS_B[0:1, 0:512], ones_col[b64:b64 + 64, :],
                   bsq[b64:b64 + 64, g * 512:(g + 1) * 512], True, True)
                vop(p, "tensor_copy", brow[0:1, g * 512:(g + 1) * 512],
                    PS_B[0:1, 0:512])
            vop(p, "reduce_max", brow[0:1, T:T + 1], brow[0:1, 0:T], AX.X)
            vop(p, "tensor_scalar", nbf[0:1, 0:T], nbf[0:1, 0:T],
                brow[0:1, T:T + 1], -0.0625, ALU.add, ALU.mult)
            for r4 in range(4):
                trgroup(p, [(PS_B[:, i * 128:(i + 1) * 128],
                             nbf[:, (r4 * 4 + i) * 128:(r4 * 4 + i + 1) * 128])
                            for i in range(4)], sb["identf"][:])
                vop(p, "tensor_copy",
                    nbias[:, h2 * NQT + r4 * 4:h2 * NQT + r4 * 4 + 4],
                    PS_B[:, 0:512].rearrange("p (i c) -> p i c", c=128)[:, :, 0:1])
        if hp == 0 and "dbg" in tn:
            p.dma("sync", tn["dbg"][0:128, 0:32], nbias[:, :])
            p.dma("gpsimd", tn["dbg"][0:128, 32:2080], q2[:, :])
            p.dma("gpsimd", tn["dbg"][0:128, 2080:4128], k2[:, :])
        for h2 in range(2):
            b64 = h2 * 64
            for qt in range(NQT):
                par = qt % 2
                nk = (qt + 1) * 128 if causal else T
                bcol = nbias[:, h2 * NQT + qt:h2 * NQT + qt + 1]
                lsum = stt4[:, par * 4:par * 4 + 1]
                lsum2 = stt4[:, par * 4 + 1:par * 4 + 2]
                linv = stt4[:, par * 4 + 2:par * 4 + 3]
                if nk <= 1024:
                    spans = [(par * 1024, 0, nk)]
                else:
                    spans = [(0, 0, 1024), (1024, 1024, nk - 1024)]
                for si, (base, lo, ln_) in enumerate(spans):
                    mm_group(p, [(PS_A[:, base + g * 512:base + min((g + 1) * 512, ln_)],
                                  qsrc[b64:b64 + 64, qt * 128:(qt + 1) * 128],
                                  k2[b64:b64 + 64, lo + g * 512:lo + min((g + 1) * 512, ln_)],
                                  True, True) for g in range((ln_ + 511) // 512)])
                    if causal and lo + ln_ == nk:
                        vop(p, "tensor_add", PS_A[:, base + ln_ - 128:base + ln_],
                            PS_A[:, base + ln_ - 128:base + ln_], sb["tri"][:])
                    sop(p, "activation", Pt[par][:, lo:lo + ln_],
                        PS_A[:, base:base + ln_], AF.Exp,
                        bias=bcol, scale=scale,
                        accum_out=(lsum if si == 0 else lsum2))
                if len(spans) > 1:
                    vop(p, "tensor_add", lsum, lsum, lsum2)
                vop(p, "reciprocal", linv, lsum)
                vop(p, "tensor_scalar_mul", Pt[par][:, 0:nk], Pt[par][:, 0:nk],
                    linv)
                ktn = nk // 128
                for half in range((ktn + 7) // 8):
                    n8 = min(8, ktn - half * 8)
                    trgroup(p, [(PS_T[half][:, i * 128:(i + 1) * 128],
                                 Pt[par][:, (half * 8 + i) * 128:(half * 8 + i + 1) * 128])
                                for i in range(n8)], ident[:])
                    vop(p, "tensor_copy",
                        ptq[par][:, half * 1024:half * 1024 + n8 * 128],
                        PS_T[half][:, 0:n8 * 128])
                pvps = PS_O if par == 0 else PS_B
                mm_group(p, [(pvps[0:64, 0:128],
                              vtm[:, kt * 128 + b64:kt * 128 + b64 + 64],
                              ptq[par][:, kt * 128:(kt + 1) * 128],
                              kt == 0, kt == ktn - 1) for kt in range(ktn)])
                vop(p, "tensor_copy",
                    Ob[hp][b64:b64 + 64, qt * 128:(qt + 1) * 128],
                    pvps[0:64, 0:128])
                if hp == 0 and h2 == 0 and qt == 0 and "dbg" in tn:
                    p.dma("gpsimd", tn["dbg"][0:128, 4128:6176], Pt[par][:, :])
                    p.dma("sync", tn["dbg"][0:128, 6176:6184], stt4[:, :])
    pbias = sb[cfg["proj_bias"] + "_s"] if cfg.get("proj_bias") else None
    xt = [S_("at_xt0", [128, 512], F32), S_("at_xt1", [128, 512], F32)]
    s_t = [S_("at_s0", [128, 512], F32), S_("at_s1", [128, 512], F32)]
    wp = tn[cfg["proj_w"]]
    out = tn[cfg["out"]]
    for oc in range(NCH):
        if oc in cfg.get("skip_oc", ()):
            for g in range(NTG):
                gsl = slice(g * 512, (g + 1) * 512)
                par = g % 2
                if cfg.get("add_xn"):
                    p.dma("sync", xt[par][:], tn["xn32"][seq, oc * 128:(oc + 1) * 128, gsl])
                    p.dma("sync", out[seq, oc * 128:(oc + 1) * 128, gsl], xt[par][:])
                else:
                    gop(p, "memset", s_t[par][:], 0.0)
                    p.dma("sync", out[seq, oc * 128:(oc + 1) * 128, gsl], s_t[par][:])
            continue
        p.dma("gpsimd", wt[oc % 2][:, 0:1024], wp[oc][:, :])
        for g in range(NTG):
            gsl = slice(g * 512, (g + 1) * 512)
            par = (oc * NTG + g) % 2
            pps = PS_O if par == 0 else PS_B
            mm_group(p, [(pps[:, 0:512], wt[oc % 2][:, c * 128:(c + 1) * 128],
                          Ob[c][:, gsl], c == 0, c == NCH - 1)
                         for c in range(NCH)])
            if cfg.get("add_xn"):
                p.dma("sync", xt[par][:], tn["xn32"][seq, oc * 128:(oc + 1) * 128, gsl])
                if pbias is not None:
                    vop(p, "scalar_tensor_tensor", s_t[par][:], pps[:, 0:512],
                        pbias[:, oc:oc + 1], xt[par][:], ALU.add, ALU.add)
                else:
                    vop(p, "tensor_add", s_t[par][:], pps[:, 0:512], xt[par][:])
            else:
                if pbias is not None:
                    vop(p, "tensor_scalar_add", s_t[par][:], pps[:, 0:512],
                        pbias[:, oc:oc + 1])
                else:
                    vop(p, "tensor_copy", s_t[par][:], pps[:, 0:512])
            p.dma("sync", out[seq, oc * 128:(oc + 1) * 128, gsl], s_t[par][:])


def _emit_attn0(p, tn, seq):
    emit_attn(p, tn, seq, dict(
        src="znbf", fused_qkv="w_qkv0", qkv_bias="b_qkv0", causal=False,
        proj_w="w_proj0", proj_bias="b_proj0", add_xn=True, out="sbr"))


def _emit_attn1(p, tn, seq):
    emit_attn(p, tn, seq, dict(
        src="xn32cast", fused_qkv="w_attn1", qkv_bias="b_attn1", causal=True,
        latent=True, proj_w="w_proj1", proj_bias="b_proj1", add_xn=False,
        out="sbr"))


def _emit_ffn1a(p, tn, seq):
    emit_attn(p, tn, seq, dict(
        src="znbf", sep_qkv=("wq1", "wk1", "wv1"), causal=True,
        qk_rms=("qn1", "kn1", 1.1920929e-07),
        proj_w="wo1", add_xn=True, skip_oc={0, 1}, out="xloc"))


EMITTERS["attn0"] = _emit_attn0
EMITTERS["attn1"] = _emit_attn1
EMITTERS["ffn1a"] = _emit_ffn1a



def emit_moe0(p, tn, seq):
    nc = tn["nc"]
    st = tn["segstack"]
    sb = tn["sb"]
    PS_A, PS_B, PS_T, PS_O = tn["PS_A"], tn["PS_B"], tn["PS_T"], tn["PS_O"]
    ones_row = sb["ones_row"]
    identf = sb["identf"]

    def S_(name, shape, dt):
        return st.enter_context(nc.sbuf_tensor(uname(name), shape, dt))

    xnb = [S_(f"mo_xn{c}", [128, T], BF16) for c in range(NCH)]
    act = [S_(f"mo_act{j}", [128, T], BF16) for j in range(21)]
    sg = S_("mo_sg", [128, T], BF16)
    wt = [S_("mo_wt0", [128, 1024], BF16), S_("mo_wt1", [128, 1024], BF16)]
    wt2 = [S_("mo_wt20", [128, 2688], BF16), S_("mo_wt21", [128, 2688], BF16)]
    rw = S_("mo_rw", [128, 64], F32)
    xc = [S_("mo_xc0", [128, T], F32), S_("mo_xc1", [128, T], F32)]
    route_f = S_("mo_rf", [128, T], F32)
    route_tm = S_("mo_rtm", [128, 128], F32)
    cw_tm = S_("mo_cwtm", [128, 128], F32)
    cw_fm = S_("mo_cwfm", [8, T], F32)
    eqt = S_("mo_eq", [128, 8], F32)
    mskt = S_("mo_msk", [128, 8], F32)
    stt8 = S_("mo_st", [128, 8], F32)
    cwrow = S_("mo_cwrow", [1, T], F32)
    cwb = S_("mo_cwb", [128, T], F32)
    y_t = [S_(f"mo_y{i}", [128, 512], F32) for i in range(2)]
    cmb = S_("mo_cmb", [128, T], F32)
    ft = route_f
    s_t = cwb

    for c in range(NCH):
        p.dma("gpsimd", xnb[c][:], tn["xn32"][seq, c * 128:(c + 1) * 128, :])
    p.dma("sync", rw[:], tn["rw0"][0][:, :])
    for c in range(NCH):
        p.dma("sync", xc[c % 2][:], tn["xn32"][seq, c * 128:(c + 1) * 128, :])
        mm_group(p, [(PS_A[0:8, g * 512:(g + 1) * 512], rw[:, c * 8:(c + 1) * 8],
                      xc[c % 2][:, g * 512:(g + 1) * 512], c == 0, c == NCH - 1)
                     for g in range(NTG)])
    gop(p, "memset", route_f[:], 0.0)
    vop(p, "tensor_copy", route_f[0:8, :], PS_A[0:8, :])
    for kt in range(NQT):
        trgroup(p, [(PS_B[:, 0:128], route_f[:, kt * 128:(kt + 1) * 128])], identf[:])
        vop(p, "tensor_copy", route_tm[:, kt * 8:(kt + 1) * 8], PS_B[:, 0:8])
    for kt in range(NQT):
        blk = route_tm[:, kt * 8:(kt + 1) * 8]
        m1 = stt8[:, 0:1]; negm2 = stt8[:, 1:2]; m2 = stt8[:, 2:3]
        negm1 = stt8[:, 3:4]; w1 = stt8[:, 4:5]; w2 = stt8[:, 5:6]
        vop(p, "reduce_max", m1, blk, AX.X)
        vop(p, "tensor_scalar", eqt[:], blk, m1, None, ALU.is_equal)
        vop(p, "scalar_tensor_tensor", mskt[:], eqt[:], -1e30, blk,
            ALU.mult, ALU.add)
        vop(p, "reduce_max", negm2, mskt[:], AX.X, negate=True)
        vop(p, "tensor_scalar_mul", m2, negm2, -1.0)
        vop(p, "tensor_scalar_mul", negm1, m1, -1.0)
        sop(p, "activation", w1, m1, AF.Sigmoid, bias=negm2, scale=1.0)
        sop(p, "activation", w2, m2, AF.Sigmoid, bias=negm1, scale=1.0)
        vop(p, "tensor_scalar", eqt[:], mskt[:], m2, None, ALU.is_equal)
        cwb_blk = cw_tm[:, kt * 8:(kt + 1) * 8]
        vop(p, "tensor_scalar", mskt[:], blk, m1, None, ALU.is_equal)
        vop(p, "tensor_scalar_mul", mskt[:], mskt[:], w1)
        vop(p, "tensor_scalar", eqt[:], eqt[:], w2, None, ALU.mult)
        vop(p, "tensor_add", cwb_blk, mskt[:], eqt[:])
    for kt in range(NQT):
        trgroup(p, [(PS_B[0:8, 0:128], cw_tm[:, kt * 8:(kt + 1) * 8])], identf[:])
        vop(p, "tensor_copy", cw_fm[0:8, kt * 128:(kt + 1) * 128], PS_B[0:8, 0:128])

    b1s = sb["b1_0_s"]
    b2s = sb["b2_0_s"]
    for e in range(8):
        p.dma("sync", cwrow[:], cw_fm[e:e + 1, :])
        for g in range(NTG):
            mm(p, PS_B[:, 0:512], ones_row[:], cwrow[0:1, g * 512:(g + 1) * 512],
               True, True)
            vop(p, "tensor_copy", cwb[:, g * 512:(g + 1) * 512], PS_B[:, 0:512])
        for j in range(21):
            for oc, kind in ((21 + j, "g"), (j, "a")):
                wbuf = wt[oc % 2]
                p.dma("gpsimd", wbuf[:], tn["w1_0"][e, oc][:, :])
                bcol = b1s[:, e * 42 + oc:e * 42 + oc + 1]
                for h in range(2):
                    hsl = slice(h * 1024, (h + 1) * 1024)
                    mm_group(p, [(PS_A[:, h * 1024 + g * 512:h * 1024 + (g + 1) * 512],
                                  wbuf[:, c * 128:(c + 1) * 128],
                                  xnb[c][:, h * 1024 + g * 512:h * 1024 + (g + 1) * 512],
                                  c == 0, c == NCH - 1)
                                 for c in range(NCH) for g in range(2)])
                    if kind == "g":
                        sop(p, "activation", sg[:, hsl], PS_A[:, hsl], AF.Silu,
                            bias=bcol, scale=1.0)
                    else:
                        vop(p, "scalar_tensor_tensor", act[j][:, hsl],
                            PS_A[:, hsl], bcol, sg[:, hsl], ALU.add, ALU.mult)
        for oc in range(NCH):
            w2b = wt2[oc % 2]
            p.dma("gpsimd", w2b[:], tn["w2_0"][e, oc][:, :])
            for g in range(NTG):
                gsl = slice(g * 512, (g + 1) * 512)
                par = (oc * NTG + g) % 2
                pps = PS_O if par == 0 else PS_B
                yb = y_t[(oc * NTG + g) % 2]
                mm_group(p, [(pps[:, 0:512], w2b[:, kc * 128:(kc + 1) * 128],
                              act[kc][:, gsl], kc == 0, kc == 20)
                             for kc in range(21)])
                vop(p, "scalar_tensor_tensor", yb[:], pps[:, 0:512],
                    b2s[:, e * 8 + oc:e * 8 + oc + 1], cwb[:, gsl],
                    ALU.add, ALU.mult)
                dst = tn["facc"][seq, oc * 128:(oc + 1) * 128, gsl]
                if e == 0:
                    p.dma("gpsimd", dst, yb[:])
                else:
                    p.do("gpsimd", lambda en, d=dst, s=yb: en.dma_start(
                        out=d, in_=s[:], accum_op=ALU.add),
                        reads=[yb[:], dst], writes=[dst], dma=True)
    for c in range(NCH):
        csl = slice(c * 128, (c + 1) * 128)
        fb = ft if c % 2 == 0 else cmb
        sb_ = s_t
        xb = xc[c % 2]
        p.dma("sync", fb[:], tn["facc"][seq, csl, :])
        p.dma("sync", sb_[:], tn["sbr"][seq, csl, :])
        p.dma("sync", xb[:], tn["xin"][seq, csl, :])
        sop(p, "activation", fb[:], fb[:], AF.Silu, bias=0.0, scale=1.0)
        vop(p, "tensor_mul", fb[:], fb[:], sb_[:])
        vop(p, "tensor_add", fb[:], fb[:], xb[:])
        p.dma("sync", tn["xcur"][seq, csl, :], fb[:])


EMITTERS["moe0"] = emit_moe0



def emit_ffn1b(p, tn, seq):
    nc = tn["nc"]
    st = tn["segstack"]
    sb = tn["sb"]
    PS_A, PS_B, PS_T, PS_O = tn["PS_A"], tn["PS_B"], tn["PS_T"], tn["PS_O"]
    identf = sb["identf"]
    onesF = sb["onesF"]
    ones_col = sb["ones_col"]
    EPS = 1.1920929e-07

    def S_(name, shape, dt):
        return st.enter_context(nc.sbuf_tensor(uname(name), shape, dt))

    xfb = [S_(f"fb_xf{c}", [128, T], BF16) for c in range(NCH)]
    A = [S_(f"fb_A{j}", [128, T], BF16) for j in range(16)]
    B = [S_(f"fb_B{j}", [128, T], BF16) for j in range(16)]
    wt = [S_("fb_wt0", [128, 2048], BF16), S_("fb_wt1", [128, 2048], BF16)]
    xlc = S_("fb_xl", [128, T], F32)
    gw2 = S_("fb_gw", [128, 32], F32)
    gc_f = S_("fb_gc", [128, T], F32)
    Ctm = S_("fb_ctm", [128, 16], F32)
    ltm = S_("fb_ltm", [128, 64], F32)
    cw_tm = S_("fb_cwtm", [128, 64], F32)
    cw_fm = S_("fb_cwfm", [8, T], F32)
    eqt = S_("fb_eq", [128, 4], F32)
    mskt = S_("fb_msk", [128, 4], F32)
    stt8 = S_("fb_st", [128, 8], F32)
    cwb = S_("fb_cwb", [128, T], F32)
    Cb = cwb[:, 0:512]
    sq = gc_f
    y_t = [gc_f[:, i * 512:(i + 1) * 512] for i in range(4)]
    fb_s = cwb
    n2s = sb["n2w_s"]
    cwrow = gc_f[0:1, 0:T]

    p.dma("sync", gw2[:], tn["gw1"][0][:, :])
    for c in range(NCH):
        vop(p, "tensor_scalar_mul", gw2[:, c * 4:(c + 1) * 4],
            gw2[:, c * 4:(c + 1) * 4], n2s[:, c:c + 1])
    for c in range(NCH):
        p.dma("sync", xlc[:], tn["xloc"][seq, c * 128:(c + 1) * 128, :])
        sop(p, "square", sq[:], xlc[:])
        mm_group(p, [(PS_A[64:65, g * 512:(g + 1) * 512], ones_col[:],
                      sq[:, g * 512:(g + 1) * 512], c == 0, c == NCH - 1)
                     for g in range(NTG)])
        mm_group(p, [(PS_A[0:4, g * 512:(g + 1) * 512], gw2[:, c * 4:(c + 1) * 4],
                      xlc[:, g * 512:(g + 1) * 512], c == 0, c == NCH - 1)
                     for g in range(NTG)])
    gop(p, "memset", gc_f[:], 0.0)
    Crow = gc_f[64:65, :]
    vop(p, "tensor_copy", Crow, PS_A[64:65, :])
    vop(p, "tensor_scalar_mul", Crow, Crow, 1.0 / D)
    vop(p, "tensor_scalar_add", Crow, Crow, EPS)
    sop(p, "sqrt", Crow, Crow)
    vop(p, "reciprocal", Crow, Crow)
    vop(p, "tensor_copy", gc_f[0:4, :], PS_A[0:4, :])
    for kt in range(NQT):
        trgroup(p, [(PS_B[:, 0:128], gc_f[:, kt * 128:(kt + 1) * 128])], identf[:])
        vop(p, "tensor_copy", ltm[:, kt * 4:(kt + 1) * 4], PS_B[:, 0:4])
        vop(p, "tensor_copy", Ctm[:, kt:kt + 1], PS_B[:, 64:65])
    for g in range(NTG):
        gsl = slice(g * 512, (g + 1) * 512)
        mm(p, PS_B[:, 0:512], onesF[64:65, 0:128], Crow[0:1, gsl], True, True)
        vop(p, "tensor_copy", Cb, PS_B[:, 0:512])
        for c in range(NCH):
            p.dma("sync", xlc[:, 0:512], tn["xloc"][seq, c * 128:(c + 1) * 128, gsl])
            vop(p, "scalar_tensor_tensor", xfb[c][:, gsl], xlc[:, 0:512],
                n2s[:, c:c + 1], Cb, ALU.mult, ALU.mult)
    for kt in range(NQT):
        blk = ltm[:, kt * 4:(kt + 1) * 4]
        vop(p, "tensor_scalar_mul", blk, blk, Ctm[:, kt:kt + 1])
        negm = stt8[:, 0:1]; Z = stt8[:, 1:2]; rZ = stt8[:, 2:3]
        m1 = stt8[:, 3:4]; negm2 = stt8[:, 4:5]; m2 = stt8[:, 5:6]
        s12 = stt8[:, 6:7]; r12 = stt8[:, 7:8]
        vop(p, "reduce_max", negm, blk, AX.X, negate=True)
        sop(p, "activation", blk, blk, AF.Exp, bias=negm, scale=1.0,
            accum_out=Z)
        vop(p, "reciprocal", rZ, Z)
        vop(p, "tensor_scalar_mul", blk, blk, rZ)
        vop(p, "reduce_max", m1, blk, AX.X)
        vop(p, "tensor_scalar", eqt[:], blk, m1, None, ALU.is_equal)
        vop(p, "scalar_tensor_tensor", mskt[:], eqt[:], -1e30, blk,
            ALU.mult, ALU.add)
        vop(p, "reduce_max", negm2, mskt[:], AX.X, negate=True)
        vop(p, "tensor_scalar_mul", m2, negm2, -1.0)
        vop(p, "tensor_sub", s12, m1, negm2)
        vop(p, "tensor_scalar_add", s12, s12, 1e-8)
        vop(p, "reciprocal", r12, s12)
        cwblk = cw_tm[:, kt * 4:(kt + 1) * 4]
        vop(p, "tensor_scalar", cwblk, eqt[:], m1, None, ALU.mult)
        vop(p, "tensor_scalar", eqt[:], mskt[:], m2, None, ALU.is_equal)
        vop(p, "tensor_scalar", eqt[:], eqt[:], m2, None, ALU.mult)
        vop(p, "tensor_add", cwblk, cwblk, eqt[:])
        vop(p, "tensor_scalar_mul", cwblk, cwblk, r12)
    for kt in range(NQT):
        trgroup(p, [(PS_B[0:4, 0:128], cw_tm[:, kt * 4:(kt + 1) * 4])], identf[:])
        vop(p, "tensor_copy", cw_fm[0:4, kt * 128:(kt + 1) * 128], PS_B[0:4, 0:128])

    def bcast_cw(e):
        p.dma("sync", cwrow[:], cw_fm[e:e + 1, :])
        for g in range(NTG):
            mm(p, PS_B[:, 0:512], onesF[0:1, 0:128],
               cwrow[0:1, g * 512:(g + 1) * 512], True, True)
            vop(p, "tensor_copy", cwb[:, g * 512:(g + 1) * 512], PS_B[:, 0:512])

    def up_mm(wname, oc, src_list, nkc, ev):
        wbuf = wt[oc % 2]
        p.dma("gpsimd", wbuf[:, 0:nkc * 128], tn[wname][oc][:, :])
        for h in range(2):
            hsl = slice(h * 1024, (h + 1) * 1024)
            mm_group(p, [(PS_A[:, h * 1024 + g * 512:h * 1024 + (g + 1) * 512],
                          wbuf[:, kc * 128:(kc + 1) * 128],
                          src_list[kc][:, h * 1024 + g * 512:h * 1024 + (g + 1) * 512],
                          kc == 0, kc == nkc - 1)
                         for kc in range(nkc) for g in range(2)])
            ev(hsl)

    def down_acc(wname, src_list, nkc, first_e):
        for oc in range(2, NCH):
            wbuf = wt[oc % 2]
            p.dma("gpsimd", wbuf[:, 0:nkc * 128], tn[wname][oc][:, :])
            for g in range(NTG):
                gsl = slice(g * 512, (g + 1) * 512)
                par = (oc * NTG + g) % 2
                pps = PS_O if par == 0 else PS_B
                yb = y_t[(oc * NTG + g) % 4]
                mm_group(p, [(pps[:, 0:512], wbuf[:, kc * 128:(kc + 1) * 128],
                              src_list[kc][:, gsl], kc == 0, kc == nkc - 1)
                             for kc in range(nkc)])
                vop(p, "tensor_mul", yb, pps[:, 0:512], cwb[:, gsl])
                dst = tn["facc"][seq, oc * 128:(oc + 1) * 128, gsl]
                if first_e:
                    p.dma("gpsimd", dst, yb)
                else:
                    p.do("gpsimd", lambda en, d=dst, s=yb: en.dma_start(
                        out=d, in_=s, accum_op=ALU.add),
                        reads=[s if False else yb, dst], writes=[dst], dma=True)

    def gelu_to(dst):
        return lambda hsl: sop(p, "activation", dst[:, hsl], PS_A[:, hsl],
                               AF.Gelu, bias=0.0, scale=1.0)

    def silu_to(dst):
        return lambda hsl: sop(p, "activation", dst[:, hsl], PS_A[:, hsl],
                               AF.Silu, bias=0.0, scale=1.0)

    def mul_into(dst):
        return lambda hsl: vop(p, "tensor_mul", dst[:, hsl], PS_A[:, hsl],
                               dst[:, hsl])

    bcast_cw(0)
    for oc in range(16):
        up_mm("e0up", oc, xfb, NCH, gelu_to(A[oc]))
    down_acc("e0dn", A, 16, True)
    bcast_cw(1)
    for oc in range(16):
        up_mm("e1w1", oc, xfb, NCH, silu_to(A[oc]))
    for oc in range(16):
        up_mm("e1w2", oc, xfb, NCH, mul_into(A[oc]))
    down_acc("e1dn", A, 16, False)
    bcast_cw(2)
    for oc in range(16):
        up_mm("e2l1", oc, xfb, NCH, gelu_to(A[oc]))
    for oc in range(16):
        up_mm("e2l2", oc, A, 16, gelu_to(B[oc]))
    for oc in range(16):
        up_mm("e2l3", oc, B, 16, gelu_to(A[oc]))
    down_acc("e2l4", A, 16, False)
    bcast_cw(3)
    for oc in range(NCH):
        up_mm("e3dn", oc, xfb, NCH, gelu_to(A[oc]))
    for oc in range(16):
        up_mm("e3up", oc, A, NCH, gelu_to(B[oc]))
    down_acc("e3out", B, 16, False)
    for c in range(NCH):
        csl = slice(c * 128, (c + 1) * 128)
        p.dma("sync", xlc[:], tn["xloc"][seq, csl, :])
        if c >= 2:
            p.dma("sync", gc_f[:], tn["facc"][seq, csl, :])
            vop(p, "tensor_add", gc_f[:], gc_f[:], xlc[:])
        else:
            vop(p, "tensor_copy", gc_f[:], xlc[:])
        sop(p, "activation", gc_f[:], gc_f[:], AF.Silu, bias=0.0, scale=1.0)
        p.dma("sync", fb_s[:], tn["sbr"][seq, csl, :])
        vop(p, "tensor_mul", gc_f[:], gc_f[:], fb_s[:])
        p.dma("sync", fb_s[:], tn["xcur"][seq, csl, :])
        vop(p, "tensor_add", gc_f[:], gc_f[:], fb_s[:])
        p.dma("sync", tn["yout"][seq, csl, :], gc_f[:])


EMITTERS["ffn1b"] = emit_ffn1b



SCRATCH_F32 = ("xn32", "xcur", "sbr", "xloc", "facc")
SCRATCH_BF16 = ("znbf",)


def declare_tensors(nc, segs, produce, consume):
    tn = {}
    tn["xin"] = nc.dram_tensor("xin", [NSEQ, D, T], F32, kind="ExternalInput").ap()
    tn["yout"] = nc.dram_tensor("yout", [NSEQ, D, T], F32, kind="ExternalOutput").ap()
    wshapes = {
        "w_qkv0": ([24, 128, 1024], F32), "b_qkv0": ([128, 24], F32),
        "w_proj0": ([8, 128, 1024], F32), "b_proj0": ([128, 8], F32),
        "ln0w": ([128, 8], F32), "ln0b": ([128, 8], F32),
        "rw0": ([1, 128, 64], F32),
        "w1_0": ([8, 42, 128, 1024], F32), "b1_0": ([128, 336], F32),
        "w2_0": ([8, 8, 128, 2688], F32), "b2_0": ([128, 64], F32),
        "w_attn1": ([24, 128, 1024], F32), "b_attn1": ([128, 24], F32),
        "w_proj1": ([8, 128, 1024], F32), "b_proj1": ([128, 8], F32),
        "wkc1T": ([32, 64], F32), "wke1": ([32, 64], F32),
        "wvc1T": ([32, 64], F32), "wve1": ([32, 64], F32),
        "n1w": ([128, 8], F32), "n2w": ([128, 8], F32),
        "wq1": ([8, 128, 1024], F32), "wk1": ([8, 128, 1024], F32),
        "wv1": ([8, 128, 1024], F32), "wo1": ([8, 128, 1024], F32),
        "qn1": ([128, 1], F32), "kn1": ([128, 1], F32),
        "gw1": ([1, 128, 32], F32),
        "e0up": ([16, 128, 1024], F32), "e0dn": ([8, 128, 2048], F32),
        "e1w1": ([16, 128, 1024], F32), "e1w2": ([16, 128, 1024], F32),
        "e1dn": ([8, 128, 2048], F32),
        "e2l1": ([16, 128, 1024], F32), "e2l2": ([16, 128, 2048], F32),
        "e2l3": ([16, 128, 2048], F32), "e2l4": ([8, 128, 2048], F32),
        "e3dn": ([8, 128, 1024], F32), "e3up": ([16, 128, 1024], F32),
        "e3out": ([8, 128, 2048], F32),
    }
    need = weights_needed(segs)
    for name in need:
        shape, dt = wshapes[name]
        tn[name] = nc.dram_tensor(name, shape, dt, kind="ExternalInput").ap()
    for name in SCRATCH_F32:
        if name in produce and name in consume:
            kind = "Internal"
        elif name in produce:
            kind = "ExternalOutput"
        elif name in consume:
            kind = "ExternalInput"
        else:
            continue
        tn[name] = nc.dram_tensor(name, [NSEQ, D, T], F32, kind=kind).ap()
    for name in SCRATCH_BF16:
        if name in produce and name in consume:
            kind = "Internal"
        elif name in produce:
            kind = "ExternalOutput"
        elif name in consume:
            kind = "ExternalInput"
        else:
            continue
        tn[name] = nc.dram_tensor(name, [NSEQ, D, T], BF16, kind=kind).ap()
    return tn


def weights_needed(segs):
    m = {
        "ln0": ["ln0w", "ln0b"],
        "ln1": ["n1w"],
        "attn0": ["w_qkv0", "b_qkv0", "w_proj0", "b_proj0"],
        "moe0": ["rw0", "w1_0", "b1_0", "w2_0", "b2_0"],
        "attn1": ["w_attn1", "b_attn1", "w_proj1", "b_proj1",
                  "wkc1T", "wke1", "wvc1T", "wve1"],
        "ffn1a": ["wq1", "wk1", "wv1", "wo1", "qn1", "kn1"],
        "ffn1b": ["n2w", "gw1", "e0up", "e0dn", "e1w1", "e1w2", "e1dn",
                  "e2l1", "e2l2", "e2l3", "e2l4", "e3dn", "e3up", "e3out"],
    }
    out = []
    for s in segs:
        for wn in m.get(s, []):
            if wn not in out:
                out.append(wn)
    return out


SEG_IO = {
    "ln0": ((), ("xn32", "znbf")),
    "attn0": (("xn32", "znbf"), ("sbr",)),
    "moe0": (("xn32", "sbr"), ("xcur", "facc")),
    "ln1": (("xcur",), ("xn32", "znbf")),
    "attn1": (("xn32",), ("sbr",)),
    "ffn1a": (("xn32", "znbf"), ("xloc",)),
    "ffn1b": (("xloc", "xcur", "sbr"), ("facc",)),
}


SMALLS = {
    "ln0w": [128, 8], "ln0b": [128, 8], "n1w": [128, 8], "n2w": [128, 8],
    "b_qkv0": [128, 24], "b_attn1": [128, 24],
    "b_proj0": [128, 8], "b_proj1": [128, 8],
    "qn1": [128, 1], "kn1": [128, 1],
    "b1_0": [128, 8 * 42], "b2_0": [128, 8 * 8],
}


def alloc_sbuf(nc, stack, segs):
    sb = {}
    def S(name, shape, dt):
        sb[name] = stack.enter_context(nc.sbuf_tensor(name, shape, dt))
        return sb[name]
    for name in weights_needed(segs):
        if name in SMALLS:
            S(name + "_s", SMALLS[name], F32)
    S("ones_col", [128, 1], F32)
    S("ones_row", [1, 128], F32)
    S("onesb_col", [128, 1], BF16)
    S("onesb_row", [1, 128], BF16)
    S("ident", [128, 128], BF16)
    S("tri", [128, 128], F32)
    S("identf", [128, 128], F32)
    S("onesF", [128, 128], F32)
    return sb


def init_const(p, tn, segs):
    sb = tn["sb"]
    for name in weights_needed(segs):
        if name in SMALLS:
            mir = sb[name + "_s"]
            src = tn[name]
            p.dma("sync", mir[:], src[:])
    gop(p, "memset", sb["ones_col"][:], 1.0)
    gop(p, "memset", sb["ones_row"][:], 1.0)
    gop(p, "memset", sb["onesb_col"][:], 1.0)
    gop(p, "memset", sb["onesb_row"][:], 1.0)
    gop(p, "memset", sb["ident"][:], 1.0)
    gop(p, "affine_select", sb["ident"][:], sb["ident"][:], [[-1, 128]],
        ALU.is_equal, 0.0, base=0, channel_multiplier=1)
    gop(p, "memset", sb["tri"][:], 0.0)
    gop(p, "affine_select", sb["tri"][:], sb["tri"][:], [[-1, 128]],
        ALU.is_ge, -1e30, base=0, channel_multiplier=1)
    gop(p, "memset", sb["identf"][:], 1.0)
    gop(p, "affine_select", sb["identf"][:], sb["identf"][:], [[-1, 128]],
        ALU.is_equal, 0.0, base=0, channel_multiplier=1)
    gop(p, "memset", sb["onesF"][:], 1.0)


def build_nc(segs):
    consume, produce = set(), set()
    for s in segs:
        c, pr = SEG_IO[s]
        consume.update(x for x in c if x not in produce)
        produce.update(pr)
    nc = bass.Bass("TRN2", target_bir_lowering=False)
    stack = ExitStack()
    with stack:
        tn = declare_tensors(nc, segs, produce, consume)
        sb = alloc_sbuf(nc, stack, segs)
        tn["sb"] = sb
        tn["PS_A"] = stack.enter_context(nc.psum_tensor("PS_A", [128, 2048], F32))
        tn["PS_B"] = stack.enter_context(nc.psum_tensor("PS_B", [128, 512], F32))
        tn["PS_T"] = stack.enter_context(nc.psum_tensor("PS_T", [128, 1024], BF16))
        tn["PS_T2"] = stack.enter_context(nc.psum_tensor("PS_T2", [128, 1024], BF16))
        tn["PS_O"] = stack.enter_context(nc.psum_tensor("PS_O", [128, 512], F32))
        tn["nc"] = nc
        with nc.Block() as block:
            p = Prog(nc, stack, serial=SERIAL_MODE)
            init_const(p, tn, segs)
            for seq in range(NSEQ):
                for s in segs:
                    with ExitStack() as segstack:
                        tn["segstack"] = segstack
                        EMITTERS[s](p, tn, seq)
                    p.fence()
            p.finalize(block)
    return nc


def _emit_ln0(p, tn, seq):
    emit_ln(p, tn, seq, 0)


def _emit_ln1(p, tn, seq):
    emit_ln(p, tn, seq, 1)


EMITTERS["ln0"] = _emit_ln0
EMITTERS["ln1"] = _emit_ln1



ALL_SEGS = ["ln0", "attn0", "moe0", "ln1", "attn1", "ffn1a", "ffn1b"]


def kernel(**inputs):
    w = prep_weights(inputs)
    nc = build_nc(ALL_SEGS)
    need = weights_needed(ALL_SEGS)
    in_maps = []
    for core in range(NCORE):
        im = {"xin": prep_xin(inputs, core)}
        for n in need:
            im[n] = w[n]
        in_maps.append(im)
    res = run_bass_kernel_spmd(nc, in_maps, core_ids=list(range(NCORE)))
    out = np.empty((16, 2048, 1024), np.float32)
    for core in range(NCORE):
        y = np.asarray(res.results[core]["yout"])
        out[2 * core:2 * core + 2] = y.transpose(0, 2, 1)
    return out


# version 04
# speedup vs baseline: 1.4327x; 1.0329x over previous
import numpy as np
from contextlib import ExitStack

import concourse.bass as bass
import concourse.mybir as mybir
from concourse.bass_utils import run_bass_kernel_spmd

F32 = mybir.dt.float32
BF16 = mybir.dt.bfloat16
AF = mybir.ActivationFunctionType
ALU = mybir.AluOpType
AX = mybir.AxisListType

NCORE = 8
SERIAL_MODE = False
DMA_FULL_SERIAL = False
NSEQ = 2
D = 1024
T = 2048
NCH = D // 128
NH = 16
DH = 64
NTG = T // 512
NQT = T // 128


def tile_w(W, oc=128):
    IN, OUT = W.shape
    return np.ascontiguousarray(
        W.reshape(IN // 128, 128, OUT // oc, oc).transpose(2, 1, 0, 3)
        .reshape(OUT // oc, 128, (IN // 128) * oc))


def colvec(v):
    return np.ascontiguousarray(np.asarray(v).reshape(-1, 128).T)


def prep_weights(inp):
    w = {}
    w["w_qkv0"] = tile_w(inp["L0_attn_W_qkv"])
    w["b_qkv0"] = colvec(inp["L0_attn_b_qkv"])
    w["w_proj0"] = tile_w(inp["L0_attn_W_proj"])
    w["b_proj0"] = colvec(inp["L0_attn_b_proj"])
    w["ln0w"] = colvec(inp["L0_attn_ln_w"])
    w["ln0b"] = colvec(inp["L0_attn_ln_b"])
    w["rw0"] = tile_w(inp["L0_ffn_router_w"], oc=8)
    w["w1_0"] = np.stack([tile_w(inp["L0_ffn_W1"][e]) for e in range(8)])
    w["b1_0"] = np.concatenate([colvec(inp["L0_ffn_b1"][e]) for e in range(8)], axis=1)
    w["w2_0"] = np.stack([tile_w(inp["L0_ffn_W2"][e]) for e in range(8)])
    w["b2_0"] = np.concatenate([colvec(inp["L0_ffn_b2"][e]) for e in range(8)], axis=1)
    w["w_attn1"] = tile_w(inp["L1_attn_c_attn_w"])
    w["b_attn1"] = colvec(inp["L1_attn_c_attn_b"])
    w["w_proj1"] = tile_w(inp["L1_attn_c_proj_w"])
    w["b_proj1"] = colvec(inp["L1_attn_c_proj_b"])
    w["wkc1T"] = np.ascontiguousarray(inp["L1_attn_wk_c"].T)
    w["wke1"] = np.ascontiguousarray(inp["L1_attn_wk_e"])
    w["wvc1T"] = np.ascontiguousarray(inp["L1_attn_wv_c"].T)
    w["wve1"] = np.ascontiguousarray(inp["L1_attn_wv_e"])
    w["n1w"] = colvec(inp["L1_ffn_n1_w"])
    w["n2w"] = colvec(inp["L1_ffn_n2_w"])
    w["wq1"] = tile_w(inp["L1_ffn_wq"].T)
    w["wk1"] = tile_w(inp["L1_ffn_wk"].T)
    w["wv1"] = tile_w(inp["L1_ffn_wv"].T)
    w["wo1"] = tile_w(inp["L1_ffn_wo"].T)
    w["qn1"] = np.ascontiguousarray(np.tile(inp["L1_ffn_qn_w"], 2)[:, None])
    w["kn1"] = np.ascontiguousarray(np.tile(inp["L1_ffn_kn_w"], 2)[:, None])
    w["gw1"] = tile_w(inp["L1_ffn_gate_w"].T, oc=4)
    w["e0up"] = tile_w(inp["L1_ffn_e0_up"].T)
    w["e0dn"] = tile_w(inp["L1_ffn_e0_down"].T)
    w["e1w1"] = tile_w(inp["L1_ffn_e1_w1"].T)
    w["e1w2"] = tile_w(inp["L1_ffn_e1_w2"].T)
    w["e1dn"] = tile_w(inp["L1_ffn_e1_down"].T)
    w["e2l1"] = tile_w(inp["L1_ffn_e2_l1"].T)
    w["e2l2"] = tile_w(inp["L1_ffn_e2_l2"].T)
    w["e2l3"] = tile_w(inp["L1_ffn_e2_l3"].T)
    w["e2l4"] = tile_w(inp["L1_ffn_e2_l4"].T)
    w["e3dn"] = tile_w(inp["L1_ffn_e3_down"].T)
    w["e3up"] = tile_w(inp["L1_ffn_e3_up"].T)
    w["e3out"] = tile_w(inp["L1_ffn_e3_out"].T)
    return w


def prep_xin(inp, core):
    z = inp["L0_attn_z"][2 * core:2 * core + 2]
    return np.ascontiguousarray(z.transpose(0, 2, 1))



def _is_ap(x):
    return hasattr(x, "tensor") and hasattr(x, "ap") and hasattr(x, "offset")


def _ap_keys(ap):
    t = ap.tensor
    name = t.name
    kind = type(t).__name__
    dims = [(int(s), int(c)) for s, c in ap.ap]
    off = int(ap.offset)
    if kind == "PSumTensorHandle" or kind == "SBTensorHandle":
        F = int(t.shape[-1])
        pcount = 1
        free = []
        for s, c in dims:
            if s % F == 0 and s != 0 and c > 1:
                pcount = max(pcount, ((c - 1) * s) // F + 1)
            else:
                free.append((s, c))
        p0 = off // F
        c0 = off % F
        cspan = 1 + sum(s * (c - 1) for s, c in free)
        if c0 + cspan > F:
            p0, pcount, c0, cspan = 0, 128, 0, F
        if kind == "PSumTensorHandle":
            bank = 512 if str(t.dtype) == "dt.float32" else 1024
            return {(name, b) for b in range(c0 // bank, (c0 + cspan - 1) // bank + 1)}
        keys = set()
        for pb in range(p0 // 32, (p0 + pcount - 1) // 32 + 1):
            for cb in range(c0 // 512, (c0 + cspan - 1) // 512 + 1):
                keys.add((name, pb, cb))
        return keys
    span = 1 + sum(s * (c - 1) for s, c in dims)
    return {(name, "f", b) for b in range(off // 262144, (off + span - 1) // 262144 + 1)}


class Prog:
    ENG = ("sync", "gpsimd", "scalar", "vector", "tensor")

    def __init__(self, nc, stack, serial=False):
        self.nc = nc
        self.serial = serial
        self.steps = []
        self.writer = {}
        self.readers = {}
        self.sems = {}
        for e in self.ENG:
            self.sems[e] = stack.enter_context(nc.semaphore("sem_" + e))
        self.NSLOT = 8
        for q in ("dsync", "dgps"):
            for s in range(self.NSLOT):
                self.sems[(q, s)] = stack.enter_context(
                    nc.semaphore(f"sem_{q}{s}"))
        self.dma_count = {"sync": 0, "gpsimd": 0}
        self.last_slot_step = {}
        self.last_by_sem = {}
        self.pending_fence = None
        self.fence_done = set()

    def do(self, eng, fn, reads=(), writes=(), dma=False):
        idx = len(self.steps)
        if dma:
            qname = {"sync": "dsync", "gpsimd": "dgps"}[eng]
            slot = self.dma_count[eng] % self.NSLOT
            self.dma_count[eng] += 1
            sem_id = (qname, slot)
        else:
            sem_id = eng
        edges = set()
        if dma:
            key = qname if DMA_FULL_SERIAL else sem_id
            prev = self.last_slot_step.get(key)
            if prev is not None:
                edges.add(prev)
            self.last_slot_step[key] = idx
        rkeys = set()
        for ap in reads:
            rkeys |= _ap_keys(ap)
        wkeys = set()
        for ap in writes:
            wkeys |= _ap_keys(ap)
        if self.serial and idx > 0:
            edges.add(idx - 1)
        for k in rkeys:
            w = self.writer.get(k)
            if w is not None:
                edges.add(w)
        for k in wkeys:
            w = self.writer.get(k)
            if w is not None:
                edges.add(w)
            for r in self.readers.get(k, ()):
                edges.add(r)
        pruned = edges
        if self.pending_fence is not None and eng not in self.fence_done:
            for pi in self.pending_fence.values():
                pruned.add(pi)
            self.fence_done.add(eng)
            if len(self.fence_done) == len(self.ENG):
                self.pending_fence = None
        for k in rkeys:
            self.readers.setdefault(k, []).append(idx)
        for k in wkeys:
            self.writer[k] = idx
            self.readers[k] = []
        self.last_by_sem[sem_id] = idx
        self.steps.append((eng, fn, sem_id, 16 if dma else 1, pruned))

    def fence(self):
        self.pending_fence = dict(self.last_by_sem)
        self.fence_done = set()

    def dma(self, q, out, in_):
        self.do(q, lambda e, o=out, i=in_: e.dma_start(out=o, in_=i),
                reads=[in_], writes=[out], dma=True)

    def finalize(self, block):
        steps = self.steps
        n = len(steps)
        need = [False] * n
        for eng, fn, sem_id, amt, edges in steps:
            for e in edges:
                need[e] = True
        for i, (eng, fn, sem_id, amt, edges) in enumerate(steps):
            if amt == 16:
                need[i] = True
        val = [0] * n
        cum = {}
        for i, (eng, fn, sem_id, amt, edges) in enumerate(steps):
            if need[i]:
                cum[sem_id] = cum.get(sem_id, 0) + amt
            val[i] = cum.get(sem_id, 0)
        totals = dict(cum)
        per_eng = {e: [] for e in self.ENG}
        for i, (eng, fn, sem_id, amt, edges) in enumerate(steps):
            per_eng[eng].append(i)
        sems = self.sems

        def make_body(eng, idxs):
            def body(engh):
                known = {}
                for i in idxs:
                    _, fn, sem_id, amt, edges = steps[i]
                    waits = {}
                    for e in edges:
                        psem = steps[e][2]
                        v = val[e]
                        if v > waits.get(psem, 0):
                            waits[psem] = v
                    for psem, v in sorted(waits.items(), key=lambda kv: str(kv[0])):
                        if known.get(psem, 0) >= v:
                            continue
                        engh.wait_ge(sems[psem], v)
                        known[psem] = v
                    ins = fn(engh)
                    if need[i]:
                        if ins is None:
                            ins = engh.nop()
                        ins.then_inc(sems[sem_id], amt)
                return None
            return body

        for e in self.ENG:
            if per_eng[e]:
                block.__getattribute__(e)(make_body(e, per_eng[e]))

        def tail(engh):
            for q, tot in totals.items():
                if isinstance(q, tuple) and tot:
                    engh.wait_ge(sems[q], tot)
            for e in ("tensor", "vector", "scalar", "sync"):
                if totals.get(e):
                    engh.wait_ge(sems[e], totals[e])
        block.gpsimd(tail)



def mm(p, out_ps, lhsT, rhs, start, stop):
    p.do("tensor", lambda t: t.matmul(out_ps, lhsT, rhs, start=start, stop=stop),
         reads=[lhsT, rhs], writes=[out_ps])


def mm_group(p, group):
    def fn(t):
        last = None
        for o, l, r, st, sp in group:
            last = t.matmul(o, l, r, start=st, stop=sp)
        return last
    reads = [g[1] for g in group] + [g[2] for g in group]
    writes = [g[0] for g in group]
    p.do("tensor", fn, reads=reads, writes=writes)


def trgroup(p, items, ident):
    def fn(t):
        last = None
        for o, i in items:
            last = t.transpose(o, i, ident)
        return last
    p.do("tensor", fn, reads=[i for _, i in items] + [ident],
         writes=[o for o, _ in items])


def _roles(args, kw):
    writes = [a for a in args[:1] if _is_ap(a)]
    reads = [a for a in args[1:] if _is_ap(a)]
    for k, v in kw.items():
        if _is_ap(v):
            (writes if k == "accum_out" else reads).append(v)
    return reads, writes


def vop(p, name, *args, **kw):
    reads, writes = _roles(args, kw)
    p.do("vector", lambda e: getattr(e, name)(*args, **kw),
         reads=reads, writes=writes)


def sop(p, name, *args, **kw):
    reads, writes = _roles(args, kw)
    p.do("scalar", lambda e: getattr(e, name)(*args, **kw),
         reads=reads, writes=writes)


def gop(p, name, *args, **kw):
    reads, writes = _roles(args, kw)
    p.do("gpsimd", lambda e: getattr(e, name)(*args, **kw),
         reads=reads, writes=writes)


EMITTERS = {}
_UNIQ = [0]


def uname(n):
    _UNIQ[0] += 1
    return f"{n}_{_UNIQ[0]}"


def emit_ln(p, tn, seq, layer):
    x = tn["xin"] if layer == 0 else tn["xcur"]
    sb = tn["sb"]
    st = tn["segstack"]
    nc = tn["nc"]
    xf = [st.enter_context(nc.sbuf_tensor(uname(f"lnx{c}"), [128, T], F32))
          for c in range(NCH)]
    xsq = [st.enter_context(nc.sbuf_tensor(uname(f"lnsq{i}"), [128, T], F32))
           for i in range(2)]
    t1 = [st.enter_context(nc.sbuf_tensor(uname(f"lnt1{i}"), [128, T], F32))
          for i in range(2)]
    znt = [st.enter_context(nc.sbuf_tensor(uname(f"lnzn{i}"), [128, T], BF16))
           for i in range(2)]
    R = st.enter_context(nc.sbuf_tensor(uname("lnrows"), [1, 4 * T], F32))
    Ab = st.enter_context(nc.sbuf_tensor(uname("lnAb"), [128, T], F32))
    Bb = st.enter_context(nc.sbuf_tensor(uname("lnBb"), [128, T], F32))
    Cb = st.enter_context(nc.sbuf_tensor(uname("lnCb"), [128, T], F32))
    ones_col = sb["ones_col"]
    ones_row = sb["ones_row"]
    PS_A = tn["PS_A"]
    PS_B = tn["PS_B"]
    R1 = R[0:1, 0:2048]; R2 = R[0:1, 2048:4096]
    R3 = R[0:1, 4096:6144]; R4 = R[0:1, 6144:8192]

    for c in range(NCH):
        p.dma("sync", xf[c][:], x[seq, c * 128:(c + 1) * 128, :])
    for c in range(NCH):
        mm_group(p, [(PS_A[0:1, g * 512:(g + 1) * 512], ones_col[:],
                      xf[c][:, g * 512:(g + 1) * 512], c == 0, c == NCH - 1)
                     for g in range(NTG)])
    vop(p, "tensor_copy", R1, PS_A[0:1, :])
    for c in range(NCH):
        sop(p, "square", xsq[c % 2][:], xf[c][:])
        mm_group(p, [(PS_A[0:1, g * 512:(g + 1) * 512], ones_col[:],
                      xsq[c % 2][:, g * 512:(g + 1) * 512], c == 0, c == NCH - 1)
                     for g in range(NTG)])
    vop(p, "tensor_copy", R2, PS_A[0:1, :])
    vop(p, "tensor_scalar_mul", R1, R1, 1.0 / D)
    vop(p, "tensor_scalar_mul", R2, R2, 1.0 / D)
    vop(p, "tensor_mul", R3, R1, R1)
    vop(p, "tensor_sub", R2, R2, R3)
    vop(p, "tensor_scalar_add", R3, R2, 1e-5)
    sop(p, "sqrt", R3, R3)
    vop(p, "reciprocal", R3, R3)
    vop(p, "tensor_mul", R4, R1, R3)
    def bcast(row, dst):
        for g in range(NTG):
            mm(p, PS_B[:, 0:512], ones_row[:], row[0:1, g * 512:(g + 1) * 512],
               True, True)
            vop(p, "tensor_copy", dst[:, g * 512:(g + 1) * 512], PS_B[:, 0:512])
    bcast(R3, Ab)
    bcast(R4, Bb)
    eps_i = 1e-5 if layer == 0 else 1.1920929e-07
    vop(p, "tensor_mul", R1, R3, R3)
    vop(p, "tensor_mul", R1, R1, R2)
    vop(p, "tensor_scalar_add", R1, R1, eps_i)
    sop(p, "sqrt", R1, R1)
    vop(p, "reciprocal", R1, R1)
    bcast(R1, Cb)
    wcol = sb["ln0w_s"] if layer == 0 else sb["n1w_s"]
    bcol = sb["ln0b_s"] if layer == 0 else None
    for c in range(NCH):
        tb, zb = t1[c % 2], znt[c % 2]
        vop(p, "tensor_mul", tb[:], xf[c][:], Ab[:])
        vop(p, "tensor_sub", tb[:], tb[:], Bb[:])
        p.dma("sync", tn["xn32"][seq, c * 128:(c + 1) * 128, :], tb[:])
        vop(p, "tensor_mul", tb[:], tb[:], Cb[:])
        if bcol is not None:
            vop(p, "tensor_scalar", zb[:], tb[:], wcol[:, c:c + 1],
                bcol[:, c:c + 1], ALU.mult, ALU.add)
        else:
            vop(p, "tensor_scalar_mul", zb[:], tb[:], wcol[:, c:c + 1])
        p.dma("sync", tn["znbf"][seq, c * 128:(c + 1) * 128, :], zb[:])



def emit_attn(p, tn, seq, cfg):
    nc = tn["nc"]
    st = tn["segstack"]
    sb = tn["sb"]
    PS_A, PS_B, PS_O = tn["PS_A"], tn["PS_B"], tn["PS_O"]
    PS_T = [tn["PS_T"], tn["PS_T2"]]
    ident = sb["ident"]
    ones_col = sb["ones_col"]
    ones_row = sb["ones_row"]
    causal = cfg["causal"]
    scale = 0.125

    def S_(name, shape, dt):
        return st.enter_context(nc.sbuf_tensor(uname(name), shape, dt))

    zn = [S_(f"at_zn{c}", [128, T], BF16) for c in range(NCH)]
    Ob = [S_(f"at_o{c}", [128, T], BF16) for c in range(NCH)]
    q2 = S_("at_q2", [128, T], BF16)
    k2 = S_("at_k2", [128, T], BF16)
    v2 = S_("at_v2", [128, T], BF16)
    vtm = S_("at_vtm", [128, T], BF16)
    Pt = [S_("at_p0", [128, T], BF16), S_("at_p1", [128, T], BF16)]
    ptq = [S_("at_pt0", [128, T], BF16), S_("at_pt1", [128, T], BF16)]
    wt = [S_("at_wt0", [128, 3 * 1024], BF16), S_("at_wt1", [128, 3 * 1024], BF16)]
    stt4 = S_("at_st", [128, 8], F32)
    bsq = S_("at_bsq", [128, T], BF16)
    nbf = S_("at_nbf", [128, T], F32)
    nbias = S_("at_nb", [128, 2 * NQT], F32)
    brow = S_("at_brow", [1, T + 8], F32)
    qk_rms = cfg.get("qk_rms")
    if qk_rms:
        rqb = S_("at_rqb", [128, T], F32)
        rkb = S_("at_rkb", [128, T], F32)
        R = S_("at_rows", [1, T], F32)
    if cfg.get("latent"):
        q2p = S_("at_q2p", [128, T], BF16)
        v2e = S_("at_v2e", [128, T], BF16)
        lw = S_("at_lw", [32, 4 * 64], BF16)
        W64 = S_("at_w64", [128, 128], BF16)
        p.dma("gpsimd", lw[:, 0:64], tn["wke1"][:])
        p.dma("gpsimd", lw[:, 64:128], tn["wkc1T"][:])
        p.dma("gpsimd", lw[:, 128:192], tn["wvc1T"][:])
        p.dma("gpsimd", lw[:, 192:256], tn["wve1"][:])
        for b64 in (0, 64):
            mm(p, PS_O[b64:b64 + 64, 0:64], lw[:, 0:64], lw[:, 64:128], True, True)
            vop(p, "tensor_copy", W64[b64:b64 + 64, 0:64], PS_O[b64:b64 + 64, 0:64])
            mm(p, PS_O[b64:b64 + 64, 0:64], lw[:, 128:192], lw[:, 192:256], True, True)
            vop(p, "tensor_copy", W64[b64:b64 + 64, 64:128], PS_O[b64:b64 + 64, 0:64])

    gop(p, "memset", nbf[:], 0.0)
    if cfg["src"] == "xn32cast":
        for c in range(NCH):
            p.dma("gpsimd", zn[c][:], tn["xn32"][seq, c * 128:(c + 1) * 128, :])
    else:
        src = tn[cfg["src"]]
        for c in range(NCH):
            p.dma("sync", zn[c][:], src[seq, c * 128:(c + 1) * 128, :])

    bias = sb[cfg["qkv_bias"] + "_s"] if cfg.get("qkv_bias") else None

    for hp in range(8):
        wbuf = wt[hp % 2]
        if "fused_qkv" in cfg:
            wten = tn[cfg["fused_qkv"]]
            ocs = [hp, 8 + hp, 16 + hp]
            srcs = [wten[o] for o in ocs]
        else:
            srcs = [tn[n][hp] for n in cfg["sep_qkv"]]
            ocs = [hp, 8 + hp, 16 + hp]
        for i in range(3):
            p.dma("gpsimd", wbuf[:, i * 1024:(i + 1) * 1024], srcs[i][:, :])
        for i, dst in enumerate((q2, k2, v2)):
            if qk_rms and i < 2:
                mm_group(p, [(PS_A[:, g * 512:(g + 1) * 512],
                              wbuf[:, i * 1024 + c * 128:i * 1024 + (c + 1) * 128],
                              zn[c][:, g * 512:(g + 1) * 512], c == 0, c == NCH - 1)
                             for c in range(NCH) for g in range(NTG)])
                qn_s = sb[qk_rms[i] + "_s"]
                eps = qk_rms[2]
                rb = rqb if i == 0 else rkb
                sop(p, "square", bsq[:], PS_A[:, :])
                for h2 in range(2):
                    b64 = h2 * 64
                    Rrow = R[0:1, 0:T]
                    for g in range(NTG):
                        mm(p, PS_B[0:1, 0:512], sb["onesb_col"][b64:b64 + 64, :],
                           bsq[b64:b64 + 64, g * 512:(g + 1) * 512], True, True)
                        vop(p, "tensor_copy", Rrow[0:1, g * 512:(g + 1) * 512],
                            PS_B[0:1, 0:512])
                    vop(p, "tensor_scalar_mul", Rrow, Rrow, 1.0 / 64)
                    vop(p, "tensor_scalar_add", Rrow, Rrow, eps)
                    sop(p, "sqrt", Rrow, Rrow)
                    vop(p, "reciprocal", Rrow, Rrow)
                    for g in range(NTG):
                        mm(p, PS_B[:, 0:512], ones_row[:],
                           Rrow[0:1, g * 512:(g + 1) * 512], True, True)
                        vop(p, "tensor_copy",
                            rb[b64:b64 + 64, g * 512:(g + 1) * 512],
                            PS_B[b64:b64 + 64, 0:512])
                vop(p, "scalar_tensor_tensor", dst[:], PS_A[:, :],
                    qn_s[:, 0:1], rb[:], ALU.mult, ALU.mult)
            else:
                for h in range(2):
                    hsl = slice(h * 1024, (h + 1) * 1024)
                    mm_group(p, [(PS_A[:, h * 1024 + g * 512:h * 1024 + (g + 1) * 512],
                                  wbuf[:, i * 1024 + c * 128:i * 1024 + (c + 1) * 128],
                                  zn[c][:, h * 1024 + g * 512:h * 1024 + (g + 1) * 512],
                                  c == 0, c == NCH - 1)
                                 for c in range(NCH) for g in range(2)])
                    if bias is not None:
                        vop(p, "tensor_scalar_add", dst[:, hsl], PS_A[:, hsl],
                            bias[:, ocs[i]:ocs[i] + 1])
                    else:
                        vop(p, "tensor_copy", dst[:, hsl], PS_A[:, hsl])
        qsrc, vsrc = q2, v2
        if cfg.get("latent"):
            for h2 in range(2):
                b64 = h2 * 64
                for g in range(NTG):
                    gsl = slice(g * 512, (g + 1) * 512)
                    mm(p, PS_B[b64:b64 + 64, 0:512], W64[b64:b64 + 64, 0:64],
                       q2[b64:b64 + 64, gsl], True, True)
                    vop(p, "tensor_copy", q2p[b64:b64 + 64, gsl],
                        PS_B[b64:b64 + 64, 0:512])
                    mm(p, PS_B[b64:b64 + 64, 0:512], W64[b64:b64 + 64, 64:128],
                       v2[b64:b64 + 64, gsl], True, True)
                    vop(p, "tensor_copy", v2e[b64:b64 + 64, gsl],
                        PS_B[b64:b64 + 64, 0:512])
            qsrc, vsrc = q2p, v2e
        for half in range(2):
            trgroup(p, [(PS_T[half][:, i * 128:(i + 1) * 128],
                         vsrc[:, (half * 8 + i) * 128:(half * 8 + i + 1) * 128])
                        for i in range(8)], ident[:])
            vop(p, "tensor_copy", vtm[:, half * 1024:(half + 1) * 1024],
                PS_T[half][:, :])
        onesb = sb["onesb_col"]
        sop(p, "square", bsq[:], qsrc[:, :])
        for h2 in range(2):
            b64 = h2 * 64
            for g in range(NTG):
                mm(p, PS_B[0:1, 0:512], onesb[b64:b64 + 64, :],
                   bsq[b64:b64 + 64, g * 512:(g + 1) * 512], True, True)
                vop(p, "tensor_copy", brow[0:1, g * 512:(g + 1) * 512],
                    PS_B[0:1, 0:512])
            vop(p, "tensor_copy", nbf[0:1, 0:T], brow[0:1, 0:T])
            sop(p, "square", bsq[b64:b64 + 64, :], k2[b64:b64 + 64, :])
            for g in range(NTG):
                mm(p, PS_B[0:1, 0:512], onesb[b64:b64 + 64, :],
                   bsq[b64:b64 + 64, g * 512:(g + 1) * 512], True, True)
                vop(p, "tensor_copy", brow[0:1, g * 512:(g + 1) * 512],
                    PS_B[0:1, 0:512])
            vop(p, "reduce_max", brow[0:1, T:T + 1], brow[0:1, 0:T], AX.X)
            vop(p, "tensor_scalar", nbf[0:1, 0:T], nbf[0:1, 0:T],
                brow[0:1, T:T + 1], -0.0625, ALU.add, ALU.mult)
            for r4 in range(4):
                trgroup(p, [(PS_B[:, i * 128:(i + 1) * 128],
                             nbf[:, (r4 * 4 + i) * 128:(r4 * 4 + i + 1) * 128])
                            for i in range(4)], sb["identf"][:])
                vop(p, "tensor_copy",
                    nbias[:, h2 * NQT + r4 * 4:h2 * NQT + r4 * 4 + 4],
                    PS_B[:, 0:512].rearrange("p (i c) -> p i c", c=128)[:, :, 0:1])
        if hp == 0 and "dbg" in tn:
            p.dma("sync", tn["dbg"][0:128, 0:32], nbias[:, :])
            p.dma("gpsimd", tn["dbg"][0:128, 32:2080], q2[:, :])
            p.dma("gpsimd", tn["dbg"][0:128, 2080:4128], k2[:, :])
        for it in range(2 * NQT):
            qt, h2 = divmod(it, 2)
            if True:
                b64 = h2 * 64
                par = it % 2
                nk = (qt + 1) * 128 if causal else T
                bcol = nbias[:, h2 * NQT + qt:h2 * NQT + qt + 1]
                lsum = stt4[:, par * 4:par * 4 + 1]
                lsum2 = stt4[:, par * 4 + 1:par * 4 + 2]
                linv = stt4[:, par * 4 + 2:par * 4 + 3]
                if nk <= 1024:
                    spans = [(par * 1024, 0, nk)]
                else:
                    spans = [(0, 0, 1024), (1024, 1024, nk - 1024)]
                for si, (base, lo, ln_) in enumerate(spans):
                    mm_group(p, [(PS_A[:, base + g * 512:base + min((g + 1) * 512, ln_)],
                                  qsrc[b64:b64 + 64, qt * 128:(qt + 1) * 128],
                                  k2[b64:b64 + 64, lo + g * 512:lo + min((g + 1) * 512, ln_)],
                                  True, True) for g in range((ln_ + 511) // 512)])
                    if causal and lo + ln_ == nk:
                        vop(p, "tensor_add", PS_A[:, base + ln_ - 128:base + ln_],
                            PS_A[:, base + ln_ - 128:base + ln_], sb["tri"][:])
                    sop(p, "activation", Pt[par][:, lo:lo + ln_],
                        PS_A[:, base:base + ln_], AF.Exp,
                        bias=bcol, scale=scale,
                        accum_out=(lsum if si == 0 else lsum2))
                if len(spans) > 1:
                    vop(p, "tensor_add", lsum, lsum, lsum2)
                vop(p, "reciprocal", linv, lsum)
                vop(p, "tensor_scalar_mul", Pt[par][:, 0:nk], Pt[par][:, 0:nk],
                    linv)
                ktn = nk // 128
                for half in range((ktn + 7) // 8):
                    n8 = min(8, ktn - half * 8)
                    trgroup(p, [(PS_T[half][:, i * 128:(i + 1) * 128],
                                 Pt[par][:, (half * 8 + i) * 128:(half * 8 + i + 1) * 128])
                                for i in range(n8)], ident[:])
                    vop(p, "tensor_copy",
                        ptq[par][:, half * 1024:half * 1024 + n8 * 128],
                        PS_T[half][:, 0:n8 * 128])
                pvps = PS_O if par == 0 else PS_B
                mm_group(p, [(pvps[0:64, 0:128],
                              vtm[:, kt * 128 + b64:kt * 128 + b64 + 64],
                              ptq[par][:, kt * 128:(kt + 1) * 128],
                              kt == 0, kt == ktn - 1) for kt in range(ktn)])
                vop(p, "tensor_copy",
                    Ob[hp][b64:b64 + 64, qt * 128:(qt + 1) * 128],
                    pvps[0:64, 0:128])
                if hp == 0 and h2 == 0 and qt == 0 and "dbg" in tn:
                    p.dma("gpsimd", tn["dbg"][0:128, 4128:6176], Pt[par][:, :])
                    p.dma("sync", tn["dbg"][0:128, 6176:6184], stt4[:, :])
    pbias = sb[cfg["proj_bias"] + "_s"] if cfg.get("proj_bias") else None
    xt = [S_("at_xt0", [128, 512], F32), S_("at_xt1", [128, 512], F32)]
    s_t = [S_("at_s0", [128, 512], F32), S_("at_s1", [128, 512], F32)]
    wp = tn[cfg["proj_w"]]
    out = tn[cfg["out"]]
    for oc in range(NCH):
        if oc in cfg.get("skip_oc", ()):
            for g in range(NTG):
                gsl = slice(g * 512, (g + 1) * 512)
                par = g % 2
                if cfg.get("add_xn"):
                    p.dma("sync", xt[par][:], tn["xn32"][seq, oc * 128:(oc + 1) * 128, gsl])
                    p.dma("sync", out[seq, oc * 128:(oc + 1) * 128, gsl], xt[par][:])
                else:
                    gop(p, "memset", s_t[par][:], 0.0)
                    p.dma("sync", out[seq, oc * 128:(oc + 1) * 128, gsl], s_t[par][:])
            continue
        p.dma("gpsimd", wt[oc % 2][:, 0:1024], wp[oc][:, :])
        for g in range(NTG):
            gsl = slice(g * 512, (g + 1) * 512)
            par = (oc * NTG + g) % 2
            pps = PS_O if par == 0 else PS_B
            mm_group(p, [(pps[:, 0:512], wt[oc % 2][:, c * 128:(c + 1) * 128],
                          Ob[c][:, gsl], c == 0, c == NCH - 1)
                         for c in range(NCH)])
            if cfg.get("add_xn"):
                p.dma("sync", xt[par][:], tn["xn32"][seq, oc * 128:(oc + 1) * 128, gsl])
                if pbias is not None:
                    vop(p, "scalar_tensor_tensor", s_t[par][:], pps[:, 0:512],
                        pbias[:, oc:oc + 1], xt[par][:], ALU.add, ALU.add)
                else:
                    vop(p, "tensor_add", s_t[par][:], pps[:, 0:512], xt[par][:])
            else:
                if pbias is not None:
                    vop(p, "tensor_scalar_add", s_t[par][:], pps[:, 0:512],
                        pbias[:, oc:oc + 1])
                else:
                    vop(p, "tensor_copy", s_t[par][:], pps[:, 0:512])
            p.dma("sync", out[seq, oc * 128:(oc + 1) * 128, gsl], s_t[par][:])


def _emit_attn0(p, tn, seq):
    emit_attn(p, tn, seq, dict(
        src="znbf", fused_qkv="w_qkv0", qkv_bias="b_qkv0", causal=False,
        proj_w="w_proj0", proj_bias="b_proj0", add_xn=True, out="sbr"))


def _emit_attn1(p, tn, seq):
    emit_attn(p, tn, seq, dict(
        src="xn32cast", fused_qkv="w_attn1", qkv_bias="b_attn1", causal=True,
        latent=True, proj_w="w_proj1", proj_bias="b_proj1", add_xn=False,
        out="sbr"))


def _emit_ffn1a(p, tn, seq):
    emit_attn(p, tn, seq, dict(
        src="znbf", sep_qkv=("wq1", "wk1", "wv1"), causal=True,
        qk_rms=("qn1", "kn1", 1.1920929e-07),
        proj_w="wo1", add_xn=True, skip_oc={0, 1}, out="xloc"))


EMITTERS["attn0"] = _emit_attn0
EMITTERS["attn1"] = _emit_attn1
EMITTERS["ffn1a"] = _emit_ffn1a



def emit_moe0(p, tn, seq):
    nc = tn["nc"]
    st = tn["segstack"]
    sb = tn["sb"]
    PS_A, PS_B, PS_T, PS_O = tn["PS_A"], tn["PS_B"], tn["PS_T"], tn["PS_O"]
    ones_row = sb["ones_row"]
    identf = sb["identf"]

    def S_(name, shape, dt):
        return st.enter_context(nc.sbuf_tensor(uname(name), shape, dt))

    xnb = [S_(f"mo_xn{c}", [128, T], BF16) for c in range(NCH)]
    act = [S_(f"mo_act{j}", [128, T], BF16) for j in range(21)]
    sg = S_("mo_sg", [128, T], BF16)
    wt = [S_("mo_wt0", [128, 1024], BF16), S_("mo_wt1", [128, 1024], BF16)]
    wt2 = [S_("mo_wt20", [128, 2688], BF16), S_("mo_wt21", [128, 2688], BF16)]
    rw = S_("mo_rw", [128, 64], F32)
    xc = [S_("mo_xc0", [128, T], F32), S_("mo_xc1", [128, T], F32)]
    route_f = S_("mo_rf", [128, T], F32)
    route_tm = S_("mo_rtm", [128, 128], F32)
    cw_tm = S_("mo_cwtm", [128, 128], F32)
    cw_fm = S_("mo_cwfm", [8, T], F32)
    eqt = S_("mo_eq", [128, 8], F32)
    mskt = S_("mo_msk", [128, 8], F32)
    stt8 = S_("mo_st", [128, 8], F32)
    cwrow = S_("mo_cwrow", [1, T], F32)
    cwb = S_("mo_cwb", [128, T], F32)
    y_t = [S_(f"mo_y{i}", [128, 512], F32) for i in range(2)]
    cmb = S_("mo_cmb", [128, T], F32)
    ft = route_f
    s_t = cwb

    for c in range(NCH):
        p.dma("gpsimd", xnb[c][:], tn["xn32"][seq, c * 128:(c + 1) * 128, :])
    p.dma("sync", rw[:], tn["rw0"][0][:, :])
    for c in range(NCH):
        p.dma("sync", xc[c % 2][:], tn["xn32"][seq, c * 128:(c + 1) * 128, :])
        mm_group(p, [(PS_A[0:8, g * 512:(g + 1) * 512], rw[:, c * 8:(c + 1) * 8],
                      xc[c % 2][:, g * 512:(g + 1) * 512], c == 0, c == NCH - 1)
                     for g in range(NTG)])
    gop(p, "memset", route_f[:], 0.0)
    vop(p, "tensor_copy", route_f[0:8, :], PS_A[0:8, :])
    for kt in range(NQT):
        trgroup(p, [(PS_B[:, 0:128], route_f[:, kt * 128:(kt + 1) * 128])], identf[:])
        vop(p, "tensor_copy", route_tm[:, kt * 8:(kt + 1) * 8], PS_B[:, 0:8])
    for kt in range(NQT):
        blk = route_tm[:, kt * 8:(kt + 1) * 8]
        m1 = stt8[:, 0:1]; negm2 = stt8[:, 1:2]; m2 = stt8[:, 2:3]
        negm1 = stt8[:, 3:4]; w1 = stt8[:, 4:5]; w2 = stt8[:, 5:6]
        vop(p, "reduce_max", m1, blk, AX.X)
        vop(p, "tensor_scalar", eqt[:], blk, m1, None, ALU.is_equal)
        vop(p, "scalar_tensor_tensor", mskt[:], eqt[:], -1e30, blk,
            ALU.mult, ALU.add)
        vop(p, "reduce_max", negm2, mskt[:], AX.X, negate=True)
        vop(p, "tensor_scalar_mul", m2, negm2, -1.0)
        vop(p, "tensor_scalar_mul", negm1, m1, -1.0)
        sop(p, "activation", w1, m1, AF.Sigmoid, bias=negm2, scale=1.0)
        sop(p, "activation", w2, m2, AF.Sigmoid, bias=negm1, scale=1.0)
        vop(p, "tensor_scalar", eqt[:], mskt[:], m2, None, ALU.is_equal)
        cwb_blk = cw_tm[:, kt * 8:(kt + 1) * 8]
        vop(p, "tensor_scalar", mskt[:], blk, m1, None, ALU.is_equal)
        vop(p, "tensor_scalar_mul", mskt[:], mskt[:], w1)
        vop(p, "tensor_scalar", eqt[:], eqt[:], w2, None, ALU.mult)
        vop(p, "tensor_add", cwb_blk, mskt[:], eqt[:])
    for kt in range(NQT):
        trgroup(p, [(PS_B[0:8, 0:128], cw_tm[:, kt * 8:(kt + 1) * 8])], identf[:])
        vop(p, "tensor_copy", cw_fm[0:8, kt * 128:(kt + 1) * 128], PS_B[0:8, 0:128])

    b1s = sb["b1_0_s"]
    b2s = sb["b2_0_s"]
    for e in range(8):
        p.dma("sync", cwrow[:], cw_fm[e:e + 1, :])
        for g in range(NTG):
            mm(p, PS_B[:, 0:512], ones_row[:], cwrow[0:1, g * 512:(g + 1) * 512],
               True, True)
            vop(p, "tensor_copy", cwb[:, g * 512:(g + 1) * 512], PS_B[:, 0:512])
        for j in range(21):
            for oc, kind in ((21 + j, "g"), (j, "a")):
                wbuf = wt[oc % 2]
                p.dma("gpsimd", wbuf[:], tn["w1_0"][e, oc][:, :])
                bcol = b1s[:, e * 42 + oc:e * 42 + oc + 1]
                for h in range(2):
                    hsl = slice(h * 1024, (h + 1) * 1024)
                    mm_group(p, [(PS_A[:, h * 1024 + g * 512:h * 1024 + (g + 1) * 512],
                                  wbuf[:, c * 128:(c + 1) * 128],
                                  xnb[c][:, h * 1024 + g * 512:h * 1024 + (g + 1) * 512],
                                  c == 0, c == NCH - 1)
                                 for c in range(NCH) for g in range(2)])
                    if kind == "g":
                        sop(p, "activation", sg[:, hsl], PS_A[:, hsl], AF.Silu,
                            bias=bcol, scale=1.0)
                    else:
                        vop(p, "scalar_tensor_tensor", act[j][:, hsl],
                            PS_A[:, hsl], bcol, sg[:, hsl], ALU.add, ALU.mult)
        for oc in range(NCH):
            w2b = wt2[oc % 2]
            p.dma("gpsimd", w2b[:], tn["w2_0"][e, oc][:, :])
            for g in range(NTG):
                gsl = slice(g * 512, (g + 1) * 512)
                par = (oc * NTG + g) % 2
                pps = PS_O if par == 0 else PS_B
                yb = y_t[(oc * NTG + g) % 2]
                mm_group(p, [(pps[:, 0:512], w2b[:, kc * 128:(kc + 1) * 128],
                              act[kc][:, gsl], kc == 0, kc == 20)
                             for kc in range(21)])
                vop(p, "scalar_tensor_tensor", yb[:], pps[:, 0:512],
                    b2s[:, e * 8 + oc:e * 8 + oc + 1], cwb[:, gsl],
                    ALU.add, ALU.mult)
                dst = tn["facc"][seq, oc * 128:(oc + 1) * 128, gsl]
                if e == 0:
                    p.dma("gpsimd", dst, yb[:])
                else:
                    p.do("gpsimd", lambda en, d=dst, s=yb: en.dma_start(
                        out=d, in_=s[:], accum_op=ALU.add),
                        reads=[yb[:], dst], writes=[dst], dma=True)
    for c in range(NCH):
        csl = slice(c * 128, (c + 1) * 128)
        fb = ft if c % 2 == 0 else cmb
        sb_ = s_t
        xb = xc[c % 2]
        p.dma("sync", fb[:], tn["facc"][seq, csl, :])
        p.dma("sync", sb_[:], tn["sbr"][seq, csl, :])
        p.dma("sync", xb[:], tn["xin"][seq, csl, :])
        sop(p, "activation", fb[:], fb[:], AF.Silu, bias=0.0, scale=1.0)
        vop(p, "tensor_mul", fb[:], fb[:], sb_[:])
        vop(p, "tensor_add", fb[:], fb[:], xb[:])
        p.dma("sync", tn["xcur"][seq, csl, :], fb[:])


EMITTERS["moe0"] = emit_moe0



def emit_ffn1b(p, tn, seq):
    nc = tn["nc"]
    st = tn["segstack"]
    sb = tn["sb"]
    PS_A, PS_B, PS_T, PS_O = tn["PS_A"], tn["PS_B"], tn["PS_T"], tn["PS_O"]
    identf = sb["identf"]
    onesF = sb["onesF"]
    ones_col = sb["ones_col"]
    EPS = 1.1920929e-07

    def S_(name, shape, dt):
        return st.enter_context(nc.sbuf_tensor(uname(name), shape, dt))

    xfb = [S_(f"fb_xf{c}", [128, T], BF16) for c in range(NCH)]
    A = [S_(f"fb_A{j}", [128, T], BF16) for j in range(16)]
    B = [S_(f"fb_B{j}", [128, T], BF16) for j in range(16)]
    wt = [S_("fb_wt0", [128, 2048], BF16), S_("fb_wt1", [128, 2048], BF16)]
    xlc = S_("fb_xl", [128, T], F32)
    gw2 = S_("fb_gw", [128, 32], F32)
    gc_f = S_("fb_gc", [128, T], F32)
    Ctm = S_("fb_ctm", [128, 16], F32)
    ltm = S_("fb_ltm", [128, 64], F32)
    cw_tm = S_("fb_cwtm", [128, 64], F32)
    cw_fm = S_("fb_cwfm", [8, T], F32)
    eqt = S_("fb_eq", [128, 4], F32)
    mskt = S_("fb_msk", [128, 4], F32)
    stt8 = S_("fb_st", [128, 8], F32)
    cwb = S_("fb_cwb", [128, T], F32)
    Cb = cwb[:, 0:512]
    sq = gc_f
    y_t = [gc_f[:, i * 512:(i + 1) * 512] for i in range(4)]
    fb_s = cwb
    n2s = sb["n2w_s"]
    cwrow = gc_f[0:1, 0:T]

    p.dma("sync", gw2[:], tn["gw1"][0][:, :])
    for c in range(NCH):
        vop(p, "tensor_scalar_mul", gw2[:, c * 4:(c + 1) * 4],
            gw2[:, c * 4:(c + 1) * 4], n2s[:, c:c + 1])
    for c in range(NCH):
        p.dma("sync", xlc[:], tn["xloc"][seq, c * 128:(c + 1) * 128, :])
        sop(p, "square", sq[:], xlc[:])
        mm_group(p, [(PS_A[64:65, g * 512:(g + 1) * 512], ones_col[:],
                      sq[:, g * 512:(g + 1) * 512], c == 0, c == NCH - 1)
                     for g in range(NTG)])
        mm_group(p, [(PS_A[0:4, g * 512:(g + 1) * 512], gw2[:, c * 4:(c + 1) * 4],
                      xlc[:, g * 512:(g + 1) * 512], c == 0, c == NCH - 1)
                     for g in range(NTG)])
    gop(p, "memset", gc_f[:], 0.0)
    Crow = gc_f[64:65, :]
    vop(p, "tensor_copy", Crow, PS_A[64:65, :])
    vop(p, "tensor_scalar_mul", Crow, Crow, 1.0 / D)
    vop(p, "tensor_scalar_add", Crow, Crow, EPS)
    sop(p, "sqrt", Crow, Crow)
    vop(p, "reciprocal", Crow, Crow)
    vop(p, "tensor_copy", gc_f[0:4, :], PS_A[0:4, :])
    for kt in range(NQT):
        trgroup(p, [(PS_B[:, 0:128], gc_f[:, kt * 128:(kt + 1) * 128])], identf[:])
        vop(p, "tensor_copy", ltm[:, kt * 4:(kt + 1) * 4], PS_B[:, 0:4])
        vop(p, "tensor_copy", Ctm[:, kt:kt + 1], PS_B[:, 64:65])
    for g in range(NTG):
        gsl = slice(g * 512, (g + 1) * 512)
        mm(p, PS_B[:, 0:512], onesF[64:65, 0:128], Crow[0:1, gsl], True, True)
        vop(p, "tensor_copy", Cb, PS_B[:, 0:512])
        for c in range(NCH):
            p.dma("sync", xlc[:, 0:512], tn["xloc"][seq, c * 128:(c + 1) * 128, gsl])
            vop(p, "scalar_tensor_tensor", xfb[c][:, gsl], xlc[:, 0:512],
                n2s[:, c:c + 1], Cb, ALU.mult, ALU.mult)
    for kt in range(NQT):
        blk = ltm[:, kt * 4:(kt + 1) * 4]
        vop(p, "tensor_scalar_mul", blk, blk, Ctm[:, kt:kt + 1])
        negm = stt8[:, 0:1]; Z = stt8[:, 1:2]; rZ = stt8[:, 2:3]
        m1 = stt8[:, 3:4]; negm2 = stt8[:, 4:5]; m2 = stt8[:, 5:6]
        s12 = stt8[:, 6:7]; r12 = stt8[:, 7:8]
        vop(p, "reduce_max", negm, blk, AX.X, negate=True)
        sop(p, "activation", blk, blk, AF.Exp, bias=negm, scale=1.0,
            accum_out=Z)
        vop(p, "reciprocal", rZ, Z)
        vop(p, "tensor_scalar_mul", blk, blk, rZ)
        vop(p, "reduce_max", m1, blk, AX.X)
        vop(p, "tensor_scalar", eqt[:], blk, m1, None, ALU.is_equal)
        vop(p, "scalar_tensor_tensor", mskt[:], eqt[:], -1e30, blk,
            ALU.mult, ALU.add)
        vop(p, "reduce_max", negm2, mskt[:], AX.X, negate=True)
        vop(p, "tensor_scalar_mul", m2, negm2, -1.0)
        vop(p, "tensor_sub", s12, m1, negm2)
        vop(p, "tensor_scalar_add", s12, s12, 1e-8)
        vop(p, "reciprocal", r12, s12)
        cwblk = cw_tm[:, kt * 4:(kt + 1) * 4]
        vop(p, "tensor_scalar", cwblk, eqt[:], m1, None, ALU.mult)
        vop(p, "tensor_scalar", eqt[:], mskt[:], m2, None, ALU.is_equal)
        vop(p, "tensor_scalar", eqt[:], eqt[:], m2, None, ALU.mult)
        vop(p, "tensor_add", cwblk, cwblk, eqt[:])
        vop(p, "tensor_scalar_mul", cwblk, cwblk, r12)
    for kt in range(NQT):
        trgroup(p, [(PS_B[0:4, 0:128], cw_tm[:, kt * 4:(kt + 1) * 4])], identf[:])
        vop(p, "tensor_copy", cw_fm[0:4, kt * 128:(kt + 1) * 128], PS_B[0:4, 0:128])

    def bcast_cw(e):
        p.dma("sync", cwrow[:], cw_fm[e:e + 1, :])
        for g in range(NTG):
            mm(p, PS_B[:, 0:512], onesF[0:1, 0:128],
               cwrow[0:1, g * 512:(g + 1) * 512], True, True)
            vop(p, "tensor_copy", cwb[:, g * 512:(g + 1) * 512], PS_B[:, 0:512])

    def up_mm(wname, oc, src_list, nkc, ev):
        wbuf = wt[oc % 2]
        p.dma("gpsimd", wbuf[:, 0:nkc * 128], tn[wname][oc][:, :])
        for h in range(2):
            hsl = slice(h * 1024, (h + 1) * 1024)
            mm_group(p, [(PS_A[:, h * 1024 + g * 512:h * 1024 + (g + 1) * 512],
                          wbuf[:, kc * 128:(kc + 1) * 128],
                          src_list[kc][:, h * 1024 + g * 512:h * 1024 + (g + 1) * 512],
                          kc == 0, kc == nkc - 1)
                         for kc in range(nkc) for g in range(2)])
            ev(hsl)

    def down_acc(wname, src_list, nkc, first_e):
        for oc in range(2, NCH):
            wbuf = wt[oc % 2]
            p.dma("gpsimd", wbuf[:, 0:nkc * 128], tn[wname][oc][:, :])
            for g in range(NTG):
                gsl = slice(g * 512, (g + 1) * 512)
                par = (oc * NTG + g) % 2
                pps = PS_O if par == 0 else PS_B
                yb = y_t[(oc * NTG + g) % 4]
                mm_group(p, [(pps[:, 0:512], wbuf[:, kc * 128:(kc + 1) * 128],
                              src_list[kc][:, gsl], kc == 0, kc == nkc - 1)
                             for kc in range(nkc)])
                vop(p, "tensor_mul", yb, pps[:, 0:512], cwb[:, gsl])
                dst = tn["facc"][seq, oc * 128:(oc + 1) * 128, gsl]
                if first_e:
                    p.dma("gpsimd", dst, yb)
                else:
                    p.do("gpsimd", lambda en, d=dst, s=yb: en.dma_start(
                        out=d, in_=s, accum_op=ALU.add),
                        reads=[s if False else yb, dst], writes=[dst], dma=True)

    def gelu_to(dst):
        return lambda hsl: sop(p, "activation", dst[:, hsl], PS_A[:, hsl],
                               AF.Gelu, bias=0.0, scale=1.0)

    def silu_to(dst):
        return lambda hsl: sop(p, "activation", dst[:, hsl], PS_A[:, hsl],
                               AF.Silu, bias=0.0, scale=1.0)

    def mul_into(dst):
        return lambda hsl: vop(p, "tensor_mul", dst[:, hsl], PS_A[:, hsl],
                               dst[:, hsl])

    bcast_cw(0)
    for oc in range(16):
        up_mm("e0up", oc, xfb, NCH, gelu_to(A[oc]))
    down_acc("e0dn", A, 16, True)
    bcast_cw(1)
    for oc in range(16):
        up_mm("e1w1", oc, xfb, NCH, silu_to(A[oc]))
    for oc in range(16):
        up_mm("e1w2", oc, xfb, NCH, mul_into(A[oc]))
    down_acc("e1dn", A, 16, False)
    bcast_cw(2)
    for oc in range(16):
        up_mm("e2l1", oc, xfb, NCH, gelu_to(A[oc]))
    for oc in range(16):
        up_mm("e2l2", oc, A, 16, gelu_to(B[oc]))
    for oc in range(16):
        up_mm("e2l3", oc, B, 16, gelu_to(A[oc]))
    down_acc("e2l4", A, 16, False)
    bcast_cw(3)
    for oc in range(NCH):
        up_mm("e3dn", oc, xfb, NCH, gelu_to(A[oc]))
    for oc in range(16):
        up_mm("e3up", oc, A, NCH, gelu_to(B[oc]))
    down_acc("e3out", B, 16, False)
    for c in range(NCH):
        csl = slice(c * 128, (c + 1) * 128)
        p.dma("sync", xlc[:], tn["xloc"][seq, csl, :])
        if c >= 2:
            p.dma("sync", gc_f[:], tn["facc"][seq, csl, :])
            vop(p, "tensor_add", gc_f[:], gc_f[:], xlc[:])
        else:
            vop(p, "tensor_copy", gc_f[:], xlc[:])
        sop(p, "activation", gc_f[:], gc_f[:], AF.Silu, bias=0.0, scale=1.0)
        p.dma("sync", fb_s[:], tn["sbr"][seq, csl, :])
        vop(p, "tensor_mul", gc_f[:], gc_f[:], fb_s[:])
        p.dma("sync", fb_s[:], tn["xcur"][seq, csl, :])
        vop(p, "tensor_add", gc_f[:], gc_f[:], fb_s[:])
        p.dma("sync", tn["yout"][seq, csl, :], gc_f[:])


EMITTERS["ffn1b"] = emit_ffn1b



SCRATCH_F32 = ("xn32", "xcur", "sbr", "xloc", "facc")
SCRATCH_BF16 = ("znbf",)


def declare_tensors(nc, segs, produce, consume):
    tn = {}
    tn["xin"] = nc.dram_tensor("xin", [NSEQ, D, T], F32, kind="ExternalInput").ap()
    tn["yout"] = nc.dram_tensor("yout", [NSEQ, D, T], F32, kind="ExternalOutput").ap()
    wshapes = {
        "w_qkv0": ([24, 128, 1024], F32), "b_qkv0": ([128, 24], F32),
        "w_proj0": ([8, 128, 1024], F32), "b_proj0": ([128, 8], F32),
        "ln0w": ([128, 8], F32), "ln0b": ([128, 8], F32),
        "rw0": ([1, 128, 64], F32),
        "w1_0": ([8, 42, 128, 1024], F32), "b1_0": ([128, 336], F32),
        "w2_0": ([8, 8, 128, 2688], F32), "b2_0": ([128, 64], F32),
        "w_attn1": ([24, 128, 1024], F32), "b_attn1": ([128, 24], F32),
        "w_proj1": ([8, 128, 1024], F32), "b_proj1": ([128, 8], F32),
        "wkc1T": ([32, 64], F32), "wke1": ([32, 64], F32),
        "wvc1T": ([32, 64], F32), "wve1": ([32, 64], F32),
        "n1w": ([128, 8], F32), "n2w": ([128, 8], F32),
        "wq1": ([8, 128, 1024], F32), "wk1": ([8, 128, 1024], F32),
        "wv1": ([8, 128, 1024], F32), "wo1": ([8, 128, 1024], F32),
        "qn1": ([128, 1], F32), "kn1": ([128, 1], F32),
        "gw1": ([1, 128, 32], F32),
        "e0up": ([16, 128, 1024], F32), "e0dn": ([8, 128, 2048], F32),
        "e1w1": ([16, 128, 1024], F32), "e1w2": ([16, 128, 1024], F32),
        "e1dn": ([8, 128, 2048], F32),
        "e2l1": ([16, 128, 1024], F32), "e2l2": ([16, 128, 2048], F32),
        "e2l3": ([16, 128, 2048], F32), "e2l4": ([8, 128, 2048], F32),
        "e3dn": ([8, 128, 1024], F32), "e3up": ([16, 128, 1024], F32),
        "e3out": ([8, 128, 2048], F32),
    }
    need = weights_needed(segs)
    for name in need:
        shape, dt = wshapes[name]
        tn[name] = nc.dram_tensor(name, shape, dt, kind="ExternalInput").ap()
    for name in SCRATCH_F32:
        if name in produce and name in consume:
            kind = "Internal"
        elif name in produce:
            kind = "ExternalOutput"
        elif name in consume:
            kind = "ExternalInput"
        else:
            continue
        tn[name] = nc.dram_tensor(name, [NSEQ, D, T], F32, kind=kind).ap()
    for name in SCRATCH_BF16:
        if name in produce and name in consume:
            kind = "Internal"
        elif name in produce:
            kind = "ExternalOutput"
        elif name in consume:
            kind = "ExternalInput"
        else:
            continue
        tn[name] = nc.dram_tensor(name, [NSEQ, D, T], BF16, kind=kind).ap()
    return tn


def weights_needed(segs):
    m = {
        "ln0": ["ln0w", "ln0b"],
        "ln1": ["n1w"],
        "attn0": ["w_qkv0", "b_qkv0", "w_proj0", "b_proj0"],
        "moe0": ["rw0", "w1_0", "b1_0", "w2_0", "b2_0"],
        "attn1": ["w_attn1", "b_attn1", "w_proj1", "b_proj1",
                  "wkc1T", "wke1", "wvc1T", "wve1"],
        "ffn1a": ["wq1", "wk1", "wv1", "wo1", "qn1", "kn1"],
        "ffn1b": ["n2w", "gw1", "e0up", "e0dn", "e1w1", "e1w2", "e1dn",
                  "e2l1", "e2l2", "e2l3", "e2l4", "e3dn", "e3up", "e3out"],
    }
    out = []
    for s in segs:
        for wn in m.get(s, []):
            if wn not in out:
                out.append(wn)
    return out


SEG_IO = {
    "ln0": ((), ("xn32", "znbf")),
    "attn0": (("xn32", "znbf"), ("sbr",)),
    "moe0": (("xn32", "sbr"), ("xcur", "facc")),
    "ln1": (("xcur",), ("xn32", "znbf")),
    "attn1": (("xn32",), ("sbr",)),
    "ffn1a": (("xn32", "znbf"), ("xloc",)),
    "ffn1b": (("xloc", "xcur", "sbr"), ("facc",)),
}


SMALLS = {
    "ln0w": [128, 8], "ln0b": [128, 8], "n1w": [128, 8], "n2w": [128, 8],
    "b_qkv0": [128, 24], "b_attn1": [128, 24],
    "b_proj0": [128, 8], "b_proj1": [128, 8],
    "qn1": [128, 1], "kn1": [128, 1],
    "b1_0": [128, 8 * 42], "b2_0": [128, 8 * 8],
}


def alloc_sbuf(nc, stack, segs):
    sb = {}
    def S(name, shape, dt):
        sb[name] = stack.enter_context(nc.sbuf_tensor(name, shape, dt))
        return sb[name]
    for name in weights_needed(segs):
        if name in SMALLS:
            S(name + "_s", SMALLS[name], F32)
    S("ones_col", [128, 1], F32)
    S("ones_row", [1, 128], F32)
    S("onesb_col", [128, 1], BF16)
    S("onesb_row", [1, 128], BF16)
    S("ident", [128, 128], BF16)
    S("tri", [128, 128], F32)
    S("identf", [128, 128], F32)
    S("onesF", [128, 128], F32)
    return sb


def init_const(p, tn, segs):
    sb = tn["sb"]
    for name in weights_needed(segs):
        if name in SMALLS:
            mir = sb[name + "_s"]
            src = tn[name]
            p.dma("sync", mir[:], src[:])
    gop(p, "memset", sb["ones_col"][:], 1.0)
    gop(p, "memset", sb["ones_row"][:], 1.0)
    gop(p, "memset", sb["onesb_col"][:], 1.0)
    gop(p, "memset", sb["onesb_row"][:], 1.0)
    gop(p, "memset", sb["ident"][:], 1.0)
    gop(p, "affine_select", sb["ident"][:], sb["ident"][:], [[-1, 128]],
        ALU.is_equal, 0.0, base=0, channel_multiplier=1)
    gop(p, "memset", sb["tri"][:], 0.0)
    gop(p, "affine_select", sb["tri"][:], sb["tri"][:], [[-1, 128]],
        ALU.is_ge, -1e30, base=0, channel_multiplier=1)
    gop(p, "memset", sb["identf"][:], 1.0)
    gop(p, "affine_select", sb["identf"][:], sb["identf"][:], [[-1, 128]],
        ALU.is_equal, 0.0, base=0, channel_multiplier=1)
    gop(p, "memset", sb["onesF"][:], 1.0)


def build_nc(segs):
    consume, produce = set(), set()
    for s in segs:
        c, pr = SEG_IO[s]
        consume.update(x for x in c if x not in produce)
        produce.update(pr)
    nc = bass.Bass("TRN2", target_bir_lowering=False)
    stack = ExitStack()
    with stack:
        tn = declare_tensors(nc, segs, produce, consume)
        sb = alloc_sbuf(nc, stack, segs)
        tn["sb"] = sb
        tn["PS_A"] = stack.enter_context(nc.psum_tensor("PS_A", [128, 2048], F32))
        tn["PS_B"] = stack.enter_context(nc.psum_tensor("PS_B", [128, 512], F32))
        tn["PS_T"] = stack.enter_context(nc.psum_tensor("PS_T", [128, 1024], BF16))
        tn["PS_T2"] = stack.enter_context(nc.psum_tensor("PS_T2", [128, 1024], BF16))
        tn["PS_O"] = stack.enter_context(nc.psum_tensor("PS_O", [128, 512], F32))
        tn["nc"] = nc
        with nc.Block() as block:
            p = Prog(nc, stack, serial=SERIAL_MODE)
            init_const(p, tn, segs)
            for seq in range(NSEQ):
                for s in segs:
                    with ExitStack() as segstack:
                        tn["segstack"] = segstack
                        EMITTERS[s](p, tn, seq)
                    p.fence()
            p.finalize(block)
    return nc


def _emit_ln0(p, tn, seq):
    emit_ln(p, tn, seq, 0)


def _emit_ln1(p, tn, seq):
    emit_ln(p, tn, seq, 1)


EMITTERS["ln0"] = _emit_ln0
EMITTERS["ln1"] = _emit_ln1



ALL_SEGS = ["ln0", "attn0", "moe0", "ln1", "attn1", "ffn1a", "ffn1b"]


def kernel(**inputs):
    w = prep_weights(inputs)
    nc = build_nc(ALL_SEGS)
    need = weights_needed(ALL_SEGS)
    in_maps = []
    for core in range(NCORE):
        im = {"xin": prep_xin(inputs, core)}
        for n in need:
            im[n] = w[n]
        in_maps.append(im)
    res = run_bass_kernel_spmd(nc, in_maps, core_ids=list(range(NCORE)))
    out = np.empty((16, 2048, 1024), np.float32)
    for core in range(NCORE):
        y = np.asarray(res.results[core]["yout"])
        out[2 * core:2 * core + 2] = y.transpose(0, 2, 1)
    return out


# version 07
# speedup vs baseline: 1.8954x; 1.3230x over previous
import numpy as np
from contextlib import ExitStack

import concourse.bass as bass
import concourse.mybir as mybir
from concourse.bass_utils import run_bass_kernel_spmd

F32 = mybir.dt.float32
BF16 = mybir.dt.bfloat16
AF = mybir.ActivationFunctionType
ALU = mybir.AluOpType
AX = mybir.AxisListType

NCORE = 8
SERIAL_MODE = False
DMA_FULL_SERIAL = False
NSEQ = 2
D = 1024
T = 2048
NCH = D // 128
NH = 16
DH = 64
NTG = T // 512
NQT = T // 128


def tile_w(W, oc=128):
    IN, OUT = W.shape
    return np.ascontiguousarray(
        W.reshape(IN // 128, 128, OUT // oc, oc).transpose(2, 1, 0, 3)
        .reshape(OUT // oc, 128, (IN // 128) * oc))


def colvec(v):
    return np.ascontiguousarray(np.asarray(v).reshape(-1, 128).T)


def prep_weights(inp):
    w = {}
    w["w_qkv0"] = tile_w(inp["L0_attn_W_qkv"])
    w["b_qkv0"] = colvec(inp["L0_attn_b_qkv"])
    w["w_proj0"] = tile_w(inp["L0_attn_W_proj"])
    w["b_proj0"] = colvec(inp["L0_attn_b_proj"])
    w["ln0w"] = colvec(inp["L0_attn_ln_w"])
    w["ln0b"] = colvec(inp["L0_attn_ln_b"])
    w["rw0"] = tile_w(inp["L0_ffn_router_w"], oc=8)
    w["w1_0"] = np.stack([tile_w(inp["L0_ffn_W1"][e]) for e in range(8)])
    w["b1_0"] = np.concatenate([colvec(inp["L0_ffn_b1"][e]) for e in range(8)], axis=1)
    w["w2_0"] = np.stack([tile_w(inp["L0_ffn_W2"][e]) for e in range(8)])
    w["b2_0"] = np.concatenate([colvec(inp["L0_ffn_b2"][e]) for e in range(8)], axis=1)
    w["w_attn1"] = tile_w(inp["L1_attn_c_attn_w"])
    w["b_attn1"] = colvec(inp["L1_attn_c_attn_b"])
    w["w_proj1"] = tile_w(inp["L1_attn_c_proj_w"])
    w["b_proj1"] = colvec(inp["L1_attn_c_proj_b"])
    w["wkc1T"] = np.ascontiguousarray(inp["L1_attn_wk_c"].T)
    w["wke1"] = np.ascontiguousarray(inp["L1_attn_wk_e"])
    w["wvc1T"] = np.ascontiguousarray(inp["L1_attn_wv_c"].T)
    w["wve1"] = np.ascontiguousarray(inp["L1_attn_wv_e"])
    w["n1w"] = colvec(inp["L1_ffn_n1_w"])
    w["n2w"] = colvec(inp["L1_ffn_n2_w"])
    w["wq1"] = tile_w(inp["L1_ffn_wq"].T)
    w["wk1"] = tile_w(inp["L1_ffn_wk"].T)
    w["wv1"] = tile_w(inp["L1_ffn_wv"].T)
    w["wo1"] = tile_w(inp["L1_ffn_wo"].T)
    w["qn1"] = np.ascontiguousarray(np.tile(inp["L1_ffn_qn_w"], 2)[:, None])
    w["kn1"] = np.ascontiguousarray(np.tile(inp["L1_ffn_kn_w"], 2)[:, None])
    w["gw1"] = tile_w(inp["L1_ffn_gate_w"].T, oc=4)
    w["e0up"] = tile_w(inp["L1_ffn_e0_up"].T)
    w["e0dn"] = tile_w(inp["L1_ffn_e0_down"].T)
    w["e1w1"] = tile_w(inp["L1_ffn_e1_w1"].T)
    w["e1w2"] = tile_w(inp["L1_ffn_e1_w2"].T)
    w["e1dn"] = tile_w(inp["L1_ffn_e1_down"].T)
    w["e2l1"] = tile_w(inp["L1_ffn_e2_l1"].T)
    w["e2l2"] = tile_w(inp["L1_ffn_e2_l2"].T)
    w["e2l3"] = tile_w(inp["L1_ffn_e2_l3"].T)
    w["e2l4"] = tile_w(inp["L1_ffn_e2_l4"].T)
    w["e3dn"] = tile_w(inp["L1_ffn_e3_down"].T)
    w["e3up"] = tile_w(inp["L1_ffn_e3_up"].T)
    w["e3out"] = tile_w(inp["L1_ffn_e3_out"].T)
    return w


def prep_xin(inp, core):
    z = inp["L0_attn_z"][2 * core:2 * core + 2]
    return np.ascontiguousarray(z.transpose(0, 2, 1))



def _is_ap(x):
    return hasattr(x, "tensor") and hasattr(x, "ap") and hasattr(x, "offset")


def _ap_keys(ap):
    t = ap.tensor
    name = t.name
    kind = type(t).__name__
    dims = [(int(s), int(c)) for s, c in ap.ap]
    off = int(ap.offset)
    if kind == "PSumTensorHandle" or kind == "SBTensorHandle":
        F = int(t.shape[-1])
        pcount = 1
        free = []
        for s, c in dims:
            if s % F == 0 and s != 0 and c > 1:
                pcount = max(pcount, ((c - 1) * s) // F + 1)
            else:
                free.append((s, c))
        p0 = off // F
        c0 = off % F
        cspan = 1 + sum(s * (c - 1) for s, c in free)
        if c0 + cspan > F:
            p0, pcount, c0, cspan = 0, 128, 0, F
        if kind == "PSumTensorHandle":
            bank = 512 if str(t.dtype) == "dt.float32" else 1024
            return {(name, b) for b in range(c0 // bank, (c0 + cspan - 1) // bank + 1)}
        keys = set()
        for pb in range(p0 // 32, (p0 + pcount - 1) // 32 + 1):
            for cb in range(c0 // 512, (c0 + cspan - 1) // 512 + 1):
                keys.add((name, pb, cb))
        return keys
    span = 1 + sum(s * (c - 1) for s, c in dims)
    return {(name, "f", b) for b in range(off // 262144, (off + span - 1) // 262144 + 1)}


class Prog:
    ENG = ("sync", "gpsimd", "scalar", "vector", "tensor")

    def __init__(self, nc, stack, serial=False):
        self.nc = nc
        self.serial = serial
        self.steps = []
        self.writer = {}
        self.readers = {}
        self.sems = {}
        for e in self.ENG:
            self.sems[e] = stack.enter_context(nc.semaphore("sem_" + e))
        self.NSLOT = 8
        for q in ("dsync", "dgps"):
            for s in range(self.NSLOT):
                self.sems[(q, s)] = stack.enter_context(
                    nc.semaphore(f"sem_{q}{s}"))
        self.dma_count = {"sync": 0, "gpsimd": 0}
        self.last_slot_step = {}
        self.last_by_sem = {}
        self.pending_fence = None
        self.fence_done = set()

    def do(self, eng, fn, reads=(), writes=(), dma=False):
        idx = len(self.steps)
        if dma:
            qname = {"sync": "dsync", "gpsimd": "dgps"}[eng]
            slot = self.dma_count[eng] % self.NSLOT
            self.dma_count[eng] += 1
            sem_id = (qname, slot)
        else:
            sem_id = eng
        edges = set()
        if dma:
            key = qname if DMA_FULL_SERIAL else sem_id
            prev = self.last_slot_step.get(key)
            if prev is not None:
                edges.add(prev)
            self.last_slot_step[key] = idx
        rkeys = set()
        for ap in reads:
            rkeys |= _ap_keys(ap)
        wkeys = set()
        for ap in writes:
            wkeys |= _ap_keys(ap)
        if self.serial and idx > 0:
            edges.add(idx - 1)
        for k in rkeys:
            w = self.writer.get(k)
            if w is not None:
                edges.add(w)
        for k in wkeys:
            w = self.writer.get(k)
            if w is not None:
                edges.add(w)
            for r in self.readers.get(k, ()):
                edges.add(r)
        pruned = edges
        if self.pending_fence is not None and eng not in self.fence_done:
            for pi in self.pending_fence.values():
                pruned.add(pi)
            self.fence_done.add(eng)
            if len(self.fence_done) == len(self.ENG):
                self.pending_fence = None
        for k in rkeys:
            self.readers.setdefault(k, []).append(idx)
        for k in wkeys:
            self.writer[k] = idx
            self.readers[k] = []
        self.last_by_sem[sem_id] = idx
        self.steps.append((eng, fn, sem_id, 16 if dma else 1, pruned))

    def fence(self):
        self.pending_fence = dict(self.last_by_sem)
        self.fence_done = set()

    def dma(self, q, out, in_):
        self.do(q, lambda e, o=out, i=in_: e.dma_start(out=o, in_=i),
                reads=[in_], writes=[out], dma=True)

    def finalize(self, block):
        steps = self.steps
        n = len(steps)
        need = [False] * n
        for eng, fn, sem_id, amt, edges in steps:
            for e in edges:
                need[e] = True
        for i, (eng, fn, sem_id, amt, edges) in enumerate(steps):
            if amt == 16:
                need[i] = True
        val = [0] * n
        cum = {}
        for i, (eng, fn, sem_id, amt, edges) in enumerate(steps):
            if need[i]:
                cum[sem_id] = cum.get(sem_id, 0) + amt
            val[i] = cum.get(sem_id, 0)
        totals = dict(cum)
        per_eng = {e: [] for e in self.ENG}
        for i, (eng, fn, sem_id, amt, edges) in enumerate(steps):
            per_eng[eng].append(i)
        sems = self.sems

        def make_body(eng, idxs):
            def body(engh):
                known = {}
                for i in idxs:
                    _, fn, sem_id, amt, edges = steps[i]
                    waits = {}
                    for e in edges:
                        psem = steps[e][2]
                        v = val[e]
                        if v > waits.get(psem, 0):
                            waits[psem] = v
                    for psem, v in sorted(waits.items(), key=lambda kv: str(kv[0])):
                        if known.get(psem, 0) >= v:
                            continue
                        engh.wait_ge(sems[psem], v)
                        known[psem] = v
                    ins = fn(engh)
                    if need[i]:
                        if ins is None:
                            ins = engh.nop()
                        ins.then_inc(sems[sem_id], amt)
                return None
            return body

        for e in self.ENG:
            if per_eng[e]:
                block.__getattribute__(e)(make_body(e, per_eng[e]))

        def tail(engh):
            for q, tot in totals.items():
                if isinstance(q, tuple) and tot:
                    engh.wait_ge(sems[q], tot)
            for e in ("tensor", "vector", "scalar", "sync"):
                if totals.get(e):
                    engh.wait_ge(sems[e], totals[e])
        block.gpsimd(tail)



def mm(p, out_ps, lhsT, rhs, start, stop):
    p.do("tensor", lambda t: t.matmul(out_ps, lhsT, rhs, start=start, stop=stop),
         reads=[lhsT, rhs], writes=[out_ps])


def mm_group(p, group):
    def fn(t):
        last = None
        for o, l, r, st, sp in group:
            last = t.matmul(o, l, r, start=st, stop=sp)
        return last
    reads = [g[1] for g in group] + [g[2] for g in group]
    writes = [g[0] for g in group]
    p.do("tensor", fn, reads=reads, writes=writes)


def trgroup(p, items, ident):
    def fn(t):
        last = None
        for o, i in items:
            last = t.transpose(o, i, ident)
        return last
    p.do("tensor", fn, reads=[i for _, i in items] + [ident],
         writes=[o for o, _ in items])


def _roles(args, kw):
    writes = [a for a in args[:1] if _is_ap(a)]
    reads = [a for a in args[1:] if _is_ap(a)]
    for k, v in kw.items():
        if _is_ap(v):
            (writes if k == "accum_out" else reads).append(v)
    return reads, writes


def vop(p, name, *args, **kw):
    reads, writes = _roles(args, kw)
    p.do("vector", lambda e: getattr(e, name)(*args, **kw),
         reads=reads, writes=writes)


def sop(p, name, *args, **kw):
    reads, writes = _roles(args, kw)
    p.do("scalar", lambda e: getattr(e, name)(*args, **kw),
         reads=reads, writes=writes)


def gop(p, name, *args, **kw):
    reads, writes = _roles(args, kw)
    p.do("gpsimd", lambda e: getattr(e, name)(*args, **kw),
         reads=reads, writes=writes)


EMITTERS = {}
_UNIQ = [0]


def uname(n):
    _UNIQ[0] += 1
    return f"{n}_{_UNIQ[0]}"


def emit_ln(p, tn, seq, layer):
    x = tn["xin"] if layer == 0 else tn["xcur"]
    sb = tn["sb"]
    st = tn["segstack"]
    nc = tn["nc"]
    xf = [st.enter_context(nc.sbuf_tensor(uname(f"lnx{c}"), [128, T], F32))
          for c in range(NCH)]
    xsq = [st.enter_context(nc.sbuf_tensor(uname(f"lnsq{i}"), [128, T], F32))
           for i in range(2)]
    t1 = [st.enter_context(nc.sbuf_tensor(uname(f"lnt1{i}"), [128, T], F32))
          for i in range(2)]
    znt = [st.enter_context(nc.sbuf_tensor(uname(f"lnzn{i}"), [128, T], BF16))
           for i in range(2)]
    R = st.enter_context(nc.sbuf_tensor(uname("lnrows"), [1, 4 * T], F32))
    Ab = st.enter_context(nc.sbuf_tensor(uname("lnAb"), [128, T], F32))
    Bb = st.enter_context(nc.sbuf_tensor(uname("lnBb"), [128, T], F32))
    Cb = st.enter_context(nc.sbuf_tensor(uname("lnCb"), [128, T], F32))
    ones_col = sb["ones_col"]
    ones_row = sb["ones_row"]
    PS_A = tn["PS_A"]
    PS_B = tn["PS_B"]
    R1 = R[0:1, 0:2048]; R2 = R[0:1, 2048:4096]
    R3 = R[0:1, 4096:6144]; R4 = R[0:1, 6144:8192]

    for c in range(NCH):
        p.dma("sync", xf[c][:], x[seq, c * 128:(c + 1) * 128, :])
    for c in range(NCH):
        mm_group(p, [(PS_A[0:1, g * 512:(g + 1) * 512], ones_col[:],
                      xf[c][:, g * 512:(g + 1) * 512], c == 0, c == NCH - 1)
                     for g in range(NTG)])
    vop(p, "tensor_copy", R1, PS_A[0:1, :])
    for c in range(NCH):
        sop(p, "square", xsq[c % 2][:], xf[c][:])
        mm_group(p, [(PS_A[0:1, g * 512:(g + 1) * 512], ones_col[:],
                      xsq[c % 2][:, g * 512:(g + 1) * 512], c == 0, c == NCH - 1)
                     for g in range(NTG)])
    vop(p, "tensor_copy", R2, PS_A[0:1, :])
    vop(p, "tensor_scalar_mul", R1, R1, 1.0 / D)
    vop(p, "tensor_scalar_mul", R2, R2, 1.0 / D)
    vop(p, "tensor_mul", R3, R1, R1)
    vop(p, "tensor_sub", R2, R2, R3)
    vop(p, "tensor_scalar_add", R3, R2, 1e-5)
    sop(p, "sqrt", R3, R3)
    vop(p, "reciprocal", R3, R3)
    vop(p, "tensor_mul", R4, R1, R3)
    def bcast(row, dst):
        for g in range(NTG):
            mm(p, PS_B[:, 0:512], ones_row[:], row[0:1, g * 512:(g + 1) * 512],
               True, True)
            vop(p, "tensor_copy", dst[:, g * 512:(g + 1) * 512], PS_B[:, 0:512])
    bcast(R3, Ab)
    bcast(R4, Bb)
    eps_i = 1e-5 if layer == 0 else 1.1920929e-07
    vop(p, "tensor_mul", R1, R3, R3)
    vop(p, "tensor_mul", R1, R1, R2)
    vop(p, "tensor_scalar_add", R1, R1, eps_i)
    sop(p, "sqrt", R1, R1)
    vop(p, "reciprocal", R1, R1)
    bcast(R1, Cb)
    wcol = sb["ln0w_s"] if layer == 0 else sb["n1w_s"]
    bcol = sb["ln0b_s"] if layer == 0 else None
    for c in range(NCH):
        tb, zb = t1[c % 2], znt[c % 2]
        vop(p, "tensor_mul", tb[:], xf[c][:], Ab[:])
        vop(p, "tensor_sub", tb[:], tb[:], Bb[:])
        p.dma("sync", tn["xn32"][seq, c * 128:(c + 1) * 128, :], tb[:])
        vop(p, "tensor_mul", tb[:], tb[:], Cb[:])
        if bcol is not None:
            vop(p, "tensor_scalar", zb[:], tb[:], wcol[:, c:c + 1],
                bcol[:, c:c + 1], ALU.mult, ALU.add)
        else:
            vop(p, "tensor_scalar_mul", zb[:], tb[:], wcol[:, c:c + 1])
        p.dma("sync", tn["znbf"][seq, c * 128:(c + 1) * 128, :], zb[:])



def emit_attn(p, tn, seq, cfg):
    nc = tn["nc"]
    st = tn["segstack"]
    sb = tn["sb"]
    PS_A, PS_B, PS_O = tn["PS_A"], tn["PS_B"], tn["PS_O"]
    PS_T = [tn["PS_T"], tn["PS_T2"]]
    ident = sb["ident"]
    ones_col = sb["ones_col"]
    ones_row = sb["ones_row"]
    causal = cfg["causal"]
    scale = 0.125

    def S_(name, shape, dt):
        return st.enter_context(nc.sbuf_tensor(uname(name), shape, dt))

    zn = [S_(f"at_zn{c}", [128, T], BF16) for c in range(NCH)]
    Ob = [S_(f"at_o{c}", [128, T], BF16) for c in range(NCH)]
    q2_ = [S_("at_q2a", [128, T], BF16), S_("at_q2b", [128, T], BF16)]
    k2_ = [S_("at_k2a", [128, T], BF16), S_("at_k2b", [128, T], BF16)]
    v2_ = [S_("at_v2a", [128, T], BF16), S_("at_v2b", [128, T], BF16)]
    vtm_ = [S_("at_vtma", [128, T], BF16), S_("at_vtmb", [128, T], BF16)]
    Pt = [S_("at_p0", [128, T], BF16), S_("at_p1", [128, T], BF16)]
    ptq = [S_("at_pt0", [128, T], BF16), S_("at_pt1", [128, T], BF16)]
    wt = [S_(f"at_wt{i}", [128, 3 * 1024], BF16) for i in range(3)]
    stt4 = S_("at_st", [128, 8], F32)
    bsq_ = [S_("at_bsqa", [128, T], BF16), S_("at_bsqb", [128, T], BF16)]
    nbf = S_("at_nbf", [128, T], F32)
    nbias_ = [S_("at_nba", [128, 2 * NQT], F32), S_("at_nbb", [128, 2 * NQT], F32)]
    brow = S_("at_brow", [1, T + 8], F32)
    qk_rms = cfg.get("qk_rms")
    if qk_rms:
        rqb = S_("at_rqb", [128, T], F32)
        rkb = S_("at_rkb", [128, T], F32)
        R = S_("at_rows", [1, T], F32)
    if cfg.get("latent"):
        q2p_ = [S_("at_q2pa", [128, T], BF16), S_("at_q2pb", [128, T], BF16)]
        v2e_ = [S_("at_v2ea", [128, T], BF16), S_("at_v2eb", [128, T], BF16)]
        lw = S_("at_lw", [32, 4 * 64], BF16)
        W64 = S_("at_w64", [128, 128], BF16)
        p.dma("gpsimd", lw[:, 0:64], tn["wke1"][:])
        p.dma("gpsimd", lw[:, 64:128], tn["wkc1T"][:])
        p.dma("gpsimd", lw[:, 128:192], tn["wvc1T"][:])
        p.dma("gpsimd", lw[:, 192:256], tn["wve1"][:])
        for b64 in (0, 64):
            mm(p, PS_O[b64:b64 + 64, 0:64], lw[:, 0:64], lw[:, 64:128], True, True)
            vop(p, "tensor_copy", W64[b64:b64 + 64, 0:64], PS_O[b64:b64 + 64, 0:64])
            mm(p, PS_O[b64:b64 + 64, 0:64], lw[:, 128:192], lw[:, 192:256], True, True)
            vop(p, "tensor_copy", W64[b64:b64 + 64, 64:128], PS_O[b64:b64 + 64, 0:64])

    gop(p, "memset", nbf[:], 0.0)
    if cfg["src"] == "xn32cast":
        for c in range(NCH):
            p.dma("gpsimd", zn[c][:], tn["xn32"][seq, c * 128:(c + 1) * 128, :])
    else:
        src = tn[cfg["src"]]
        for c in range(NCH):
            p.dma("sync", zn[c][:], src[seq, c * 128:(c + 1) * 128, :])

    bias = sb[cfg["qkv_bias"] + "_s"] if cfg.get("qkv_bias") else None

    for hp in range(8):
        wbuf = wt[hp % 3]
        pp = hp % 2
        q2, k2, v2, vtm = q2_[pp], k2_[pp], v2_[pp], vtm_[pp]
        bsq, nbias = bsq_[pp], nbias_[pp]
        if cfg.get("latent"):
            q2p, v2e = q2p_[pp], v2e_[pp]
        if "fused_qkv" in cfg:
            wten = tn[cfg["fused_qkv"]]
            ocs = [hp, 8 + hp, 16 + hp]
            srcs = [wten[o] for o in ocs]
        else:
            srcs = [tn[n][hp] for n in cfg["sep_qkv"]]
            ocs = [hp, 8 + hp, 16 + hp]
        for i in range(3):
            p.dma("gpsimd", wbuf[:, i * 1024:(i + 1) * 1024], srcs[i][:, :])
        for i, dst in enumerate((q2, k2, v2)):
            if qk_rms and i < 2:
                for h in range(2):
                    hsl = slice(h * 1024, (h + 1) * 1024)
                    mm_group(p, [(PS_A[:, h * 1024 + g * 512:h * 1024 + (g + 1) * 512],
                                  wbuf[:, i * 1024 + c * 128:i * 1024 + (c + 1) * 128],
                                  zn[c][:, h * 1024 + g * 512:h * 1024 + (g + 1) * 512],
                                  c == 0, c == NCH - 1)
                                 for c in range(NCH) for g in range(2)])
                    vop(p, "tensor_copy", dst[:, hsl], PS_A[:, hsl])
                qn_s = sb[qk_rms[i] + "_s"]
                eps = qk_rms[2]
                rb = rqb if i == 0 else rkb
                sop(p, "square", bsq[:], dst[:])
                for h2 in range(2):
                    b64 = h2 * 64
                    Rrow = R[0:1, 0:T]
                    for g in range(NTG):
                        mm(p, PS_B[0:1, 0:512], sb["onesb_col"][b64:b64 + 64, :],
                           bsq[b64:b64 + 64, g * 512:(g + 1) * 512], True, True)
                        vop(p, "tensor_copy", Rrow[0:1, g * 512:(g + 1) * 512],
                            PS_B[0:1, 0:512])
                    vop(p, "tensor_scalar_mul", Rrow, Rrow, 1.0 / 64)
                    vop(p, "tensor_scalar_add", Rrow, Rrow, eps)
                    sop(p, "sqrt", Rrow, Rrow)
                    vop(p, "reciprocal", Rrow, Rrow)
                    for g in range(NTG):
                        mm(p, PS_O[:, 0:512], ones_row[:],
                           Rrow[0:1, g * 512:(g + 1) * 512], True, True)
                        vop(p, "tensor_copy",
                            rb[b64:b64 + 64, g * 512:(g + 1) * 512],
                            PS_O[b64:b64 + 64, 0:512])
                vop(p, "scalar_tensor_tensor", dst[:], dst[:],
                    qn_s[:, 0:1], rb[:], ALU.mult, ALU.mult)
            else:
                for h in range(2):
                    hsl = slice(h * 1024, (h + 1) * 1024)
                    mm_group(p, [(PS_A[:, h * 1024 + g * 512:h * 1024 + (g + 1) * 512],
                                  wbuf[:, i * 1024 + c * 128:i * 1024 + (c + 1) * 128],
                                  zn[c][:, h * 1024 + g * 512:h * 1024 + (g + 1) * 512],
                                  c == 0, c == NCH - 1)
                                 for c in range(NCH) for g in range(2)])
                    if bias is not None:
                        vop(p, "tensor_scalar_add", dst[:, hsl], PS_A[:, hsl],
                            bias[:, ocs[i]:ocs[i] + 1])
                    else:
                        vop(p, "tensor_copy", dst[:, hsl], PS_A[:, hsl])
        qsrc, vsrc = q2, v2
        if cfg.get("latent"):
            for h2 in range(2):
                b64 = h2 * 64
                for g in range(NTG):
                    gsl = slice(g * 512, (g + 1) * 512)
                    mm(p, PS_B[b64:b64 + 64, 0:512], W64[b64:b64 + 64, 0:64],
                       q2[b64:b64 + 64, gsl], True, True)
                    vop(p, "tensor_copy", q2p[b64:b64 + 64, gsl],
                        PS_B[b64:b64 + 64, 0:512])
                    mm(p, PS_B[b64:b64 + 64, 0:512], W64[b64:b64 + 64, 64:128],
                       v2[b64:b64 + 64, gsl], True, True)
                    vop(p, "tensor_copy", v2e[b64:b64 + 64, gsl],
                        PS_B[b64:b64 + 64, 0:512])
            qsrc, vsrc = q2p, v2e
        for half in range(2):
            trgroup(p, [(PS_T[half][:, i * 128:(i + 1) * 128],
                         vsrc[:, (half * 8 + i) * 128:(half * 8 + i + 1) * 128])
                        for i in range(8)], ident[:])
            vop(p, "tensor_copy", vtm[:, half * 1024:(half + 1) * 1024],
                PS_T[half][:, :])
        onesb = sb["onesb_col"]
        sop(p, "square", bsq[:], qsrc[:, :])
        for h2 in range(2):
            b64 = h2 * 64
            for g in range(NTG):
                mm(p, PS_B[0:1, 0:512], onesb[b64:b64 + 64, :],
                   bsq[b64:b64 + 64, g * 512:(g + 1) * 512], True, True)
                vop(p, "tensor_copy", brow[0:1, g * 512:(g + 1) * 512],
                    PS_B[0:1, 0:512])
            vop(p, "tensor_copy", nbf[0:1, 0:T], brow[0:1, 0:T])
            sop(p, "square", bsq[b64:b64 + 64, :], k2[b64:b64 + 64, :])
            for g in range(NTG):
                mm(p, PS_B[0:1, 0:512], onesb[b64:b64 + 64, :],
                   bsq[b64:b64 + 64, g * 512:(g + 1) * 512], True, True)
                vop(p, "tensor_copy", brow[0:1, g * 512:(g + 1) * 512],
                    PS_B[0:1, 0:512])
            vop(p, "reduce_max", brow[0:1, T:T + 1], brow[0:1, 0:T], AX.X)
            vop(p, "tensor_scalar", nbf[0:1, 0:T], nbf[0:1, 0:T],
                brow[0:1, T:T + 1], -0.0625, ALU.add, ALU.mult)
            for r4 in range(4):
                trgroup(p, [(PS_B[:, i * 128:(i + 1) * 128],
                             nbf[:, (r4 * 4 + i) * 128:(r4 * 4 + i + 1) * 128])
                            for i in range(4)], sb["identf"][:])
                vop(p, "tensor_copy",
                    nbias[:, h2 * NQT + r4 * 4:h2 * NQT + r4 * 4 + 4],
                    PS_B[:, 0:512].rearrange("p (i c) -> p i c", c=128)[:, :, 0:1])
        if hp == 0 and "dbg" in tn:
            p.dma("sync", tn["dbg"][0:128, 0:32], nbias[:, :])
            p.dma("gpsimd", tn["dbg"][0:128, 32:2080], q2[:, :])
            p.dma("gpsimd", tn["dbg"][0:128, 2080:4128], k2[:, :])
        def it_params(it):
            qt, h2 = divmod(it, 2)
            par = it % 2
            nk = (qt + 1) * 128 if causal else T
            return qt, h2, h2 * 64, par, nk

        def stageA(it):
            qt, h2, b64, par, nk = it_params(it)
            bcol = nbias[:, h2 * NQT + qt:h2 * NQT + qt + 1]
            lsum = stt4[:, par * 4:par * 4 + 1]
            lsum2 = stt4[:, par * 4 + 1:par * 4 + 2]
            linv = stt4[:, par * 4 + 2:par * 4 + 3]
            if nk <= 1024:
                spans = [(par * 1024, 0, nk)]
            else:
                spans = [(0, 0, 1024), (1024, 1024, nk - 1024)]
            for si, (base, lo, ln_) in enumerate(spans):
                mm_group(p, [(PS_A[:, base + g * 512:base + min((g + 1) * 512, ln_)],
                              qsrc[b64:b64 + 64, qt * 128:(qt + 1) * 128],
                              k2[b64:b64 + 64, lo + g * 512:lo + min((g + 1) * 512, ln_)],
                              True, True) for g in range((ln_ + 511) // 512)])
                if causal and lo + ln_ == nk:
                    vop(p, "tensor_add", PS_A[:, base + ln_ - 128:base + ln_],
                        PS_A[:, base + ln_ - 128:base + ln_], sb["tri"][:])
                sop(p, "activation", Pt[par][:, lo:lo + ln_],
                    PS_A[:, base:base + ln_], AF.Exp,
                    bias=bcol, scale=scale,
                    accum_out=(lsum if si == 0 else lsum2))
            if len(spans) > 1:
                vop(p, "tensor_add", lsum, lsum, lsum2)
            vop(p, "reciprocal", linv, lsum)
            vop(p, "tensor_scalar_mul", Pt[par][:, 0:nk], Pt[par][:, 0:nk],
                linv)

        def stageB(it):
            qt, h2, b64, par, nk = it_params(it)
            ktn = nk // 128
            for half in range((ktn + 7) // 8):
                n8 = min(8, ktn - half * 8)
                trgroup(p, [(PS_T[half][:, i * 128:(i + 1) * 128],
                             Pt[par][:, (half * 8 + i) * 128:(half * 8 + i + 1) * 128])
                            for i in range(n8)], ident[:])
                vop(p, "tensor_copy",
                    ptq[par][:, half * 1024:half * 1024 + n8 * 128],
                    PS_T[half][:, 0:n8 * 128])

        def stageC(it):
            qt, h2, b64, par, nk = it_params(it)
            ktn = nk // 128
            pvps = PS_O if par == 0 else PS_B
            mm_group(p, [(pvps[0:64, 0:128],
                          vtm[:, kt * 128 + b64:kt * 128 + b64 + 64],
                          ptq[par][:, kt * 128:(kt + 1) * 128],
                          kt == 0, kt == ktn - 1) for kt in range(ktn)])
            vop(p, "tensor_copy",
                Ob[hp][b64:b64 + 64, qt * 128:(qt + 1) * 128],
                pvps[0:64, 0:128])

        nits = 2 * NQT
        for t in range(nits + 2):
            if t < nits:
                stageA(t)
            if 1 <= t < nits + 1:
                stageB(t - 1)
            if 2 <= t < nits + 2:
                stageC(t - 2)
    pbias = sb[cfg["proj_bias"] + "_s"] if cfg.get("proj_bias") else None
    xt = [S_("at_xt0", [128, 512], F32), S_("at_xt1", [128, 512], F32)]
    s_t = [S_("at_s0", [128, 512], F32), S_("at_s1", [128, 512], F32)]
    wp = tn[cfg["proj_w"]]
    out = tn[cfg["out"]]
    for oc in range(NCH):
        if oc in cfg.get("skip_oc", ()):
            for g in range(NTG):
                gsl = slice(g * 512, (g + 1) * 512)
                par = g % 2
                if cfg.get("add_xn"):
                    p.dma("sync", xt[par][:], tn["xn32"][seq, oc * 128:(oc + 1) * 128, gsl])
                    p.dma("sync", out[seq, oc * 128:(oc + 1) * 128, gsl], xt[par][:])
                else:
                    gop(p, "memset", s_t[par][:], 0.0)
                    p.dma("sync", out[seq, oc * 128:(oc + 1) * 128, gsl], s_t[par][:])
            continue
        p.dma("gpsimd", wt[oc % 3][:, 0:1024], wp[oc][:, :])
        for g in range(NTG):
            gsl = slice(g * 512, (g + 1) * 512)
            par = (oc * NTG + g) % 2
            pps = PS_O if par == 0 else PS_B
            mm_group(p, [(pps[:, 0:512], wt[oc % 3][:, c * 128:(c + 1) * 128],
                          Ob[c][:, gsl], c == 0, c == NCH - 1)
                         for c in range(NCH)])
            if cfg.get("add_xn"):
                p.dma("sync", xt[par][:], tn["xn32"][seq, oc * 128:(oc + 1) * 128, gsl])
                if pbias is not None:
                    vop(p, "scalar_tensor_tensor", s_t[par][:], pps[:, 0:512],
                        pbias[:, oc:oc + 1], xt[par][:], ALU.add, ALU.add)
                else:
                    vop(p, "tensor_add", s_t[par][:], pps[:, 0:512], xt[par][:])
            else:
                if pbias is not None:
                    vop(p, "tensor_scalar_add", s_t[par][:], pps[:, 0:512],
                        pbias[:, oc:oc + 1])
                else:
                    vop(p, "tensor_copy", s_t[par][:], pps[:, 0:512])
            p.dma("sync", out[seq, oc * 128:(oc + 1) * 128, gsl], s_t[par][:])


def _emit_attn0(p, tn, seq):
    emit_attn(p, tn, seq, dict(
        src="znbf", fused_qkv="w_qkv0", qkv_bias="b_qkv0", causal=False,
        proj_w="w_proj0", proj_bias="b_proj0", add_xn=True, out="sbr"))


def _emit_attn1(p, tn, seq):
    emit_attn(p, tn, seq, dict(
        src="xn32cast", fused_qkv="w_attn1", qkv_bias="b_attn1", causal=True,
        latent=True, proj_w="w_proj1", proj_bias="b_proj1", add_xn=False,
        out="sbr"))


def _emit_ffn1a(p, tn, seq):
    emit_attn(p, tn, seq, dict(
        src="znbf", sep_qkv=("wq1", "wk1", "wv1"), causal=True,
        qk_rms=("qn1", "kn1", 1.1920929e-07),
        proj_w="wo1", add_xn=True, skip_oc={0, 1}, out="xloc"))


EMITTERS["attn0"] = _emit_attn0
EMITTERS["attn1"] = _emit_attn1
EMITTERS["ffn1a"] = _emit_ffn1a



def emit_moe0(p, tn, seq):
    nc = tn["nc"]
    st = tn["segstack"]
    sb = tn["sb"]
    PS_A, PS_B, PS_T, PS_O = tn["PS_A"], tn["PS_B"], tn["PS_T"], tn["PS_O"]
    ones_row = sb["ones_row"]
    identf = sb["identf"]

    def S_(name, shape, dt):
        return st.enter_context(nc.sbuf_tensor(uname(name), shape, dt))

    xnb = [S_(f"mo_xn{c}", [128, T], BF16) for c in range(NCH)]
    act = [S_(f"mo_act{j}", [128, T], BF16) for j in range(21)]
    sg = S_("mo_sg", [128, T], BF16)
    wt = [S_(f"mo_wt{i}", [128, 1024], BF16) for i in range(4)]
    wt2 = [S_(f"mo_wt2{i}", [128, 2688], BF16) for i in range(2)]
    rw = S_("mo_rw", [128, 64], F32)
    xc = [S_("mo_xc0", [128, T], F32), S_("mo_xc1", [128, T], F32)]
    route_f = S_("mo_rf", [128, T], F32)
    route_tm = S_("mo_rtm", [128, 128], F32)
    cw_tm = S_("mo_cwtm", [128, 128], F32)
    cw_fm = S_("mo_cwfm", [8, T], F32)
    eqt = S_("mo_eq", [128, 8], F32)
    mskt = S_("mo_msk", [128, 8], F32)
    stt8 = S_("mo_st", [128, 8], F32)
    cwrow = S_("mo_cwrow", [1, T], F32)
    cwb = S_("mo_cwb", [128, T], F32)
    y_t = [S_(f"mo_y{i}", [128, 512], F32) for i in range(2)]
    cmb = S_("mo_cmb", [128, T], F32)
    ft = route_f
    s_t = cwb

    for c in range(NCH):
        p.dma("gpsimd", xnb[c][:], tn["xn32"][seq, c * 128:(c + 1) * 128, :])
    p.dma("sync", rw[:], tn["rw0"][0][:, :])
    for c in range(NCH):
        p.dma("sync", xc[c % 2][:], tn["xn32"][seq, c * 128:(c + 1) * 128, :])
        mm_group(p, [(PS_A[0:8, g * 512:(g + 1) * 512], rw[:, c * 8:(c + 1) * 8],
                      xc[c % 2][:, g * 512:(g + 1) * 512], c == 0, c == NCH - 1)
                     for g in range(NTG)])
    gop(p, "memset", route_f[:], 0.0)
    vop(p, "tensor_copy", route_f[0:8, :], PS_A[0:8, :])
    for kt in range(NQT):
        trgroup(p, [(PS_B[:, 0:128], route_f[:, kt * 128:(kt + 1) * 128])], identf[:])
        vop(p, "tensor_copy", route_tm[:, kt * 8:(kt + 1) * 8], PS_B[:, 0:8])
    for kt in range(NQT):
        blk = route_tm[:, kt * 8:(kt + 1) * 8]
        m1 = stt8[:, 0:1]; negm2 = stt8[:, 1:2]; m2 = stt8[:, 2:3]
        negm1 = stt8[:, 3:4]; w1 = stt8[:, 4:5]; w2 = stt8[:, 5:6]
        vop(p, "reduce_max", m1, blk, AX.X)
        vop(p, "tensor_scalar", eqt[:], blk, m1, None, ALU.is_equal)
        vop(p, "scalar_tensor_tensor", mskt[:], eqt[:], -1e30, blk,
            ALU.mult, ALU.add)
        vop(p, "reduce_max", negm2, mskt[:], AX.X, negate=True)
        vop(p, "tensor_scalar_mul", m2, negm2, -1.0)
        vop(p, "tensor_scalar_mul", negm1, m1, -1.0)
        sop(p, "activation", w1, m1, AF.Sigmoid, bias=negm2, scale=1.0)
        sop(p, "activation", w2, m2, AF.Sigmoid, bias=negm1, scale=1.0)
        vop(p, "tensor_scalar", eqt[:], mskt[:], m2, None, ALU.is_equal)
        cwb_blk = cw_tm[:, kt * 8:(kt + 1) * 8]
        vop(p, "tensor_scalar", mskt[:], blk, m1, None, ALU.is_equal)
        vop(p, "tensor_scalar_mul", mskt[:], mskt[:], w1)
        vop(p, "tensor_scalar", eqt[:], eqt[:], w2, None, ALU.mult)
        vop(p, "tensor_add", cwb_blk, mskt[:], eqt[:])
    for kt in range(NQT):
        trgroup(p, [(PS_B[0:8, 0:128], cw_tm[:, kt * 8:(kt + 1) * 8])], identf[:])
        vop(p, "tensor_copy", cw_fm[0:8, kt * 128:(kt + 1) * 128], PS_B[0:8, 0:128])

    b1s = sb["b1_0_s"]
    b2s = sb["b2_0_s"]
    for e in range(8):
        p.dma("sync", cwrow[:], cw_fm[e:e + 1, :])
        for g in range(NTG):
            mm(p, PS_B[:, 0:512], ones_row[:], cwrow[0:1, g * 512:(g + 1) * 512],
               True, True)
            vop(p, "tensor_copy", cwb[:, g * 512:(g + 1) * 512], PS_B[:, 0:512])
        for j in range(21):
            for oc, kind in ((21 + j, "g"), (j, "a")):
                wbuf = wt[(2 * j + (0 if kind == "g" else 1)) % 4]
                p.dma("gpsimd", wbuf[:], tn["w1_0"][e, oc][:, :])
                bcol = b1s[:, e * 42 + oc:e * 42 + oc + 1]
                for h in range(2):
                    hsl = slice(h * 1024, (h + 1) * 1024)
                    mm_group(p, [(PS_A[:, h * 1024 + g * 512:h * 1024 + (g + 1) * 512],
                                  wbuf[:, c * 128:(c + 1) * 128],
                                  xnb[c][:, h * 1024 + g * 512:h * 1024 + (g + 1) * 512],
                                  c == 0, c == NCH - 1)
                                 for c in range(NCH) for g in range(2)])
                    if kind == "g":
                        sop(p, "activation", sg[:, hsl], PS_A[:, hsl], AF.Silu,
                            bias=bcol, scale=1.0)
                    else:
                        vop(p, "scalar_tensor_tensor", act[j][:, hsl],
                            PS_A[:, hsl], bcol, sg[:, hsl], ALU.add, ALU.mult)
        for oc in range(NCH):
            w2b = wt2[oc % 2]
            p.dma("gpsimd", w2b[:], tn["w2_0"][e, oc][:, :])
            for g in range(NTG):
                gsl = slice(g * 512, (g + 1) * 512)
                par = (oc * NTG + g) % 2
                pps = PS_O if par == 0 else PS_B
                yb = y_t[(oc * NTG + g) % 2]
                mm_group(p, [(pps[:, 0:512], w2b[:, kc * 128:(kc + 1) * 128],
                              act[kc][:, gsl], kc == 0, kc == 20)
                             for kc in range(21)])
                vop(p, "scalar_tensor_tensor", yb[:], pps[:, 0:512],
                    b2s[:, e * 8 + oc:e * 8 + oc + 1], cwb[:, gsl],
                    ALU.add, ALU.mult)
                dst = tn["facc"][seq, oc * 128:(oc + 1) * 128, gsl]
                if e == 0:
                    p.dma("gpsimd", dst, yb[:])
                else:
                    p.do("gpsimd", lambda en, d=dst, s=yb: en.dma_start(
                        out=d, in_=s[:], accum_op=ALU.add),
                        reads=[yb[:], dst], writes=[dst], dma=True)
    for c in range(NCH):
        csl = slice(c * 128, (c + 1) * 128)
        fb = ft if c % 2 == 0 else cmb
        sb_ = s_t
        xb = xc[c % 2]
        p.dma("sync", fb[:], tn["facc"][seq, csl, :])
        p.dma("sync", sb_[:], tn["sbr"][seq, csl, :])
        p.dma("sync", xb[:], tn["xin"][seq, csl, :])
        sop(p, "activation", fb[:], fb[:], AF.Silu, bias=0.0, scale=1.0)
        vop(p, "tensor_mul", fb[:], fb[:], sb_[:])
        vop(p, "tensor_add", fb[:], fb[:], xb[:])
        p.dma("sync", tn["xcur"][seq, csl, :], fb[:])


EMITTERS["moe0"] = emit_moe0



def emit_ffn1b(p, tn, seq):
    nc = tn["nc"]
    st = tn["segstack"]
    sb = tn["sb"]
    PS_A, PS_B, PS_T, PS_O = tn["PS_A"], tn["PS_B"], tn["PS_T"], tn["PS_O"]
    identf = sb["identf"]
    onesF = sb["onesF"]
    ones_col = sb["ones_col"]
    EPS = 1.1920929e-07

    def S_(name, shape, dt):
        return st.enter_context(nc.sbuf_tensor(uname(name), shape, dt))

    xfb = [S_(f"fb_xf{c}", [128, T], BF16) for c in range(NCH)]
    A = [S_(f"fb_A{j}", [128, T], BF16) for j in range(16)]
    B = [S_(f"fb_B{j}", [128, T], BF16) for j in range(16)]
    wt = [S_(f"fb_wt{i}", [128, 2048], BF16) for i in range(3)]
    xlc = S_("fb_xl", [128, T], F32)
    gw2 = S_("fb_gw", [128, 32], F32)
    gc_f = S_("fb_gc", [128, T], F32)
    Ctm = S_("fb_ctm", [128, 16], F32)
    ltm = S_("fb_ltm", [128, 64], F32)
    cw_tm = S_("fb_cwtm", [128, 64], F32)
    cwstg = S_("fb_cwstg", [4, 128], F32)
    eqt = S_("fb_eq", [128, 4], F32)
    mskt = S_("fb_msk", [128, 4], F32)
    stt8 = S_("fb_st", [128, 8], F32)
    cwb = S_("fb_cwb", [128, T], F32)
    Cb = cwb[:, 0:512]
    sq = gc_f
    y_t = [gc_f[:, i * 512:(i + 1) * 512] for i in range(4)]
    fb_s = cwb
    n2s = sb["n2w_s"]
    cwrow = gc_f[0:1, 0:T]

    p.dma("sync", gw2[:], tn["gw1"][0][:, :])
    for c in range(NCH):
        vop(p, "tensor_scalar_mul", gw2[:, c * 4:(c + 1) * 4],
            gw2[:, c * 4:(c + 1) * 4], n2s[:, c:c + 1])
    for c in range(NCH):
        p.dma("sync", xlc[:], tn["xloc"][seq, c * 128:(c + 1) * 128, :])
        sop(p, "square", sq[:], xlc[:])
        mm_group(p, [(PS_A[64:65, g * 512:(g + 1) * 512], ones_col[:],
                      sq[:, g * 512:(g + 1) * 512], c == 0, c == NCH - 1)
                     for g in range(NTG)])
        mm_group(p, [(PS_A[0:4, g * 512:(g + 1) * 512], gw2[:, c * 4:(c + 1) * 4],
                      xlc[:, g * 512:(g + 1) * 512], c == 0, c == NCH - 1)
                     for g in range(NTG)])
    gop(p, "memset", gc_f[:], 0.0)
    Crow = gc_f[64:65, :]
    vop(p, "tensor_copy", Crow, PS_A[64:65, :])
    vop(p, "tensor_scalar_mul", Crow, Crow, 1.0 / D)
    vop(p, "tensor_scalar_add", Crow, Crow, EPS)
    sop(p, "sqrt", Crow, Crow)
    vop(p, "reciprocal", Crow, Crow)
    vop(p, "tensor_copy", gc_f[0:4, :], PS_A[0:4, :])
    for kt in range(NQT):
        trgroup(p, [(PS_B[:, 0:128], gc_f[:, kt * 128:(kt + 1) * 128])], identf[:])
        vop(p, "tensor_copy", ltm[:, kt * 4:(kt + 1) * 4], PS_B[:, 0:4])
        vop(p, "tensor_copy", Ctm[:, kt:kt + 1], PS_B[:, 64:65])
    for g in range(NTG):
        gsl = slice(g * 512, (g + 1) * 512)
        mm(p, PS_B[:, 0:512], onesF[64:65, 0:128], Crow[0:1, gsl], True, True)
        vop(p, "tensor_copy", Cb, PS_B[:, 0:512])
        for c in range(NCH):
            p.dma("sync", xlc[:, 0:512], tn["xloc"][seq, c * 128:(c + 1) * 128, gsl])
            vop(p, "scalar_tensor_tensor", xfb[c][:, gsl], xlc[:, 0:512],
                n2s[:, c:c + 1], Cb, ALU.mult, ALU.mult)
    for kt in range(NQT):
        blk = ltm[:, kt * 4:(kt + 1) * 4]
        vop(p, "tensor_scalar_mul", blk, blk, Ctm[:, kt:kt + 1])
        negm = stt8[:, 0:1]; Z = stt8[:, 1:2]; rZ = stt8[:, 2:3]
        m1 = stt8[:, 3:4]; negm2 = stt8[:, 4:5]; m2 = stt8[:, 5:6]
        s12 = stt8[:, 6:7]; r12 = stt8[:, 7:8]
        vop(p, "reduce_max", negm, blk, AX.X, negate=True)
        sop(p, "activation", blk, blk, AF.Exp, bias=negm, scale=1.0,
            accum_out=Z)
        vop(p, "reciprocal", rZ, Z)
        vop(p, "tensor_scalar_mul", blk, blk, rZ)
        vop(p, "reduce_max", m1, blk, AX.X)
        vop(p, "tensor_scalar", eqt[:], blk, m1, None, ALU.is_equal)
        vop(p, "scalar_tensor_tensor", mskt[:], eqt[:], -1e30, blk,
            ALU.mult, ALU.add)
        vop(p, "reduce_max", negm2, mskt[:], AX.X, negate=True)
        vop(p, "tensor_scalar_mul", m2, negm2, -1.0)
        vop(p, "tensor_sub", s12, m1, negm2)
        vop(p, "tensor_scalar_add", s12, s12, 1e-8)
        vop(p, "reciprocal", r12, s12)
        cwblk = cw_tm[:, kt * 4:(kt + 1) * 4]
        vop(p, "tensor_scalar", cwblk, eqt[:], m1, None, ALU.mult)
        vop(p, "tensor_scalar", eqt[:], mskt[:], m2, None, ALU.is_equal)
        vop(p, "tensor_scalar", eqt[:], eqt[:], m2, None, ALU.mult)
        vop(p, "tensor_add", cwblk, cwblk, eqt[:])
        vop(p, "tensor_scalar_mul", cwblk, cwblk, r12)
    for kt in range(NQT):
        trgroup(p, [(PS_B[0:4, 0:128], cw_tm[:, kt * 4:(kt + 1) * 4])], identf[:])
        vop(p, "tensor_copy", cwstg[:], PS_B[0:4, 0:128])
        p.dma("sync", tn["cwsc"][0:4, kt * 128:(kt + 1) * 128], cwstg[:])

    def bcast_cw(e):
        p.dma("sync", cwrow, tn["cwsc"][e:e + 1, :])
        for g in range(NTG):
            mm(p, PS_B[:, 0:512], onesF[0:1, 0:128],
               cwrow[0:1, g * 512:(g + 1) * 512], True, True)
            vop(p, "tensor_copy", cwb[:, g * 512:(g + 1) * 512], PS_B[:, 0:512])

    def up_mm(wname, oc, src_list, nkc, ev):
        wbuf = wt[oc % 3]
        p.dma("gpsimd", wbuf[:, 0:nkc * 128], tn[wname][oc][:, :])
        for h in range(2):
            hsl = slice(h * 1024, (h + 1) * 1024)
            mm_group(p, [(PS_A[:, h * 1024 + g * 512:h * 1024 + (g + 1) * 512],
                          wbuf[:, kc * 128:(kc + 1) * 128],
                          src_list[kc][:, h * 1024 + g * 512:h * 1024 + (g + 1) * 512],
                          kc == 0, kc == nkc - 1)
                         for kc in range(nkc) for g in range(2)])
            ev(hsl)

    def down_acc(wname, src_list, nkc, first_e):
        for oc in range(2, NCH):
            wbuf = wt[oc % 3]
            p.dma("gpsimd", wbuf[:, 0:nkc * 128], tn[wname][oc][:, :])
            for g in range(NTG):
                gsl = slice(g * 512, (g + 1) * 512)
                par = (oc * NTG + g) % 2
                pps = PS_O if par == 0 else PS_B
                yb = y_t[(oc * NTG + g) % 4]
                mm_group(p, [(pps[:, 0:512], wbuf[:, kc * 128:(kc + 1) * 128],
                              src_list[kc][:, gsl], kc == 0, kc == nkc - 1)
                             for kc in range(nkc)])
                vop(p, "tensor_mul", yb, pps[:, 0:512], cwb[:, gsl])
                dst = tn["facc"][seq, oc * 128:(oc + 1) * 128, gsl]
                if first_e:
                    p.dma("gpsimd", dst, yb)
                else:
                    p.do("gpsimd", lambda en, d=dst, s=yb: en.dma_start(
                        out=d, in_=s, accum_op=ALU.add),
                        reads=[s if False else yb, dst], writes=[dst], dma=True)

    def gelu_to(dst):
        return lambda hsl: sop(p, "activation", dst[:, hsl], PS_A[:, hsl],
                               AF.Gelu, bias=0.0, scale=1.0)

    def silu_to(dst):
        return lambda hsl: sop(p, "activation", dst[:, hsl], PS_A[:, hsl],
                               AF.Silu, bias=0.0, scale=1.0)

    def mul_into(dst):
        return lambda hsl: vop(p, "tensor_mul", dst[:, hsl], PS_A[:, hsl],
                               dst[:, hsl])

    bcast_cw(0)
    for oc in range(16):
        up_mm("e0up", oc, xfb, NCH, gelu_to(A[oc]))
    down_acc("e0dn", A, 16, True)
    bcast_cw(1)
    for oc in range(16):
        up_mm("e1w1", oc, xfb, NCH, silu_to(A[oc]))
    for oc in range(16):
        up_mm("e1w2", oc, xfb, NCH, mul_into(A[oc]))
    down_acc("e1dn", A, 16, False)
    bcast_cw(2)
    for oc in range(16):
        up_mm("e2l1", oc, xfb, NCH, gelu_to(A[oc]))
    for oc in range(16):
        up_mm("e2l2", oc, A, 16, gelu_to(B[oc]))
    for oc in range(16):
        up_mm("e2l3", oc, B, 16, gelu_to(A[oc]))
    down_acc("e2l4", A, 16, False)
    bcast_cw(3)
    for oc in range(NCH):
        up_mm("e3dn", oc, xfb, NCH, gelu_to(A[oc]))
    for oc in range(16):
        up_mm("e3up", oc, A, NCH, gelu_to(B[oc]))
    down_acc("e3out", B, 16, False)
    for c in range(NCH):
        csl = slice(c * 128, (c + 1) * 128)
        p.dma("sync", xlc[:], tn["xloc"][seq, csl, :])
        if c >= 2:
            p.dma("sync", gc_f[:], tn["facc"][seq, csl, :])
            vop(p, "tensor_add", gc_f[:], gc_f[:], xlc[:])
        else:
            vop(p, "tensor_copy", gc_f[:], xlc[:])
        sop(p, "activation", gc_f[:], gc_f[:], AF.Silu, bias=0.0, scale=1.0)
        p.dma("sync", fb_s[:], tn["sbr"][seq, csl, :])
        vop(p, "tensor_mul", gc_f[:], gc_f[:], fb_s[:])
        p.dma("sync", fb_s[:], tn["xcur"][seq, csl, :])
        vop(p, "tensor_add", gc_f[:], gc_f[:], fb_s[:])
        p.dma("sync", tn["yout"][seq, csl, :], gc_f[:])


EMITTERS["ffn1b"] = emit_ffn1b



SCRATCH_F32 = ("xn32", "xcur", "sbr", "xloc", "facc")
SCRATCH_BF16 = ("znbf",)


def declare_tensors(nc, segs, produce, consume):
    tn = {}
    tn["xin"] = nc.dram_tensor("xin", [NSEQ, D, T], F32, kind="ExternalInput").ap()
    tn["cwsc"] = nc.dram_tensor("cwsc", [8, T], F32, kind="Internal").ap()
    tn["yout"] = nc.dram_tensor("yout", [NSEQ, D, T], F32, kind="ExternalOutput").ap()
    wshapes = {
        "w_qkv0": ([24, 128, 1024], F32), "b_qkv0": ([128, 24], F32),
        "w_proj0": ([8, 128, 1024], F32), "b_proj0": ([128, 8], F32),
        "ln0w": ([128, 8], F32), "ln0b": ([128, 8], F32),
        "rw0": ([1, 128, 64], F32),
        "w1_0": ([8, 42, 128, 1024], F32), "b1_0": ([128, 336], F32),
        "w2_0": ([8, 8, 128, 2688], F32), "b2_0": ([128, 64], F32),
        "w_attn1": ([24, 128, 1024], F32), "b_attn1": ([128, 24], F32),
        "w_proj1": ([8, 128, 1024], F32), "b_proj1": ([128, 8], F32),
        "wkc1T": ([32, 64], F32), "wke1": ([32, 64], F32),
        "wvc1T": ([32, 64], F32), "wve1": ([32, 64], F32),
        "n1w": ([128, 8], F32), "n2w": ([128, 8], F32),
        "wq1": ([8, 128, 1024], F32), "wk1": ([8, 128, 1024], F32),
        "wv1": ([8, 128, 1024], F32), "wo1": ([8, 128, 1024], F32),
        "qn1": ([128, 1], F32), "kn1": ([128, 1], F32),
        "gw1": ([1, 128, 32], F32),
        "e0up": ([16, 128, 1024], F32), "e0dn": ([8, 128, 2048], F32),
        "e1w1": ([16, 128, 1024], F32), "e1w2": ([16, 128, 1024], F32),
        "e1dn": ([8, 128, 2048], F32),
        "e2l1": ([16, 128, 1024], F32), "e2l2": ([16, 128, 2048], F32),
        "e2l3": ([16, 128, 2048], F32), "e2l4": ([8, 128, 2048], F32),
        "e3dn": ([8, 128, 1024], F32), "e3up": ([16, 128, 1024], F32),
        "e3out": ([8, 128, 2048], F32),
    }
    need = weights_needed(segs)
    for name in need:
        shape, dt = wshapes[name]
        tn[name] = nc.dram_tensor(name, shape, dt, kind="ExternalInput").ap()
    for name in SCRATCH_F32:
        if name in produce and name in consume:
            kind = "Internal"
        elif name in produce:
            kind = "ExternalOutput"
        elif name in consume:
            kind = "ExternalInput"
        else:
            continue
        tn[name] = nc.dram_tensor(name, [NSEQ, D, T], F32, kind=kind).ap()
    for name in SCRATCH_BF16:
        if name in produce and name in consume:
            kind = "Internal"
        elif name in produce:
            kind = "ExternalOutput"
        elif name in consume:
            kind = "ExternalInput"
        else:
            continue
        tn[name] = nc.dram_tensor(name, [NSEQ, D, T], BF16, kind=kind).ap()
    return tn


def weights_needed(segs):
    m = {
        "ln0": ["ln0w", "ln0b"],
        "ln1": ["n1w"],
        "attn0": ["w_qkv0", "b_qkv0", "w_proj0", "b_proj0"],
        "moe0": ["rw0", "w1_0", "b1_0", "w2_0", "b2_0"],
        "attn1": ["w_attn1", "b_attn1", "w_proj1", "b_proj1",
                  "wkc1T", "wke1", "wvc1T", "wve1"],
        "ffn1a": ["wq1", "wk1", "wv1", "wo1", "qn1", "kn1"],
        "ffn1b": ["n2w", "gw1", "e0up", "e0dn", "e1w1", "e1w2", "e1dn",
                  "e2l1", "e2l2", "e2l3", "e2l4", "e3dn", "e3up", "e3out"],
    }
    out = []
    for s in segs:
        for wn in m.get(s, []):
            if wn not in out:
                out.append(wn)
    return out


SEG_IO = {
    "ln0": ((), ("xn32", "znbf")),
    "attn0": (("xn32", "znbf"), ("sbr",)),
    "moe0": (("xn32", "sbr"), ("xcur", "facc")),
    "ln1": (("xcur",), ("xn32", "znbf")),
    "attn1": (("xn32",), ("sbr",)),
    "ffn1a": (("xn32", "znbf"), ("xloc",)),
    "ffn1b": (("xloc", "xcur", "sbr"), ("facc",)),
}


SMALLS = {
    "ln0w": [128, 8], "ln0b": [128, 8], "n1w": [128, 8], "n2w": [128, 8],
    "b_qkv0": [128, 24], "b_attn1": [128, 24],
    "b_proj0": [128, 8], "b_proj1": [128, 8],
    "qn1": [128, 1], "kn1": [128, 1],
    "b1_0": [128, 8 * 42], "b2_0": [128, 8 * 8],
}


def alloc_sbuf(nc, stack, segs):
    sb = {}
    def S(name, shape, dt):
        sb[name] = stack.enter_context(nc.sbuf_tensor(name, shape, dt))
        return sb[name]
    for name in weights_needed(segs):
        if name in SMALLS:
            S(name + "_s", SMALLS[name], F32)
    S("ones_col", [128, 1], F32)
    S("ones_row", [1, 128], F32)
    S("onesb_col", [128, 1], BF16)
    S("onesb_row", [1, 128], BF16)
    S("ident", [128, 128], BF16)
    S("tri", [128, 128], F32)
    S("identf", [128, 128], F32)
    S("onesF", [128, 128], F32)
    return sb


def init_const(p, tn, segs):
    sb = tn["sb"]
    for name in weights_needed(segs):
        if name in SMALLS:
            mir = sb[name + "_s"]
            src = tn[name]
            p.dma("sync", mir[:], src[:])
    gop(p, "memset", sb["ones_col"][:], 1.0)
    gop(p, "memset", sb["ones_row"][:], 1.0)
    gop(p, "memset", sb["onesb_col"][:], 1.0)
    gop(p, "memset", sb["onesb_row"][:], 1.0)
    gop(p, "memset", sb["ident"][:], 1.0)
    gop(p, "affine_select", sb["ident"][:], sb["ident"][:], [[-1, 128]],
        ALU.is_equal, 0.0, base=0, channel_multiplier=1)
    gop(p, "memset", sb["tri"][:], 0.0)
    gop(p, "affine_select", sb["tri"][:], sb["tri"][:], [[-1, 128]],
        ALU.is_ge, -1e30, base=0, channel_multiplier=1)
    gop(p, "memset", sb["identf"][:], 1.0)
    gop(p, "affine_select", sb["identf"][:], sb["identf"][:], [[-1, 128]],
        ALU.is_equal, 0.0, base=0, channel_multiplier=1)
    gop(p, "memset", sb["onesF"][:], 1.0)


def build_nc(segs):
    consume, produce = set(), set()
    for s in segs:
        c, pr = SEG_IO[s]
        consume.update(x for x in c if x not in produce)
        produce.update(pr)
    nc = bass.Bass("TRN2", target_bir_lowering=False)
    stack = ExitStack()
    with stack:
        tn = declare_tensors(nc, segs, produce, consume)
        sb = alloc_sbuf(nc, stack, segs)
        tn["sb"] = sb
        tn["PS_A"] = stack.enter_context(nc.psum_tensor("PS_A", [128, 2048], F32))
        tn["PS_B"] = stack.enter_context(nc.psum_tensor("PS_B", [128, 512], F32))
        tn["PS_T"] = stack.enter_context(nc.psum_tensor("PS_T", [128, 1024], BF16))
        tn["PS_T2"] = stack.enter_context(nc.psum_tensor("PS_T2", [128, 1024], BF16))
        tn["PS_O"] = stack.enter_context(nc.psum_tensor("PS_O", [128, 512], F32))
        tn["nc"] = nc
        with nc.Block() as block:
            p = Prog(nc, stack, serial=SERIAL_MODE)
            init_const(p, tn, segs)
            for seq in range(NSEQ):
                for s in segs:
                    with ExitStack() as segstack:
                        tn["segstack"] = segstack
                        EMITTERS[s](p, tn, seq)
                    p.fence()
            p.finalize(block)
    return nc


def _emit_ln0(p, tn, seq):
    emit_ln(p, tn, seq, 0)


def _emit_ln1(p, tn, seq):
    emit_ln(p, tn, seq, 1)


EMITTERS["ln0"] = _emit_ln0
EMITTERS["ln1"] = _emit_ln1



ALL_SEGS = ["ln0", "attn0", "moe0", "ln1", "attn1", "ffn1a", "ffn1b"]


def kernel(**inputs):
    w = prep_weights(inputs)
    nc = build_nc(ALL_SEGS)
    need = weights_needed(ALL_SEGS)
    in_maps = []
    for core in range(NCORE):
        im = {"xin": prep_xin(inputs, core)}
        for n in need:
            im[n] = w[n]
        in_maps.append(im)
    res = run_bass_kernel_spmd(nc, in_maps, core_ids=list(range(NCORE)))
    out = np.empty((16, 2048, 1024), np.float32)
    for core in range(NCORE):
        y = np.asarray(res.results[core]["yout"])
        out[2 * core:2 * core + 2] = y.transpose(0, 2, 1)
    return out
